# version 01
# baseline (speedup 1.0000x reference)
from contextlib import ExitStack

import numpy as np
import ml_dtypes
import concourse.bass as bass
import concourse.mybir as mybir
from concourse.bass_utils import run_bass_kernel_spmd

F32 = mybir.dt.float32
BF16 = mybir.dt.bfloat16
AF = mybir.ActivationFunctionType
ALU = mybir.AluOpType
AX = mybir.AxisListType

NCORE = 8
T = 16384
D = 1024
OWN = 2048
HALO = 512
WTOT = OWN + HALO
NT = WTOT // 128
L = 128
NCHUNK = WTOT // L
OWNCH0 = HALO // L
H = 16
DH = 64
DI = 2048
DS = 16
EPS = 1e-5
BHALO = 128
BW = OWN + BHALO

ENGS = ["sync", "scalar", "vector", "gpsimd", "tensor"]


class Prog:

    def __init__(self, nc, sem):
        self.nc, self.sem, self.items, self.cnt = nc, sem, [], 0

    def op(self, eng, meth, *a, inc=1, **kw):
        self.items.append((eng, meth, a, kw, self.cnt, inc))
        self.cnt += inc

    def dma(self, out, in_, eng="sync", **kw):
        self.op(eng, "dma_start", inc=16, out=out, in_=in_, **kw)

    def flush(self, final=False):
        if not self.items and not final:
            return
        with self.nc.Block() as block:
            for eng_name in ENGS:
                mine = [it for it in self.items if it[0] == eng_name]
                if not mine and not final:
                    continue

                def body(eng, mine=mine):
                    for (_, meth, a, kw, wait, inc) in mine:
                        if wait > 0:
                            eng.wait_ge(self.sem, wait)
                        getattr(eng, meth)(*a, **kw).then_inc(self.sem, inc)
                    eng.wait_ge(self.sem, self.cnt)

                getattr(block, eng_name)(body)
        self.items = []


def f2(ap):
    return ap.rearrange("p a b -> p (a b)")


def sl(t, i, a, b):
    return f2(t[:, i:i + 1, a:b])


def emit_ident(p, nc, irow, icol, identf, identb, maskut):
    p.op("gpsimd", "iota", irow[:, :], [[1, 128]], channel_multiplier=0,
         allow_small_or_imprecise_dtypes=True)
    p.op("gpsimd", "iota", icol[:, :], [[0, 1]], channel_multiplier=1,
         allow_small_or_imprecise_dtypes=True)
    p.op("vector", "tensor_scalar", out=identf[:, :], in0=irow[:, :],
         scalar1=icol[:, :], scalar2=None, op0=ALU.is_equal)
    p.op("vector", "tensor_copy", out=identb[:, :], in_=identf[:, :])
    if maskut is not None:
        p.op("vector", "tensor_scalar", out=maskut[:, :], in0=irow[:, :],
             scalar1=icol[:, :], scalar2=None, op0=ALU.is_gt)


def emit_ln(p, xin, xout, scr, cols, n=1024, eps=EPS):
    p.op("vector", "tensor_reduce", out=cols[:, 0:1], in_=xin, axis=AX.X, op=ALU.add)
    p.op("vector", "tensor_scalar", out=cols[:, 1:2], in0=cols[:, 0:1],
         scalar1=-1.0 / n, scalar2=None, op0=ALU.mult)
    p.op("vector", "tensor_scalar", out=xout, in0=xin, scalar1=cols[:, 1:2],
         scalar2=None, op0=ALU.add)
    p.op("scalar", "activation", out=scr, in_=xout, func=AF.Square,
         accum_out=cols[:, 2:3])
    p.op("vector", "tensor_scalar", out=cols[:, 3:4], in0=cols[:, 2:3],
         scalar1=1.0 / n, scalar2=eps, op0=ALU.mult, op1=ALU.add)
    p.op("scalar", "activation", out=cols[:, 3:4], in_=cols[:, 3:4], func=AF.Ln)
    p.op("scalar", "activation", out=cols[:, 2:3], in_=cols[:, 3:4], func=AF.Exp,
         scale=-0.5)
    p.op("vector", "tensor_scalar", out=xout, in0=xout, scalar1=cols[:, 2:3],
         scalar2=None, op0=ALU.mult)


def emit_softplus(p, zin, out, f1, f2_, bias_col=None):
    if bias_col is not None:
        p.op("vector", "tensor_scalar", out=f1, in0=zin, scalar1=bias_col,
             scalar2=None, op0=ALU.add)
        zsrc = f1
    else:
        zsrc = zin
    p.op("scalar", "activation", out=f2_, in_=zsrc, func=AF.Abs)
    p.op("scalar", "activation", out=f2_, in_=f2_, func=AF.Exp, scale=-1.0)
    p.op("scalar", "activation", out=f2_, in_=f2_, func=AF.Ln, bias=1.0)
    p.op("vector", "tensor_scalar", out=out, in0=zsrc, scalar1=0.0, scalar2=None,
         op0=ALU.max)
    p.op("vector", "tensor_tensor", out=out, in0=out, in1=f2_, op=ALU.add)



def build_A():
    nc = bass.Bass("TRN2", target_bir_lowering=False)
    xs = nc.dram_tensor("xs", [WTOT, D], F32, kind="ExternalInput").ap()
    wrT = nc.dram_tensor("wrT", [D, D], F32, kind="ExternalInput").ap()
    wkT = nc.dram_tensor("wkT", [D, D], F32, kind="ExternalInput").ap()
    wvT = nc.dram_tensor("wvT", [D, D], F32, kind="ExternalInput").ap()
    wwT = nc.dram_tensor("wwT", [D, D], F32, kind="ExternalInput").ap()
    woT = nc.dram_tensor("woT", [D, D], F32, kind="ExternalInput").ap()
    tmc = nc.dram_tensor("tmc", [128, 32], F32, kind="ExternalInput").ap()
    ucol = nc.dram_tensor("ucol", [128, 8], F32, kind="ExternalInput").ap()
    lncol = nc.dram_tensor("lncol", [128, 16], F32, kind="ExternalInput").ap()
    fw1 = nc.dram_tensor("fw1", [D, 4096], F32, kind="ExternalInput").ap()
    fb1 = nc.dram_tensor("fb1", [128, 32], F32, kind="ExternalInput").ap()
    fw2 = nc.dram_tensor("fw2", [4096, D], F32, kind="ExternalInput").ap()
    fb2 = nc.dram_tensor("fb2", [1, D], F32, kind="ExternalInput").ap()
    bW1T = nc.dram_tensor("bW1T", [D, DI], F32, kind="ExternalInput").ap()
    bb1 = nc.dram_tensor("bb1", [128, 16], F32, kind="ExternalInput").ap()
    bW2T = nc.dram_tensor("bW2T", [DI, D], F32, kind="ExternalInput").ap()
    bb2 = nc.dram_tensor("bb2", [1, D], F32, kind="ExternalInput").ap()
    l1g = nc.dram_tensor("l1g", [1, D], F32, kind="ExternalInput").ap()
    l1b = nc.dram_tensor("l1b", [1, D], F32, kind="ExternalInput").ap()

    x1 = nc.dram_tensor("x1", [OWN, D], F32, kind="ExternalOutput").ap()
    xbo = nc.dram_tensor("xbo", [OWN, D], F32, kind="ExternalOutput").ap()
    xn1T = nc.dram_tensor("xn1T", [D, OWN], BF16, kind="ExternalOutput").ap()
    gmax = nc.dram_tensor("gmax", [1, 1], F32, kind="ExternalOutput").ap()

    rTs = nc.dram_tensor("rTs", [D, WTOT], BF16, kind="Internal").ap()
    kTs = nc.dram_tensor("kTs", [D, WTOT], BF16, kind="Internal").ap()
    wTs = nc.dram_tensor("wTs", [D, WTOT], F32, kind="Internal").ap()
    vs = nc.dram_tensor("vs", [WTOT, D], BF16, kind="Internal").ap()
    p1s = nc.dram_tensor("p1s", [OWN, D], F32, kind="Internal").ap()

    with ExitStack() as _st:
        S = _st.enter_context(nc.semaphore("S"))
        PA = _st.enter_context(nc.psum_tensor("PA", [128, 1024], F32))
        PB = _st.enter_context(nc.psum_tensor("PB", [128, 512], F32))
        PC = _st.enter_context(nc.psum_tensor("PC", [128, 128], F32))
        PD = _st.enter_context(nc.psum_tensor("PD", [128, 16], F32))
        PE = _st.enter_context(nc.psum_tensor("PE", [128, 128], BF16))
        PF = _st.enter_context(nc.psum_tensor("PF", [128, 128], F32))
        irow = _st.enter_context(nc.sbuf_tensor("irow", [128, 128], F32))
        icol = _st.enter_context(nc.sbuf_tensor("icol", [128, 1], F32))
        identf = _st.enter_context(nc.sbuf_tensor("identf", [128, 128], F32))
        identb = _st.enter_context(nc.sbuf_tensor("identb", [128, 128], BF16))
        maskut = _st.enter_context(nc.sbuf_tensor("maskut", [128, 128], F32))
        ones1 = _st.enter_context(nc.sbuf_tensor("ones1", [1, 128], F32))
        ones64 = _st.enter_context(nc.sbuf_tensor("ones64", [128, 1], F32))
        tmsb = _st.enter_context(nc.sbuf_tensor("tmsb", [128, 8, 4], F32))
        usb = _st.enter_context(nc.sbuf_tensor("usb", [128, 8, 1], F32))
        lnsb = _st.enter_context(nc.sbuf_tensor("lnsb", [128, 8, 2], F32))
        wosb = _st.enter_context(nc.sbuf_tensor("wosb", [128, 8, 1024], BF16))
        Ssb = _st.enter_context(nc.sbuf_tensor("Ssb", [128, 8, 64], F32))
        p = Prog(nc, S)
        emit_ident(p, nc, irow, icol, identf, identb, maskut)
        p.op("gpsimd", "memset", ones1[:, :], 1.0)
        p.op("gpsimd", "memset", ones64[:, :], 1.0)
        p.op("gpsimd", "memset", Ssb[:, :, :], 0.0)
        p.dma(tmsb[:, :, :].rearrange("p a b -> p (a b)"), tmc[:, :])
        p.dma(usb[:, :, :].rearrange("p a b -> p (a b)"), ucol[:, :])
        p.dma(lnsb[:, :, :].rearrange("p a b -> p (a b)"), lncol[:, :])

        with ExitStack() as _st:
            xnt = _st.enter_context(nc.sbuf_tensor("xnt", [128, 8, WTOT + 1], BF16))
            stg = _st.enter_context(nc.sbuf_tensor("stg", [128, 8, 512], F32))
            with ExitStack() as _st:
                xtile = _st.enter_context(nc.sbuf_tensor("xtile", [128, 1024], F32))
                lnx = _st.enter_context(nc.sbuf_tensor("lnx", [128, 1024], F32))
                xnb = _st.enter_context(nc.sbuf_tensor("xnb", [128, 1024], BF16))
                cols = _st.enter_context(nc.sbuf_tensor("cols", [128, 4], F32))
                for g in range(8):
                    p.op("gpsimd", "memset", sl(xnt, g, 0, 1), 0.0)
                for i in range(NT):
                    p.dma(xtile[:, :], xs[i * 128:(i + 1) * 128, :])
                    emit_ln(p, xtile[:, :], lnx[:, :], xtile[:, :], cols)
                    p.op("vector", "tensor_copy", out=xnb[:, :], in_=lnx[:, :])
                    for g in range(8):
                        p.op("tensor", "transpose", PE[:, :],
                             xnb[:, g * 128:(g + 1) * 128], identb[:, :])
                        p.op("scalar", "copy",
                             out=sl(xnt, g, 1 + i * 128, 1 + (i + 1) * 128),
                             in_=PE[:, :])
                p.flush()

            with ExitStack() as _st:
                wrsb = _st.enter_context(nc.sbuf_tensor("wrsb", [128, 8, 1024], BF16))
                wksb = _st.enter_context(nc.sbuf_tensor("wksb", [128, 8, 1024], BF16))
                wvsb = _st.enter_context(nc.sbuf_tensor("wvsb", [128, 8, 1024], BF16))
                wwsb = _st.enter_context(nc.sbuf_tensor("wwsb", [128, 8, 1024], BF16))
                mixsb = _st.enter_context(nc.sbuf_tensor("mixsb", [128, 32, 512], BF16))
                dstrip = _st.enter_context(nc.sbuf_tensor("dstrip", [128, 512], F32))
                xpf = _st.enter_context(nc.sbuf_tensor("xpf", [128, 512], F32))
                ev1 = _st.enter_context(nc.sbuf_tensor("ev1", [128, 512], F32))
                ev2 = _st.enter_context(nc.sbuf_tensor("ev2", [128, 512], F32))
                evb = _st.enter_context(nc.sbuf_tensor("evb", [128, 512], BF16))
                for wdram, wsb in ((wrT, wrsb), (wkT, wksb), (wvT, wvsb),
                                   (wwT, wwsb), (woT, wosb)):
                    wv3 = wdram.rearrange("(g p) c -> p g c", p=128)
                    for half in range(2):
                        p.dma(stg[:, :, :], wv3[:, :, half * 512:(half + 1) * 512])
                        for g in range(8):
                            p.op("vector", "tensor_copy",
                                 out=sl(wsb, g, half * 512, (half + 1) * 512),
                                 in_=sl(stg, g, 0, 512))
                for tc in range(5):
                    c0 = tc * 512
                    for ci in range(8):
                        p.op("vector", "tensor_tensor", out=dstrip[:, :],
                             in0=sl(xnt, ci, 1 + c0, 1 + c0 + 512),
                             in1=sl(xnt, ci, c0, c0 + 512), op=ALU.subtract)
                        p.op("vector", "tensor_copy", out=xpf[:, :],
                             in_=sl(xnt, ci, c0, c0 + 512))
                        for ai in range(4):
                            p.op("vector", "scalar_tensor_tensor",
                                 out=sl(mixsb, ai * 8 + ci, 0, 512),
                                 in0=dstrip[:, :], scalar=f2(tmsb[:, ci:ci + 1, ai:ai + 1]),
                                 in1=xpf[:, :], op0=ALU.mult, op1=ALU.add)
                    for g in range(8):
                        for ai, wsb, spill in ((0, wrsb, rTs), (1, wksb, kTs),
                                               (3, wwsb, wTs)):
                            for ci in range(8):
                                p.op("tensor", "matmul", PB[:, :],
                                     sl(wsb, ci, g * 128, (g + 1) * 128),
                                     sl(mixsb, ai * 8 + ci, 0, 512),
                                     start=(ci == 0), stop=(ci == 7))
                            dst = spill[g * 128:(g + 1) * 128, c0:c0 + 512]
                            if ai == 0:
                                p.op("scalar", "activation", out=evb[:, :],
                                     in_=PB[:, :], func=AF.Sigmoid)
                                p.dma(dst, evb[:, :])
                            elif ai == 1:
                                p.op("vector", "tensor_copy", out=evb[:, :],
                                     in_=PB[:, :])
                                p.dma(dst, evb[:, :])
                            else:
                                p.op("scalar", "activation", out=ev1[:, :],
                                     in_=PB[:, :], func=AF.Abs)
                                p.op("scalar", "activation", out=ev1[:, :],
                                     in_=ev1[:, :], func=AF.Exp, scale=-1.0)
                                p.op("scalar", "activation", out=ev1[:, :],
                                     in_=ev1[:, :], func=AF.Ln, bias=1.0)
                                p.op("vector", "tensor_scalar", out=ev2[:, :],
                                     in0=PB[:, :], scalar1=0.0, scalar2=-1e-4,
                                     op0=ALU.min, op1=ALU.add)
                                p.op("vector", "tensor_tensor", out=ev2[:, :],
                                     in0=ev2[:, :], in1=ev1[:, :], op=ALU.subtract)
                                p.dma(dst, ev2[:, :])
                    for i128 in range(4):
                        ti = tc * 4 + i128
                        for half in range(2):
                            for ci in range(8):
                                p.op("tensor", "matmul", PB[:, :],
                                     f2(mixsb[:, 2 * 8 + ci:2 * 8 + ci + 1,
                                              i128 * 128:(i128 + 1) * 128]),
                                     sl(wvsb, ci, half * 512, (half + 1) * 512),
                                     start=(ci == 0), stop=(ci == 7))
                            p.op("vector", "tensor_copy", out=evb[:, :], in_=PB[:, :])
                            p.dma(vs[ti * 128:(ti + 1) * 128,
                                     half * 512:(half + 1) * 512], evb[:, :])
                    p.flush()

            with ExitStack() as _st:
                fw1sb = _st.enter_context(nc.sbuf_tensor("fw1sb", [128, 8, 4096], BF16))
                fbsb = _st.enter_context(nc.sbuf_tensor("fbsb", [128, 32, 1], F32))
                b2row = _st.enter_context(nc.sbuf_tensor("b2row", [1, 1024], F32))
                b2rep = _st.enter_context(nc.sbuf_tensor("b2rep", [128, 1024], F32))
                hsb = _st.enter_context(nc.sbuf_tensor("hsb", [128, 32, 512], BF16))
                fw2c = _st.enter_context(nc.sbuf_tensor("fw2c", [128, 1024], BF16))
                oacc = _st.enter_context(nc.sbuf_tensor("oacc", [128, 4, 1024], F32))
                xrow = _st.enter_context(nc.sbuf_tensor("xrow", [128, 1024], F32))
                fw13 = fw1.rearrange("(g p) c -> p g c", p=128)
                for piece in range(8):
                    p.dma(stg[:, :, :], fw13[:, :, piece * 512:(piece + 1) * 512])
                    for g in range(8):
                        p.op("vector", "tensor_copy",
                             out=sl(fw1sb, g, piece * 512, (piece + 1) * 512),
                             in_=sl(stg, g, 0, 512))
                p.dma(fbsb[:, :, :].rearrange("p a b -> p (a b)"), fb1[:, :])
                p.dma(b2row[:, :], fb2[:, :])
                for half in range(2):
                    p.op("tensor", "matmul", PB[:, :], ones1[0:1, :],
                         b2row[0:1, half * 512:(half + 1) * 512], start=True, stop=True)
                    p.op("scalar", "copy",
                         out=b2rep[:, half * 512:(half + 1) * 512], in_=PB[:, :])
                for tco in range(4):
                    c0 = 1 + HALO + tco * 512
                    p.op("gpsimd", "memset", oacc[:, :, :], 0.0)
                    for hc in range(32):
                        for ci in range(8):
                            p.op("tensor", "matmul", PB[:, :],
                                 sl(fw1sb, ci, hc * 128, (hc + 1) * 128),
                                 sl(xnt, ci, c0, c0 + 512),
                                 start=(ci == 0), stop=(ci == 7))
                        p.op("scalar", "activation", out=sl(hsb, hc, 0, 512),
                             in_=PB[:, :], func=AF.Gelu,
                             bias=f2(fbsb[:, hc:hc + 1, 0:1]))
                    for hc in range(32):
                        p.dma(stg[:, 0:2, :],
                              fw2[hc * 128:(hc + 1) * 128, :].rearrange(
                                  "p (a c) -> p a c", a=2))
                        p.op("vector", "tensor_copy", out=fw2c[:, :],
                             in_=stg[:, 0:2, :].rearrange("p a c -> p (a c)"))
                        for i128 in range(4):
                            for half in range(2):
                                p.op("tensor", "matmul",
                                     PA[:, half * 512:(half + 1) * 512],
                                     f2(hsb[:, hc:hc + 1, i128 * 128:(i128 + 1) * 128]),
                                     fw2c[:, half * 512:(half + 1) * 512],
                                     start=True, stop=True)
                            p.op("vector", "tensor_tensor",
                                 out=f2(oacc[:, i128:i128 + 1, :]),
                                 in0=f2(oacc[:, i128:i128 + 1, :]),
                                 in1=PA[:, :], op=ALU.add)
                    for i128 in range(4):
                        r0 = HALO + tco * 512 + i128 * 128
                        p.dma(xrow[:, :], xs[r0:r0 + 128, :])
                        p.op("vector", "tensor_tensor", out=xrow[:, :], in0=xrow[:, :],
                             in1=b2rep[:, :], op=ALU.add)
                        p.op("vector", "tensor_tensor", out=xrow[:, :], in0=xrow[:, :],
                             in1=f2(oacc[:, i128:i128 + 1, :]), op=ALU.add)
                        p.dma(p1s[tco * 512 + i128 * 128:tco * 512 + (i128 + 1) * 128, :],
                              xrow[:, :])
                    p.flush()

        with ExitStack() as _st:
            rsp = _st.enter_context(nc.sbuf_tensor("rsp", [128, 8, 512], BF16))
            ksp = _st.enter_context(nc.sbuf_tensor("ksp", [128, 8, 512], BF16))
            wsp = _st.enter_context(nc.sbuf_tensor("wsp", [128, 8, 512], F32))
            vsp = _st.enter_context(nc.sbuf_tensor("vsp", [128, 4, 1024], BF16))
            zero128 = _st.enter_context(nc.sbuf_tensor("zero128", [128, 128], F32))
            cumw = _st.enter_context(nc.sbuf_tensor("cumw", [128, 128], F32))
            cw1 = _st.enter_context(nc.sbuf_tensor("cw1", [128, 128], F32))
            tf1 = _st.enter_context(nc.sbuf_tensor("tf1", [128, 128], F32))
            tf2 = _st.enter_context(nc.sbuf_tensor("tf2", [128, 128], F32))
            tf4 = _st.enter_context(nc.sbuf_tensor("tf4", [128, 128], F32))
            rf = _st.enter_context(nc.sbuf_tensor("rf", [128, 128], F32))
            kf = _st.enter_context(nc.sbuf_tensor("kf", [128, 128], F32))
            rtm = _st.enter_context(nc.sbuf_tensor("rtm", [128, 128], BF16))
            ktm = _st.enter_context(nc.sbuf_tensor("ktm", [128, 128], BF16))
            ksb = _st.enter_context(nc.sbuf_tensor("ksb", [128, 128], BF16))
            rsb = _st.enter_context(nc.sbuf_tensor("rsb", [128, 128], BF16))
            Gm0 = _st.enter_context(nc.sbuf_tensor("Gm0", [128, 128], BF16))
            Gm1 = _st.enter_context(nc.sbuf_tensor("Gm1", [128, 128], BF16))
            Sbb = _st.enter_context(nc.sbuf_tensor("Sbb", [128, 64], BF16))
            kstsb = _st.enter_context(nc.sbuf_tensor("kstsb", [128, 128], BF16))
            osb = _st.enter_context(nc.sbuf_tensor("osb", [128, 128], F32))
            o2b = _st.enter_context(nc.sbuf_tensor("o2b", [128, 128], BF16))
            dcsb = _st.enter_context(nc.sbuf_tensor("dcsb", [128, 2], F32))
            ccol = _st.enter_context(nc.sbuf_tensor("ccol", [128, 1], F32))
            gcol = _st.enter_context(nc.sbuf_tensor("gcol", [128, 4], F32))
            p1row = _st.enter_context(nc.sbuf_tensor("p1row", [128, 1024], F32))
            p.op("gpsimd", "memset", zero128[:, :], 0.0)
            for c in range(NCHUNK):
                span, vidx = c // 4, c % 4
                if vidx == 0:
                    s0 = span * 4 * 128
                    for dram, sb in ((rTs, rsp), (kTs, ksp), (wTs, wsp)):
                        p.dma(sb[:, :, :],
                              dram.rearrange("(g p) t -> p g t", p=128)[
                                  :, :, s0:s0 + 512])
                    p.dma(vsp[:, :, :],
                          vs.rearrange("(n p) c -> p n c", p=128)[
                              :, span * 4:span * 4 + 4, :])
                t0 = vidx * 128
                for g in range(8):
                    wch = sl(wsp, g, t0, t0 + 128)
                    p.op("vector", "tensor_tensor_scan", out=cumw[:, :], data0=wch,
                         data1=zero128[:, :], initial=0.0, op0=ALU.add, op1=ALU.add)
                    p.op("vector", "tensor_tensor", out=cw1[:, :], in0=cumw[:, :],
                         in1=wch, op=ALU.subtract)
                    cmid = cumw[:, 63:64]
                    dlast = cumw[:, 127:128]
                    p.op("vector", "tensor_copy", out=rf[:, :], in_=sl(rsp, g, t0, t0 + 128))
                    p.op("vector", "tensor_copy", out=kf[:, :], in_=sl(ksp, g, t0, t0 + 128))
                    p.op("vector", "tensor_scalar", out=tf1[:, :], in0=cw1[:, :],
                         scalar1=cmid, scalar2=None, op0=ALU.subtract)
                    p.op("scalar", "activation", out=tf1[:, :], in_=tf1[:, :], func=AF.Exp)
                    p.op("vector", "tensor_tensor", out=rtm[:, :], in0=tf1[:, :],
                         in1=rf[:, :], op=ALU.mult)
                    p.op("vector", "tensor_scalar", out=tf2[:, :], in0=cumw[:, :],
                         scalar1=-1.0, scalar2=cmid, op0=ALU.mult, op1=ALU.add)
                    p.op("scalar", "activation", out=tf2[:, :], in_=tf2[:, :], func=AF.Exp)
                    p.op("vector", "tensor_tensor", out=ktm[:, :], in0=tf2[:, :],
                         in1=kf[:, :], op=ALU.mult)
                    p.op("vector", "tensor_scalar", out=tf2[:, :], in0=cumw[:, :],
                         scalar1=-1.0, scalar2=dlast, op0=ALU.mult, op1=ALU.add)
                    p.op("scalar", "activation", out=tf2[:, :], in_=tf2[:, :], func=AF.Exp)
                    p.op("vector", "tensor_tensor", out=ksb[:, :], in0=tf2[:, :],
                         in1=kf[:, :], op=ALU.mult)
                    p.op("scalar", "activation", out=tf4[:, :], in_=cw1[:, :], func=AF.Exp)
                    p.op("vector", "tensor_tensor", out=rsb[:, :], in0=tf4[:, :],
                         in1=rf[:, :], op=ALU.mult)
                    p.op("scalar", "activation", out=tf4[:, :],
                         in_=sl(ksp, g, t0, t0 + 128), func=AF.Exp,
                         bias=f2(usb[:, g:g + 1, 0:1]))
                    p.op("vector", "tensor_tensor", out=tf4[:, :], in0=tf4[:, :],
                         in1=rf[:, :], op=ALU.mult)
                    p.op("scalar", "activation", out=ccol[:, 0:1], in_=dlast, func=AF.Exp)
                    for h in range(2):
                        p.op("tensor", "matmul", PD[:, h:h + 1],
                             tf4[h * 64:(h + 1) * 64, :],
                             ones64[h * 64:(h + 1) * 64, :],
                             start=True, stop=True)
                    p.op("scalar", "copy", out=dcsb[:, 0:2], in_=PD[:, 0:2])
                    for h, Gm in ((0, Gm0), (1, Gm1)):
                        p.op("tensor", "matmul", PC[:, :],
                             ktm[h * 64:(h + 1) * 64, :], rtm[h * 64:(h + 1) * 64, :],
                             start=True, stop=True)
                        p.op("vector", "tensor_scalar", out=tf1[:, :], in0=PC[:, :],
                             scalar1=1e30, scalar2=-1e30, op0=ALU.min, op1=ALU.max)
                        p.op("vector", "tensor_tensor", out=Gm[:, :], in0=tf1[:, :],
                             in1=maskut[:, :], op=ALU.mult)
                    p.op("vector", "tensor_copy", out=Sbb[:, :], in_=f2(Ssb[:, g:g + 1, :]))
                    for h, Gm in ((0, Gm0), (1, Gm1)):
                        vch = f2(vsp[:, vidx:vidx + 1, g * 128 + h * 64:g * 128 + (h + 1) * 64])
                        p.op("tensor", "matmul", PF[:, h * 64:(h + 1) * 64],
                             Gm[:, :], vch, start=True, stop=False)
                        p.op("tensor", "matmul", PF[:, h * 64:(h + 1) * 64],
                             rsb[h * 64:(h + 1) * 64, :], Sbb[h * 64:(h + 1) * 64, :],
                             start=False, stop=True)
                    for h in range(2):
                        vch = f2(vsp[:, vidx:vidx + 1, g * 128 + h * 64:g * 128 + (h + 1) * 64])
                        p.op("vector", "scalar_tensor_tensor",
                             out=osb[:, h * 64:(h + 1) * 64], in0=vch,
                             scalar=dcsb[:, h:h + 1], in1=PF[:, h * 64:(h + 1) * 64],
                             op0=ALU.mult, op1=ALU.add)
                    p.op("vector", "tensor_reduce", out=gcol[:, 0:2],
                         in_=osb[:, :].rearrange("p (a b) -> p a b", a=2),
                         axis=AX.X, op=ALU.add)
                    p.op("vector", "tensor_scalar", out=gcol[:, 2:4], in0=gcol[:, 0:2],
                         scalar1=-1.0 / 64, scalar2=None, op0=ALU.mult)
                    for h in range(2):
                        p.op("vector", "tensor_scalar", out=osb[:, h * 64:(h + 1) * 64],
                             in0=osb[:, h * 64:(h + 1) * 64],
                             scalar1=gcol[:, 2 + h:3 + h], scalar2=None, op0=ALU.add)
                        p.op("scalar", "activation", out=tf4[:, h * 64:(h + 1) * 64],
                             in_=osb[:, h * 64:(h + 1) * 64], func=AF.Square,
                             accum_out=gcol[:, h:h + 1])
                    p.op("vector", "tensor_scalar", out=gcol[:, 2:4], in0=gcol[:, 0:2],
                         scalar1=1.0 / 64, scalar2=EPS, op0=ALU.mult, op1=ALU.add)
                    p.op("scalar", "activation", out=gcol[:, 2:4], in_=gcol[:, 2:4],
                         func=AF.Ln)
                    p.op("scalar", "activation", out=gcol[:, 0:2], in_=gcol[:, 2:4],
                         func=AF.Exp, scale=-0.5)
                    for h in range(2):
                        p.op("vector", "tensor_scalar", out=osb[:, h * 64:(h + 1) * 64],
                             in0=osb[:, h * 64:(h + 1) * 64],
                             scalar1=gcol[:, h:h + 1], scalar2=None, op0=ALU.mult)
                    p.op("tensor", "transpose", PC[:, :], osb[:, :], identf[:, :])
                    p.op("vector", "tensor_scalar", out=tf4[:, :], in0=PC[:, :],
                         scalar1=f2(lnsb[:, g:g + 1, 0:1]),
                         scalar2=f2(lnsb[:, g:g + 1, 1:2]), op0=ALU.mult, op1=ALU.add)
                    p.op("vector", "tensor_tensor", out=o2b[:, :], in0=tf4[:, :],
                         in1=rf[:, :], op=ALU.mult)
                    for half in range(2):
                        p.op("tensor", "matmul", PA[:, half * 512:(half + 1) * 512],
                             o2b[:, :], sl(wosb, g, half * 512, (half + 1) * 512),
                             start=(g == 0), stop=(g == 7), skip_group_check=True)
                    p.op("tensor", "transpose", PE[:, :], ksb[:, :], identb[:, :])
                    p.op("scalar", "copy", out=kstsb[:, :], in_=PE[:, :])
                    for h in range(2):
                        vch = f2(vsp[:, vidx:vidx + 1, g * 128 + h * 64:g * 128 + (h + 1) * 64])
                        p.op("tensor", "matmul", PF[:, h * 64:(h + 1) * 64],
                             kstsb[:, :], vch, start=True, stop=True)
                    for h in range(2):
                        p.op("vector", "scalar_tensor_tensor",
                             out=f2(Ssb[h * 64:(h + 1) * 64, g:g + 1, :]),
                             in0=f2(Ssb[h * 64:(h + 1) * 64, g:g + 1, :]),
                             scalar=ccol[h * 64:(h + 1) * 64, 0:1],
                             in1=PF[h * 64:(h + 1) * 64, h * 64:(h + 1) * 64],
                             op0=ALU.mult, op1=ALU.add)
                if c >= OWNCH0:
                    r0 = (c - OWNCH0) * 128
                    p.dma(p1row[:, :], p1s[r0:r0 + 128, :])
                    p.op("vector", "tensor_tensor", out=p1row[:, :], in0=p1row[:, :],
                         in1=PA[:, :], op=ALU.add)
                    p.dma(x1[r0:r0 + 128, :], p1row[:, :])
                p.flush()

        with ExitStack() as _st:
            sw1sb = _st.enter_context(nc.sbuf_tensor("sw1sb", [128, 8, 2048], BF16))
            sw2sb = _st.enter_context(nc.sbuf_tensor("sw2sb", [128, 16, 1024], BF16))
            stg2 = _st.enter_context(nc.sbuf_tensor("stg2", [128, 8, 1024], F32))
            bbsb = _st.enter_context(nc.sbuf_tensor("bbsb", [128, 16, 1], F32))
            bb2rep = _st.enter_context(nc.sbuf_tensor("bb2rep", [128, 1024], F32))
            g1rep = _st.enter_context(nc.sbuf_tensor("g1rep", [128, 1024], F32))
            b1rep = _st.enter_context(nc.sbuf_tensor("b1rep", [128, 1024], F32))
            rowt = _st.enter_context(nc.sbuf_tensor("rowt", [1, 1024], F32))
            x1t = _st.enter_context(nc.sbuf_tensor("x1t", [128, 8, 512], BF16))
            u2t = _st.enter_context(nc.sbuf_tensor("u2t", [128, 8, 512], BF16))
            gatb = _st.enter_context(nc.sbuf_tensor("gatb", [128, 16, 512], BF16))
            xn1f = _st.enter_context(nc.sbuf_tensor("xn1f", [128, 4, 1024], F32))
            x1row = _st.enter_context(nc.sbuf_tensor("x1row", [128, 1024], F32))
            lscr = _st.enter_context(nc.sbuf_tensor("lscr", [128, 1024], F32))
            lscr2 = _st.enter_context(nc.sbuf_tensor("lscr2", [128, 1024], F32))
            xbb = _st.enter_context(nc.sbuf_tensor("xbb", [128, 1024], BF16))
            bcol = _st.enter_context(nc.sbuf_tensor("bcol", [128, 4], F32))
            gmcol = _st.enter_context(nc.sbuf_tensor("gmcol", [128, 1], F32))
            gmrow = _st.enter_context(nc.sbuf_tensor("gmrow", [1, 128], F32))
            w13 = bW1T.rearrange("(g p) c -> p g c", p=128)
            for half in range(2):
                p.dma(stg2[:, :, :], w13[:, :, half * 1024:(half + 1) * 1024])
                p.op("scalar", "activation",
                     out=sw1sb[:, :, half * 1024:(half + 1) * 1024],
                     in_=stg2[:, :, :], func=AF.Sign)
            w23 = bW2T.rearrange("(g p) c -> p g c", p=128)
            for half in range(2):
                p.dma(stg2[:, :, :], w23[:, half * 8:(half + 1) * 8, :])
                p.op("scalar", "activation",
                     out=sw2sb[:, half * 8:(half + 1) * 8, :].rearrange(
                         "p a b -> p (a b)"),
                     in_=stg2[:, :, :].rearrange("p a b -> p (a b)"), func=AF.Sign)
            p.dma(bbsb[:, :, :].rearrange("p a b -> p (a b)"), bb1[:, :])
            for name_row, rep in ((bb2, bb2rep), (l1g, g1rep), (l1b, b1rep)):
                p.dma(rowt[:, :], name_row[:, :])
                for half in range(2):
                    p.op("tensor", "matmul", PB[:, :], ones1[0:1, :],
                         rowt[0:1, half * 512:(half + 1) * 512], start=True, stop=True)
                    p.op("scalar", "copy", out=rep[:, half * 512:(half + 1) * 512],
                         in_=PB[:, :])
            p.op("gpsimd", "memset", gmcol[:, :], 0.0)
            for tco in range(4):
                for i128 in range(4):
                    r0 = tco * 512 + i128 * 128
                    p.dma(x1row[:, :], x1[r0:r0 + 128, :])
                    emit_ln(p, x1row[:, :], f2(xn1f[:, i128:i128 + 1, :]), lscr[:, :], bcol)
                    p.op("vector", "tensor_copy", out=xbb[:, :],
                         in_=f2(xn1f[:, i128:i128 + 1, :]))
                    for g in range(8):
                        p.op("tensor", "transpose", PE[:, :],
                             xbb[:, g * 128:(g + 1) * 128], identb[:, :])
                        p.op("scalar", "copy",
                             out=sl(x1t, g, i128 * 128, (i128 + 1) * 128), in_=PE[:, :])
                    emit_ln(p, f2(xn1f[:, i128:i128 + 1, :]), lscr[:, :], lscr2[:, :], bcol)
                    p.op("vector", "tensor_tensor", out=lscr[:, :], in0=lscr[:, :],
                         in1=g1rep[:, :], op=ALU.mult)
                    p.op("vector", "tensor_tensor", out=lscr[:, :], in0=lscr[:, :],
                         in1=b1rep[:, :], op=ALU.add)
                    p.op("vector", "tensor_copy", out=xbb[:, :], in_=lscr[:, :])
                    for g in range(8):
                        p.op("tensor", "transpose", PE[:, :],
                             xbb[:, g * 128:(g + 1) * 128], identb[:, :])
                        p.op("scalar", "copy",
                             out=sl(u2t, g, i128 * 128, (i128 + 1) * 128), in_=PE[:, :])
                for g in range(8):
                    p.dma(xn1T[g * 128:(g + 1) * 128, tco * 512:(tco + 1) * 512],
                          sl(x1t, g, 0, 512))
                for oc in range(16):
                    for ci in range(8):
                        p.op("tensor", "matmul", PB[:, :],
                             sl(sw1sb, ci, oc * 128, (oc + 1) * 128),
                             sl(u2t, ci, 0, 512), start=(ci == 0), stop=(ci == 7))
                    p.op("vector", "tensor_scalar", out=lscr[:, 0:512], in0=PB[:, :],
                         scalar1=f2(bbsb[:, oc:oc + 1, 0:1]), scalar2=1.0,
                         op0=ALU.add, op1=ALU.min)
                    p.op("vector", "tensor_scalar", out=sl(gatb, oc, 0, 512),
                         in0=lscr[:, 0:512], scalar1=-1.0, scalar2=None, op0=ALU.max)
                for i128 in range(4):
                    for hc in range(16):
                        for half in range(2):
                            p.op("tensor", "matmul",
                                 PA[:, half * 512:(half + 1) * 512],
                                 f2(gatb[:, hc:hc + 1, i128 * 128:(i128 + 1) * 128]),
                                 sl(sw2sb, hc, half * 512, (half + 1) * 512),
                                 start=(hc == 0), stop=(hc == 15),
                                 skip_group_check=True)
                    p.op("vector", "tensor_tensor", out=lscr[:, :],
                         in0=f2(xn1f[:, i128:i128 + 1, :]), in1=bb2rep[:, :], op=ALU.add)
                    p.op("vector", "tensor_tensor", out=lscr[:, :], in0=lscr[:, :],
                         in1=PA[:, :], op=ALU.add)
                    r0 = tco * 512 + i128 * 128
                    p.dma(xbo[r0:r0 + 128, :], lscr[:, :])
                    p.op("vector", "tensor_reduce", out=bcol[:, 0:1], in_=lscr[:, :],
                         axis=AX.X, op=ALU.max, apply_absolute_value=True)
                    p.op("vector", "tensor_tensor", out=gmcol[:, :], in0=gmcol[:, :],
                         in1=bcol[:, 0:1], op=ALU.max)
                p.flush()
            p.op("tensor", "transpose", PC[0:1, :], gmcol[:, :], identf[:, :])
            p.op("scalar", "copy", out=gmrow[0:1, :], in_=PC[0:1, :])
            p.op("vector", "tensor_reduce", out=gmrow[0:1, 0:1], in_=gmrow[0:1, :],
                 axis=AX.X, op=ALU.max)
            p.dma(gmax[:, :], gmrow[0:1, 0:1])
            p.flush(final=True)
    return nc



def build_B():
    nc = bass.Bass("TRN2", target_bir_lowering=False)
    xbh = nc.dram_tensor("xbh", [BW, D], F32, kind="ExternalInput").ap()
    gmax8 = nc.dram_tensor("gmax8", [8, 1], F32, kind="ExternalInput").ap()
    gsel = nc.dram_tensor("gsel", [8, 2], F32, kind="ExternalInput").ap()
    hmask = nc.dram_tensor("hmask", [128, 1], F32, kind="ExternalInput").ap()
    l2g = nc.dram_tensor("l2g", [1, D], F32, kind="ExternalInput").ap()
    l2b = nc.dram_tensor("l2b", [1, D], F32, kind="ExternalInput").ap()
    ipT = nc.dram_tensor("ipT", [D, 2 * DI], F32, kind="ExternalInput").ap()
    convw = nc.dram_tensor("convw", [128, 64], F32, kind="ExternalInput").ap()
    convb = nc.dram_tensor("convb", [128, 16], F32, kind="ExternalInput").ap()
    dparam = nc.dram_tensor("dparam", [128, 16], F32, kind="ExternalInput").ap()
    xpT = nc.dram_tensor("xpT", [DI, 96], F32, kind="ExternalInput").ap()
    dpT = nc.dram_tensor("dpT", [64, DI], F32, kind="ExternalInput").ap()
    dpb = nc.dram_tensor("dpb", [128, 16], F32, kind="ExternalInput").ap()

    qxo = nc.dram_tensor("qxo", [OWN, D], F32, kind="ExternalOutput").ap()
    dtT = nc.dram_tensor("dtT", [DI, OWN], F32, kind="ExternalOutput").ap()
    udT = nc.dram_tensor("udT", [DI, OWN], F32, kind="ExternalOutput").ap()
    dxT = nc.dram_tensor("dxT", [DI, OWN], F32, kind="ExternalOutput").ap()
    szT = nc.dram_tensor("szT", [DI, OWN], F32, kind="ExternalOutput").ap()
    BTo = nc.dram_tensor("BTo", [DS, OWN], F32, kind="ExternalOutput").ap()
    CTo = nc.dram_tensor("CTo", [DS, OWN], F32, kind="ExternalOutput").ap()

    xsf = nc.dram_tensor("xsf", [DI, OWN], F32, kind="Internal").ap()

    with ExitStack() as _st:
        S = _st.enter_context(nc.semaphore("S"))
        PA = _st.enter_context(nc.psum_tensor("PA", [128, 1024], F32))
        PB = _st.enter_context(nc.psum_tensor("PB", [128, 512], F32))
        PC = _st.enter_context(nc.psum_tensor("PC", [128, 128], F32))
        PE = _st.enter_context(nc.psum_tensor("PE", [128, 128], BF16))
        irow = _st.enter_context(nc.sbuf_tensor("irow", [128, 128], F32))
        icol = _st.enter_context(nc.sbuf_tensor("icol", [128, 1], F32))
        identf = _st.enter_context(nc.sbuf_tensor("identf", [128, 128], F32))
        identb = _st.enter_context(nc.sbuf_tensor("identb", [128, 128], BF16))
        ones1 = _st.enter_context(nc.sbuf_tensor("ones1", [1, 128], F32))
        grow = _st.enter_context(nc.sbuf_tensor("grow", [8, 2], F32))
        gm8 = _st.enter_context(nc.sbuf_tensor("gm8", [8, 1], F32))
        gsl = _st.enter_context(nc.sbuf_tensor("gsl", [8, 2], F32))
        rc2g = _st.enter_context(nc.sbuf_tensor("rc2g", [2, 8], F32))
        rc2 = _st.enter_context(nc.sbuf_tensor("rc2", [2, 2], F32))
        sel2 = _st.enter_context(nc.sbuf_tensor("sel2", [2, 2, 128], F32))
        ir2 = _st.enter_context(nc.sbuf_tensor("ir2", [2, 128], F32))
        rco = _st.enter_context(nc.sbuf_tensor("rco", [128, 1], F32))
        rch = _st.enter_context(nc.sbuf_tensor("rch", [128, 1], F32))
        hmsk = _st.enter_context(nc.sbuf_tensor("hmsk", [128, 1], F32))
        grep = _st.enter_context(nc.sbuf_tensor("grep", [128, 1024], F32))
        brep = _st.enter_context(nc.sbuf_tensor("brep", [128, 1024], F32))
        rowt = _st.enter_context(nc.sbuf_tensor("rowt", [1, 1024], F32))
        cwsb = _st.enter_context(nc.sbuf_tensor("cwsb", [128, 16, 4], F32))
        cbsb = _st.enter_context(nc.sbuf_tensor("cbsb", [128, 16, 1], F32))
        dsb = _st.enter_context(nc.sbuf_tensor("dsb", [128, 16, 1], F32))
        dpbsb = _st.enter_context(nc.sbuf_tensor("dpbsb", [128, 16, 1], F32))
        p = Prog(nc, S)
        emit_ident(p, nc, irow, icol, identf, identb, None)
        p.op("gpsimd", "memset", ones1[:, :], 1.0)
        p.dma(gm8[:, :], gmax8[:, :])
        p.dma(gsl[:, :], gsel[:, :])
        p.dma(hmsk[:, :], hmask[:, :])
        p.dma(cwsb[:, :, :].rearrange("p a b -> p (a b)"), convw[:, :])
        p.dma(cbsb[:, :, :].rearrange("p a b -> p (a b)"), convb[:, :])
        p.dma(dsb[:, :, :].rearrange("p a b -> p (a b)"), dparam[:, :])
        p.dma(dpbsb[:, :, :].rearrange("p a b -> p (a b)"), dpb[:, :])
        p.op("vector", "tensor_scalar", out=grow[0:8, :], in0=gsl[0:8, :],
             scalar1=gm8[0:8, 0:1], scalar2=None, op0=ALU.mult)
        p.op("tensor", "transpose", PC[0:2, 0:8], grow[0:8, :], identf[0:8, 0:8])
        p.op("scalar", "copy", out=rc2g[0:2, :], in_=PC[0:2, 0:8])
        p.op("vector", "tensor_reduce", out=rc2[0:2, 0:1], in_=rc2g[0:2, :],
             axis=AX.X, op=ALU.max)
        p.op("vector", "tensor_scalar", out=rc2[0:2, 1:2], in0=rc2[0:2, 0:1],
             scalar1=EPS, scalar2=None, op0=ALU.add)
        p.op("vector", "reciprocal", out=rc2[0:2, 0:1], in_=rc2[0:2, 1:2])
        p.op("vector", "tensor_scalar", out=rc2[0:2, 0:1], in0=rc2[0:2, 0:1],
             scalar1=128.0, scalar2=None, op0=ALU.mult)
        p.op("gpsimd", "iota", ir2[0:2, :], [[0, 128]], channel_multiplier=1,
             allow_small_or_imprecise_dtypes=True)
        for j in range(2):
            p.op("vector", "tensor_scalar", out=f2(sel2[0:2, j:j + 1, :]),
                 in0=ir2[0:2, :], scalar1=float(j), scalar2=None, op0=ALU.is_equal)
        for j, dst in ((0, rco), (1, rch)):
            p.op("tensor", "matmul", PC[:, 0:1], f2(sel2[0:2, j:j + 1, :]),
                 rc2[0:2, 0:1], start=True, stop=True)
            p.op("scalar", "copy", out=dst[:, :], in_=PC[:, 0:1])
        for dram_row, rep in ((l2g, grep), (l2b, brep)):
            p.dma(rowt[:, :], dram_row[:, :])
            for half in range(2):
                p.op("tensor", "matmul", PB[:, :], ones1[0:1, :],
                     rowt[0:1, half * 512:(half + 1) * 512], start=True, stop=True)
                p.op("scalar", "copy", out=rep[:, half * 512:(half + 1) * 512],
                     in_=PB[:, :])
        p.flush()

        UB = 128.0 - EPS
        LB = -128.0 + EPS
        with ExitStack() as _st:
            uT = _st.enter_context(nc.sbuf_tensor("uT", [128, 8, BW], BF16))
            xscb = _st.enter_context(nc.sbuf_tensor("xscb", [128, 16, 2048], BF16))
            xbt = _st.enter_context(nc.sbuf_tensor("xbt", [128, 1024], F32))
            q1 = _st.enter_context(nc.sbuf_tensor("q1", [128, 1024], F32))
            uf = _st.enter_context(nc.sbuf_tensor("uf", [128, 1024], F32))
            lscr = _st.enter_context(nc.sbuf_tensor("lscr", [128, 1024], F32))
            ubb = _st.enter_context(nc.sbuf_tensor("ubb", [128, 1024], BF16))
            bcol = _st.enter_context(nc.sbuf_tensor("bcol", [128, 4], F32))
            for i in range(17):
                p.dma(xbt[:, :], xbh[i * 128:(i + 1) * 128, :])
                rc = rch if i == 0 else rco
                p.op("vector", "tensor_scalar", out=q1[:, :], in0=xbt[:, :],
                     scalar1=rc[:, 0:1], scalar2=UB, op0=ALU.mult, op1=ALU.min)
                p.op("vector", "tensor_scalar", out=q1[:, :], in0=q1[:, :],
                     scalar1=LB, scalar2=None, op0=ALU.max)
                if i > 0:
                    p.dma(qxo[(i - 1) * 128:i * 128, :], q1[:, :])
                emit_ln(p, q1[:, :], uf[:, :], lscr[:, :], bcol)
                p.op("vector", "tensor_tensor", out=uf[:, :], in0=uf[:, :],
                     in1=grep[:, :], op=ALU.mult)
                p.op("vector", "tensor_tensor", out=uf[:, :], in0=uf[:, :],
                     in1=brep[:, :], op=ALU.add)
                if i == 0:
                    p.op("vector", "tensor_scalar", out=uf[:, :], in0=uf[:, :],
                         scalar1=hmsk[:, 0:1], scalar2=None, op0=ALU.mult)
                p.op("vector", "tensor_copy", out=ubb[:, :], in_=uf[:, :])
                for g in range(8):
                    p.op("tensor", "transpose", PE[:, :],
                         ubb[:, g * 128:(g + 1) * 128], identb[:, :])
                    p.op("scalar", "copy", out=sl(uT, g, i * 128, (i + 1) * 128),
                         in_=PE[:, :])
                p.flush()

            ip3 = ipT.rearrange("(ci p) oc -> p ci oc", p=128)
            tchunks = [(0, 512), (512, 1024), (1024, 1536), (1536, 2048), (2048, BW)]
            with ExitStack() as _st:
                ipc = _st.enter_context(nc.sbuf_tensor("ipc", [128, 8, 128], BF16))
                ipstg = _st.enter_context(nc.sbuf_tensor("ipstg", [128, 8, 128], F32))
                xrowf = _st.enter_context(nc.sbuf_tensor("xrowf", [128, BW], F32))
                cv = _st.enter_context(nc.sbuf_tensor("cv", [128, 2048], F32))
                xscf = _st.enter_context(nc.sbuf_tensor("xscf", [128, 2048], F32))
                for oc in range(32):
                    p.dma(ipstg[:, :, :], ip3[:, :, oc * 128:(oc + 1) * 128])
                    p.op("vector", "tensor_copy",
                         out=ipc[:, :, :].rearrange("p a b -> p (a b)"),
                         in_=ipstg[:, :, :].rearrange("p a b -> p (a b)"))
                    for (a, b) in tchunks:
                        for ci in range(8):
                            p.op("tensor", "matmul", PB[:, 0:b - a],
                                 sl(ipc, ci, 0, 128), sl(uT, ci, a, b),
                                 start=(ci == 0), stop=(ci == 7))
                        p.op("scalar", "copy", out=xrowf[:, a:b], in_=PB[:, 0:b - a])
                    if oc < 16:
                        p.op("vector", "tensor_scalar", out=cv[:, :],
                             in0=xrowf[:, BHALO - 3:BHALO - 3 + 2048],
                             scalar1=f2(cwsb[:, oc:oc + 1, 0:1]), scalar2=None,
                             op0=ALU.mult)
                        for tau in range(1, 4):
                            p.op("vector", "scalar_tensor_tensor", out=cv[:, :],
                                 in0=xrowf[:, BHALO - 3 + tau:BHALO - 3 + tau + 2048],
                                 scalar=f2(cwsb[:, oc:oc + 1, tau:tau + 1]),
                                 in1=cv[:, :], op0=ALU.mult, op1=ALU.add)
                        p.op("scalar", "activation", out=xscf[:, :], in_=cv[:, :],
                             func=AF.Silu, bias=f2(cbsb[:, oc:oc + 1, 0:1]))
                        p.dma(xsf[oc * 128:(oc + 1) * 128, :], xscf[:, :])
                        p.op("vector", "tensor_scalar", out=cv[:, :], in0=xscf[:, :],
                             scalar1=f2(dsb[:, oc:oc + 1, 0:1]), scalar2=None,
                             op0=ALU.mult)
                        p.dma(dxT[oc * 128:(oc + 1) * 128, :], cv[:, :])
                        p.op("vector", "tensor_copy", out=sl(xscb, oc, 0, 2048),
                             in_=xscf[:, :])
                    else:
                        zc = oc - 16
                        p.op("scalar", "activation", out=cv[:, :],
                             in_=xrowf[:, BHALO:BW], func=AF.Silu)
                        p.dma(szT[zc * 128:(zc + 1) * 128, :], cv[:, :])
                    p.flush()

            with ExitStack() as _st:
                xpsb = _st.enter_context(nc.sbuf_tensor("xpsb", [128, 16, 96], BF16))
                xpstg = _st.enter_context(nc.sbuf_tensor("xpstg", [128, 16, 96], F32))
                xdb = _st.enter_context(nc.sbuf_tensor("xdb", [128, 96], F32))
                BTsb = _st.enter_context(nc.sbuf_tensor("BTsb", [16, 2048], F32))
                CTsb = _st.enter_context(nc.sbuf_tensor("CTsb", [16, 2048], F32))
                dtl = _st.enter_context(nc.sbuf_tensor("dtl", [64, 2048], F32))
                dtlb = _st.enter_context(nc.sbuf_tensor("dtlb", [64, 2048], BF16))
                dpsb = _st.enter_context(nc.sbuf_tensor("dpsb", [64, 16, 128], BF16))
                dpstg = _st.enter_context(nc.sbuf_tensor("dpstg", [64, 16, 128], F32))
                dtf = _st.enter_context(nc.sbuf_tensor("dtf", [128, 512], F32))
                sf1 = _st.enter_context(nc.sbuf_tensor("sf1", [128, 512], F32))
                sf2 = _st.enter_context(nc.sbuf_tensor("sf2", [128, 512], F32))
                xsrow = _st.enter_context(nc.sbuf_tensor("xsrow", [128, 512], F32))
                p.dma(xpstg[:, :, :], xpT.rearrange("(g p) c -> p g c", p=128))
                p.op("vector", "tensor_copy",
                     out=xpsb[:, :, :].rearrange("p a b -> p (a b)"),
                     in_=xpstg[:, :, :].rearrange("p a b -> p (a b)"))
                for t128 in range(16):
                    for dc in range(16):
                        p.op("tensor", "matmul", PC[:, 0:96],
                             f2(xscb[:, dc:dc + 1, t128 * 128:(t128 + 1) * 128]),
                             sl(xpsb, dc, 0, 96), start=(dc == 0), stop=(dc == 15))
                    p.op("scalar", "copy", out=xdb[:, :], in_=PC[:, 0:96])
                    p.op("tensor", "transpose", PC[0:64, :], xdb[:, 0:64],
                         identf[:, :])
                    p.op("scalar", "copy", out=dtl[:, t128 * 128:(t128 + 1) * 128],
                         in_=PC[0:64, 0:128])
                    p.op("tensor", "transpose", PC[0:16, :], xdb[:, 64:80],
                         identf[:, :])
                    p.op("scalar", "copy", out=BTsb[:, t128 * 128:(t128 + 1) * 128],
                         in_=PC[0:16, 0:128])
                    p.op("tensor", "transpose", PC[0:16, :], xdb[:, 80:96],
                         identf[:, :])
                    p.op("scalar", "copy", out=CTsb[:, t128 * 128:(t128 + 1) * 128],
                         in_=PC[0:16, 0:128])
                p.dma(BTo[:, :], BTsb[:, :])
                p.dma(CTo[:, :], CTsb[:, :])
                p.op("vector", "tensor_copy", out=dtlb[:, :], in_=dtl[:, :])
                p.dma(dpstg[:, :, :], dpT.rearrange("p (g c) -> p g c", g=16))
                p.op("vector", "tensor_copy",
                     out=dpsb[:, :, :].rearrange("p a b -> p (a b)"),
                     in_=dpstg[:, :, :].rearrange("p a b -> p (a b)"))
                p.flush()
                for dc in range(16):
                    for tch in range(4):
                        a = tch * 512
                        p.op("tensor", "matmul", PB[:, :],
                             sl(dpsb, dc, 0, 128), dtlb[:, a:a + 512],
                             start=True, stop=True)
                        emit_softplus(p, PB[:, :], dtf[:, :], sf1[:, :], sf2[:, :],
                                      bias_col=f2(dpbsb[:, dc:dc + 1, 0:1]))
                        p.dma(dtT[dc * 128:(dc + 1) * 128, a:a + 512], dtf[:, :])
                        p.dma(xsrow[:, :], xsf[dc * 128:(dc + 1) * 128, a:a + 512])
                        p.op("vector", "tensor_tensor", out=sf1[:, :],
                             in0=dtf[:, :], in1=xsrow[:, :], op=ALU.mult)
                        p.dma(udT[dc * 128:(dc + 1) * 128, a:a + 512], sf1[:, :])
                    p.flush()
        p.flush(final=True)
    return nc



def build_C():
    nc = bass.Bass("TRN2", target_bir_lowering=False)
    dts = nc.dram_tensor("dts", [256, T], F32, kind="ExternalInput").ap()
    uds = nc.dram_tensor("uds", [256, T], F32, kind="ExternalInput").ap()
    dxs = nc.dram_tensor("dxs", [256, T], F32, kind="ExternalInput").ap()
    szs = nc.dram_tensor("szs", [256, T], F32, kind="ExternalInput").ap()
    alog = nc.dram_tensor("alog", [256, DS], F32, kind="ExternalInput").ap()
    BTg = nc.dram_tensor("BTg", [DS, T], F32, kind="ExternalInput").ap()
    CTg = nc.dram_tensor("CTg", [DS, T], F32, kind="ExternalInput").ap()
    ymo = nc.dram_tensor("ymo", [256, T], F32, kind="ExternalOutput").ap()

    NCK = 32
    CK = T // NCK
    with ExitStack() as _st:
        S = _st.enter_context(nc.semaphore("S"))
        PA = _st.enter_context(nc.psum_tensor("PA", [128, 1024], F32))
        ones1 = _st.enter_context(nc.sbuf_tensor("ones1", [1, 128], F32))
        asb = _st.enter_context(nc.sbuf_tensor("asb", [128, 2, DS], F32))
        hst = _st.enter_context(nc.sbuf_tensor("hst", [128, 2, DS], F32))
        dtc = _st.enter_context(nc.sbuf_tensor("dtc", [128, 2, CK], F32))
        udc = _st.enter_context(nc.sbuf_tensor("udc", [128, 2, CK], F32))
        dxc = _st.enter_context(nc.sbuf_tensor("dxc", [128, 2, CK], F32))
        szc = _st.enter_context(nc.sbuf_tensor("szc", [128, 2, CK], F32))
        BTc = _st.enter_context(nc.sbuf_tensor("BTc", [16, CK], F32))
        CTc = _st.enter_context(nc.sbuf_tensor("CTc", [16, CK], F32))
        da = _st.enter_context(nc.sbuf_tensor("da", [128, CK], F32))
        bb = _st.enter_context(nc.sbuf_tensor("bb", [128, CK], F32))
        hh = _st.enter_context(nc.sbuf_tensor("hh", [128, CK], F32))
        cm = _st.enter_context(nc.sbuf_tensor("cm", [128, CK], F32))
        yac = _st.enter_context(nc.sbuf_tensor("yac", [128, 2, CK], F32))
        sel16 = _st.enter_context(nc.sbuf_tensor("sel16", [16, 16, 128], F32))
        ir16 = _st.enter_context(nc.sbuf_tensor("ir16", [16, 128], F32))
        p = Prog(nc, S)
        p.op("gpsimd", "memset", ones1[:, :], 1.0)
        p.op("gpsimd", "memset", hst[:, :, :], 0.0)
        p.op("gpsimd", "iota", ir16[0:16, :], [[0, 128]], channel_multiplier=1,
             allow_small_or_imprecise_dtypes=True)
        for s in range(DS):
            p.op("vector", "tensor_scalar", out=f2(sel16[0:16, s:s + 1, :]),
                 in0=ir16[0:16, :], scalar1=float(s), scalar2=None, op0=ALU.is_equal)
        p.dma(asb[:, :, :], alog.rearrange("(g p) s -> p g s", p=128))
        p.op("scalar", "activation", out=asb[:, :, :].rearrange("p a b -> p (a b)"),
             in_=asb[:, :, :].rearrange("p a b -> p (a b)"), func=AF.Exp)
        p.op("vector", "tensor_scalar",
             out=asb[:, :, :].rearrange("p a b -> p (a b)"),
             in0=asb[:, :, :].rearrange("p a b -> p (a b)"),
             scalar1=-1.0, scalar2=None, op0=ALU.mult)
        p.flush()
        for ck in range(NCK):
            a = ck * CK
            for dram, sb in ((dts, dtc), (uds, udc), (dxs, dxc), (szs, szc)):
                p.dma(sb[:, :, :],
                      dram.rearrange("(g p) t -> p g t", p=128)[:, :, a:a + CK])
            p.dma(BTc[:, :], BTg[:, a:a + CK])
            p.dma(CTc[:, :], CTg[:, a:a + CK])
            for s in range(DS):
                p.op("tensor", "matmul", PA[:, 0:CK], f2(sel16[:, s:s + 1, :]),
                     BTc[0:16, :], start=True, stop=True)
                p.op("tensor", "matmul", PA[:, CK:2 * CK], f2(sel16[:, s:s + 1, :]),
                     CTc[0:16, :], start=True, stop=True)
                for g in range(2):
                    p.op("scalar", "activation", out=da[:, :],
                         in_=f2(dtc[:, g:g + 1, :]), func=AF.Exp,
                         scale=f2(asb[:, g:g + 1, s:s + 1]))
                    p.op("vector", "tensor_tensor", out=bb[:, :],
                         in0=f2(udc[:, g:g + 1, :]), in1=PA[:, 0:CK], op=ALU.mult)
                    p.op("vector", "tensor_tensor_scan", out=hh[:, :], data0=da[:, :],
                         data1=bb[:, :], initial=f2(hst[:, g:g + 1, s:s + 1]),
                         op0=ALU.mult, op1=ALU.add)
                    p.op("vector", "tensor_copy", out=f2(hst[:, g:g + 1, s:s + 1]),
                         in_=hh[:, CK - 1:CK])
                    p.op("vector", "tensor_tensor", out=cm[:, :], in0=hh[:, :],
                         in1=PA[:, CK:2 * CK], op=ALU.mult)
                    if s == 0:
                        p.op("gpsimd", "tensor_copy", out=f2(yac[:, g:g + 1, :]),
                             in_=cm[:, :])
                    else:
                        p.op("gpsimd", "tensor_add", out=f2(yac[:, g:g + 1, :]),
                             in0=f2(yac[:, g:g + 1, :]), in1=cm[:, :])
            for g in range(2):
                p.op("vector", "tensor_tensor", out=f2(yac[:, g:g + 1, :]),
                     in0=f2(yac[:, g:g + 1, :]), in1=f2(dxc[:, g:g + 1, :]), op=ALU.add)
                p.op("vector", "tensor_tensor", out=f2(yac[:, g:g + 1, :]),
                     in0=f2(yac[:, g:g + 1, :]), in1=f2(szc[:, g:g + 1, :]), op=ALU.mult)
                p.dma(ymo[g * 128:(g + 1) * 128, a:a + CK], f2(yac[:, g:g + 1, :]))
            p.flush()
        p.flush(final=True)
    return nc



def build_D():
    nc = bass.Bass("TRN2", target_bir_lowering=False)
    ymc = nc.dram_tensor("ymc", [DI, OWN], F32, kind="ExternalInput").ap()
    opT = nc.dram_tensor("opT", [DI, D], F32, kind="ExternalInput").ap()
    xn1Ti = nc.dram_tensor("xn1Ti", [D, OWN], BF16, kind="ExternalInput").ap()
    qxi = nc.dram_tensor("qxi", [OWN, D], F32, kind="ExternalInput").ap()
    x1i = nc.dram_tensor("x1i", [OWN, D], F32, kind="ExternalInput").ap()
    gw = nc.dram_tensor("gw", [128, 64], F32, kind="ExternalInput").ap()
    gb = nc.dram_tensor("gb", [1, 8], F32, kind="ExternalInput").ap()
    w1 = nc.dram_tensor("w1", [8, D, 2048], F32, kind="ExternalInput").ap()
    b1c = nc.dram_tensor("b1c", [128, 128], F32, kind="ExternalInput").ap()
    w2 = nc.dram_tensor("w2", [8, 2048, D], F32, kind="ExternalInput").ap()
    b2 = nc.dram_tensor("b2", [8, D], F32, kind="ExternalInput").ap()
    fin = nc.dram_tensor("fin", [OWN, D], F32, kind="ExternalOutput").ap()

    with ExitStack() as _st:
        S = _st.enter_context(nc.semaphore("S"))
        PA = _st.enter_context(nc.psum_tensor("PA", [128, 1024], F32))
        PB = _st.enter_context(nc.psum_tensor("PB", [128, 512], F32))
        PC = _st.enter_context(nc.psum_tensor("PC", [128, 128], F32))
        PF = _st.enter_context(nc.psum_tensor("PF", [128, 128], F32))
        irow = _st.enter_context(nc.sbuf_tensor("irow", [128, 128], F32))
        icol = _st.enter_context(nc.sbuf_tensor("icol", [128, 1], F32))
        identf = _st.enter_context(nc.sbuf_tensor("identf", [128, 128], F32))
        identb = _st.enter_context(nc.sbuf_tensor("identb", [128, 128], BF16))
        ones1 = _st.enter_context(nc.sbuf_tensor("ones1", [1, 128], F32))
        oacc = _st.enter_context(nc.sbuf_tensor("oacc", [128, 16, 1024], F32))
        xnsb = _st.enter_context(nc.sbuf_tensor("xnsb", [128, 8, OWN], BF16))
        wmT = _st.enter_context(nc.sbuf_tensor("wmT", [8, OWN], F32))
        b1sb = _st.enter_context(nc.sbuf_tensor("b1sb", [128, 16, 8], F32))
        b2sb = _st.enter_context(nc.sbuf_tensor("b2sb", [8, 1024], F32))
        gbrep = _st.enter_context(nc.sbuf_tensor("gbrep", [128, 8], F32))
        gwsb = _st.enter_context(nc.sbuf_tensor("gwsb", [128, 8, 8], BF16))
        gwstg = _st.enter_context(nc.sbuf_tensor("gwstg", [128, 8, 8], F32))
        sel8 = _st.enter_context(nc.sbuf_tensor("sel8", [8, 8, 128], F32))
        ir8 = _st.enter_context(nc.sbuf_tensor("ir8", [8, 128], F32))
        p = Prog(nc, S)
        emit_ident(p, nc, irow, icol, identf, identb, None)
        p.op("gpsimd", "memset", ones1[:, :], 1.0)
        p.op("gpsimd", "memset", oacc[:, :, :], 0.0)
        p.op("gpsimd", "iota", ir8[0:8, :], [[0, 128]], channel_multiplier=1,
             allow_small_or_imprecise_dtypes=True)
        for e in range(8):
            p.op("vector", "tensor_scalar", out=f2(sel8[0:8, e:e + 1, :]),
                 in0=ir8[0:8, :], scalar1=float(e), scalar2=None, op0=ALU.is_equal)
        p.dma(xnsb[:, :, :], xn1Ti.rearrange("(g p) t -> p g t", p=128))
        p.dma(b1sb[:, :, :].rearrange("p a b -> p (a b)"), b1c[:, :])
        p.dma(b2sb[:, :], b2[:, :])
        p.dma(gwstg[:, :, :].rearrange("p a b -> p (a b)"), gw[:, :])
        p.op("vector", "tensor_copy",
             out=gwsb[:, :, :].rearrange("p a b -> p (a b)"),
             in_=gwstg[:, :, :].rearrange("p a b -> p (a b)"))
        with ExitStack() as _st:
            gbr = _st.enter_context(nc.sbuf_tensor("gbr", [1, 8], F32))
            p.dma(gbr[:, :], gb[:, :])
            p.op("tensor", "matmul", PC[:, 0:8], ones1[0:1, :], gbr[0:1, :],
                 start=True, stop=True)
            p.op("scalar", "copy", out=gbrep[:, :], in_=PC[:, 0:8])
            p.flush()

            with ExitStack() as _st:
                pf_ = _st.enter_context(nc.sbuf_tensor("pf_", [128, 8], F32))
                prob = _st.enter_context(nc.sbuf_tensor("prob", [128, 8], F32))
                mcol = _st.enter_context(nc.sbuf_tensor("mcol", [128, 4], F32))
                wm = _st.enter_context(nc.sbuf_tensor("wm", [128, 8], F32))
                for t128 in range(16):
                    for ci in range(8):
                        p.op("tensor", "matmul", PC[:, 0:8],
                             f2(xnsb[:, ci:ci + 1, t128 * 128:(t128 + 1) * 128]),
                             sl(gwsb, ci, 0, 8), start=(ci == 0), stop=(ci == 7))
                    p.op("vector", "tensor_tensor", out=pf_[:, :], in0=PC[:, 0:8],
                         in1=gbrep[:, :], op=ALU.add)
                    p.op("vector", "tensor_reduce", out=mcol[:, 0:1], in_=pf_[:, :],
                         axis=AX.X, op=ALU.max)
                    p.op("vector", "tensor_scalar", out=mcol[:, 1:2], in0=mcol[:, 0:1],
                         scalar1=-1.0, scalar2=None, op0=ALU.mult)
                    p.op("scalar", "activation", out=prob[:, :], in_=pf_[:, :],
                         func=AF.Exp, bias=mcol[:, 1:2])
                    p.op("vector", "tensor_reduce", out=mcol[:, 2:3], in_=prob[:, :],
                         axis=AX.X, op=ALU.add)
                    p.op("vector", "reciprocal", out=mcol[:, 3:4], in_=mcol[:, 2:3])
                    p.op("vector", "tensor_scalar", out=prob[:, :], in0=prob[:, :],
                         scalar1=mcol[:, 3:4], scalar2=None, op0=ALU.mult)
                    p.op("vector", "tensor_reduce", out=mcol[:, 0:1], in_=prob[:, :],
                         axis=AX.X, op=ALU.max)
                    p.op("vector", "tensor_scalar", out=wm[:, :], in0=prob[:, :],
                         scalar1=mcol[:, 0:1], scalar2=None, op0=ALU.is_ge)
                    p.op("vector", "scalar_tensor_tensor", out=wm[:, :], in0=wm[:, :],
                         scalar=-2.0, in1=prob[:, :], op0=ALU.mult, op1=ALU.add)
                    p.op("vector", "tensor_reduce", out=mcol[:, 1:2], in_=wm[:, :],
                         axis=AX.X, op=ALU.max)
                    p.op("vector", "tensor_scalar", out=wm[:, :], in0=prob[:, :],
                         scalar1=mcol[:, 1:2], scalar2=None, op0=ALU.is_ge)
                    p.op("vector", "tensor_tensor", out=wm[:, :], in0=wm[:, :],
                         in1=prob[:, :], op=ALU.mult)
                    p.op("tensor", "transpose", PF[0:8, :], wm[:, :], identf[:, :])
                    p.op("scalar", "copy", out=wmT[:, t128 * 128:(t128 + 1) * 128],
                         in_=PF[0:8, 0:128])
                p.flush()

            with ExitStack() as _st:
                w1b = _st.enter_context(nc.sbuf_tensor("w1b", [128, 8, 2048], BF16))
                w2b = _st.enter_context(nc.sbuf_tensor("w2b", [128, 16, 1024], BF16))
                stg = _st.enter_context(nc.sbuf_tensor("stg", [128, 8, 256], F32))
                hTb = _st.enter_context(nc.sbuf_tensor("hTb", [128, 16, 512], BF16))
                hscr = _st.enter_context(nc.sbuf_tensor("hscr", [128, 512], F32))
                wmf = _st.enter_context(nc.sbuf_tensor("wmf", [128, 512], F32))
                for e in range(8):
                    w1e = w1[e:e + 1, :, :].rearrange("e (ci p) h -> p (e ci) h", p=128)
                    for piece in range(8):
                        p.dma(stg[:, :, :], w1e[:, :, piece * 256:(piece + 1) * 256])
                        for ci in range(8):
                            p.op("vector", "tensor_copy",
                                 out=sl(w1b, ci, piece * 256, (piece + 1) * 256),
                                 in_=sl(stg, ci, 0, 256))
                    w2e = w2[e:e + 1, :, :].rearrange("e (hc p) c -> p (e hc) c", p=128)
                    for piece in range(8):
                        p.dma(stg[:, :, :].rearrange("p (a u) c -> p a u c", a=2),
                              w2e[:, piece * 2:(piece + 1) * 2, :].rearrange(
                                  "p a (u c) -> p a u c", u=4))
                        for j in range(8):
                            hc = piece * 2 + j // 4
                            q = j % 4
                            p.op("vector", "tensor_copy",
                                 out=f2(w2b[:, hc:hc + 1, q * 256:(q + 1) * 256]),
                                 in_=sl(stg, j, 0, 256))
                    for tco in range(4):
                        a = tco * 512
                        p.op("tensor", "matmul", PB[:, :], f2(sel8[:, e:e + 1, :]),
                             wmT[0:8, a:a + 512], start=True, stop=True)
                        p.op("scalar", "copy", out=wmf[:, :], in_=PB[:, :])
                        for hc in range(16):
                            for ci in range(8):
                                p.op("tensor", "matmul", PB[:, :],
                                     sl(w1b, ci, hc * 128, (hc + 1) * 128),
                                     sl(xnsb, ci, a, a + 512),
                                     start=(ci == 0), stop=(ci == 7))
                            p.op("scalar", "activation", out=hscr[:, :], in_=PB[:, :],
                                 func=AF.Relu, bias=f2(b1sb[:, hc:hc + 1, e:e + 1]))
                            p.op("vector", "tensor_tensor", out=sl(hTb, hc, 0, 512),
                                 in0=hscr[:, :], in1=wmf[:, :], op=ALU.mult)
                        for i128 in range(4):
                            ti = tco * 4 + i128
                            for hc in range(16):
                                for half in range(2):
                                    p.op("tensor", "matmul",
                                         PA[:, half * 512:(half + 1) * 512],
                                         f2(hTb[:, hc:hc + 1, i128 * 128:(i128 + 1) * 128]),
                                         f2(w2b[:, hc:hc + 1, half * 512:(half + 1) * 512]),
                                         start=(hc == 0), stop=(hc == 15),
                                         skip_group_check=True)
                            p.op("vector", "tensor_tensor",
                                 out=f2(oacc[:, ti:ti + 1, :]),
                                 in0=f2(oacc[:, ti:ti + 1, :]), in1=PA[:, :], op=ALU.add)
                        p.flush()

            with ExitStack() as _st:
                ymstg = _st.enter_context(nc.sbuf_tensor("ymstg", [128, 2048], F32))
                ymbb = _st.enter_context(nc.sbuf_tensor("ymbb", [128, 2048], BF16))
                opstg = _st.enter_context(nc.sbuf_tensor("opstg", [128, 1024], F32))
                opb = _st.enter_context(nc.sbuf_tensor("opb", [128, 1024], BF16))
                b2bb = _st.enter_context(nc.sbuf_tensor("b2bb", [8, 1024], BF16))
                wmbb = _st.enter_context(nc.sbuf_tensor("wmbb", [8, OWN], BF16))
                resrow = _st.enter_context(nc.sbuf_tensor("resrow", [128, 1024], F32))
                p.op("vector", "tensor_copy", out=b2bb[:, :], in_=b2sb[:, :])
                p.op("vector", "tensor_copy", out=wmbb[:, :], in_=wmT[:, :])
                for dc in range(16):
                    p.dma(ymstg[:, :], ymc[dc * 128:(dc + 1) * 128, :])
                    p.op("vector", "tensor_copy", out=ymbb[:, :], in_=ymstg[:, :])
                    p.dma(opstg[:, :], opT[dc * 128:(dc + 1) * 128, :])
                    p.op("vector", "tensor_copy", out=opb[:, :], in_=opstg[:, :])
                    for ti in range(16):
                        for half in range(2):
                            p.op("tensor", "matmul", PA[:, half * 512:(half + 1) * 512],
                                 ymbb[:, ti * 128:(ti + 1) * 128],
                                 opb[:, half * 512:(half + 1) * 512],
                                 start=True, stop=True)
                        p.op("vector", "tensor_tensor", out=f2(oacc[:, ti:ti + 1, :]),
                             in0=f2(oacc[:, ti:ti + 1, :]), in1=PA[:, :], op=ALU.add)
                    p.flush()
                for ti in range(16):
                    for half in range(2):
                        p.op("tensor", "matmul", PA[:, half * 512:(half + 1) * 512],
                             wmbb[:, ti * 128:(ti + 1) * 128],
                             b2bb[:, half * 512:(half + 1) * 512],
                             start=True, stop=True)
                    p.op("vector", "tensor_tensor", out=f2(oacc[:, ti:ti + 1, :]),
                         in0=f2(oacc[:, ti:ti + 1, :]), in1=PA[:, :], op=ALU.add)
                    p.dma(resrow[:, :], x1i[ti * 128:(ti + 1) * 128, :])
                    p.op("vector", "tensor_tensor", out=f2(oacc[:, ti:ti + 1, :]),
                         in0=f2(oacc[:, ti:ti + 1, :]), in1=resrow[:, :], op=ALU.add)
                    p.dma(resrow[:, :], qxi[ti * 128:(ti + 1) * 128, :])
                    p.op("vector", "tensor_tensor", out=f2(oacc[:, ti:ti + 1, :]),
                         in0=f2(oacc[:, ti:ti + 1, :]), in1=resrow[:, :], op=ALU.add)
                    p.dma(fin[ti * 128:(ti + 1) * 128, :], f2(oacc[:, ti:ti + 1, :]))
                p.flush(final=True)
    return nc



def _f(a):
    return np.ascontiguousarray(a, dtype=np.float32)


def swz(a):
    a = np.asarray(a, dtype=np.float32)
    if a.ndim == 1:
        a = a.reshape(-1, 1)
    G = a.shape[0] // 128
    return np.ascontiguousarray(a.reshape(G, 128, a.shape[1]).transpose(1, 0, 2)
                                .reshape(128, G * a.shape[1]))


DBG = {}


def kernel(**inp):
    x = _f(inp["L0_attn_x"]).reshape(T, D)
    xpad = np.concatenate([np.zeros((HALO, D), np.float32), x], axis=0)

    ncA = build_A()
    common_A = dict(
        wrT=_f(inp["L0_attn_W_r"].T), wkT=_f(inp["L0_attn_W_k"].T),
        wvT=_f(inp["L0_attn_W_v"].T), wwT=_f(inp["L0_attn_W_w"].T),
        woT=_f(inp["L0_attn_W_o"].T),
        tmc=swz(np.stack([_f(inp["L0_attn_tm_r"]), _f(inp["L0_attn_tm_k"]),
                          _f(inp["L0_attn_tm_v"]), _f(inp["L0_attn_tm_w"])], axis=1)),
        ucol=swz(np.asarray(inp["L0_attn_u"]).reshape(D, 1)),
        lncol=swz(np.stack([np.tile(_f(inp["L0_attn_ln_w"]), H),
                            np.tile(_f(inp["L0_attn_ln_b"]), H)], axis=1)),
        fw1=_f(inp["L0_ffn_w1"]), fb1=swz(inp["L0_ffn_b1"]),
        fw2=_f(inp["L0_ffn_w2"]), fb2=_f(inp["L0_ffn_b2"]).reshape(1, D),
        bW1T=_f(inp["L1_attn_W1"].T), bb1=swz(inp["L1_attn_b1"]),
        bW2T=_f(inp["L1_attn_W2"].T), bb2=_f(inp["L1_attn_b2"]).reshape(1, D),
        l1g=_f(inp["L1_attn_ln1_g"]).reshape(1, D),
        l1b=_f(inp["L1_attn_ln1_b"]).reshape(1, D),
    )
    in_maps = []
    for c in range(NCORE):
        m = dict(common_A)
        m["xs"] = _f(xpad[c * OWN:c * OWN + WTOT])
        in_maps.append(m)
    resA = run_bass_kernel_spmd(ncA, in_maps, core_ids=list(range(NCORE)))
    x1_g = np.concatenate([np.asarray(r["x1"]) for r in resA.results], axis=0)
    xb_g = np.concatenate([np.asarray(r["xbo"]) for r in resA.results], axis=0)
    gmax8 = np.concatenate([np.asarray(r["gmax"]) for r in resA.results],
                           axis=0).reshape(8, 1).astype(np.float32)

    ncB = build_B()
    xbpad = np.concatenate([np.zeros((BHALO, D), np.float32), xb_g], axis=0)
    common_B = dict(
        gmax8=gmax8,
        l2g=_f(inp["L1_attn_ln2_g"]).reshape(1, D),
        l2b=_f(inp["L1_attn_ln2_b"]).reshape(1, D),
        ipT=_f(inp["L1_attn_in_proj_w"].T),
        convw=swz(np.asarray(inp["L1_attn_conv_w"]).reshape(DI, 4)),
        convb=swz(inp["L1_attn_conv_b"]),
        dparam=swz(inp["L1_attn_D_param"]),
        xpT=_f(inp["L1_attn_x_proj_w"].T),
        dpT=_f(inp["L1_attn_dt_proj_w"].T),
        dpb=swz(inp["L1_attn_dt_proj_b"]),
    )
    in_maps = []
    for c in range(NCORE):
        m = dict(common_B)
        m["xbh"] = _f(xbpad[c * OWN:c * OWN + BW])
        own_grp = (c * OWN) // 4096
        halo_grp = max(c * OWN - 1, 0) // 4096
        gs = np.zeros((8, 2), np.float32)
        gs[2 * own_grp:2 * own_grp + 2, 0] = 1.0
        gs[2 * halo_grp:2 * halo_grp + 2, 1] = 1.0
        m["gsel"] = gs
        m["hmask"] = np.ones((128, 1), np.float32) if c > 0 else np.zeros(
            (128, 1), np.float32)
        in_maps.append(m)
    resB = run_bass_kernel_spmd(ncB, in_maps, core_ids=list(range(NCORE)))
    qx_g = np.concatenate([np.asarray(r["qxo"]) for r in resB.results], axis=0)
    dtT_g = np.concatenate([np.asarray(r["dtT"]) for r in resB.results], axis=1)
    udT_g = np.concatenate([np.asarray(r["udT"]) for r in resB.results], axis=1)
    dxT_g = np.concatenate([np.asarray(r["dxT"]) for r in resB.results], axis=1)
    szT_g = np.concatenate([np.asarray(r["szT"]) for r in resB.results], axis=1)
    BT_g = np.concatenate([np.asarray(r["BTo"]) for r in resB.results], axis=1)
    CT_g = np.concatenate([np.asarray(r["CTo"]) for r in resB.results], axis=1)

    ncC = build_C()
    alog_full = _f(inp["L1_attn_A_log"])
    in_maps = []
    for c in range(NCORE):
        sl_ = slice(c * 256, (c + 1) * 256)
        in_maps.append(dict(
            dts=_f(dtT_g[sl_]), uds=_f(udT_g[sl_]), dxs=_f(dxT_g[sl_]),
            szs=_f(szT_g[sl_]), alog=_f(alog_full[sl_]),
            BTg=_f(BT_g), CTg=_f(CT_g)))
    resC = run_bass_kernel_spmd(ncC, in_maps, core_ids=list(range(NCORE)))
    ym_g = np.concatenate([np.asarray(r["ymo"]) for r in resC.results], axis=0)

    ncD = build_D()
    common_D = dict(
        opT=_f(inp["L1_attn_out_proj_w"].T),
        gw=swz(inp["L1_ffn_gate_w"]),
        gb=_f(inp["L1_ffn_gate_b"]).reshape(1, 8),
        w1=_f(inp["L1_ffn_w1"]), b1c=swz(np.asarray(inp["L1_ffn_b1"]).T),
        w2=_f(inp["L1_ffn_w2"]), b2=_f(inp["L1_ffn_b2"]),
    )
    in_maps = []
    for c in range(NCORE):
        m = dict(common_D)
        m["ymc"] = _f(ym_g[:, c * OWN:(c + 1) * OWN])
        m["xn1Ti"] = np.ascontiguousarray(
            np.asarray(resA.results[c]["xn1T"]), dtype=ml_dtypes.bfloat16)
        m["qxi"] = _f(qx_g[c * OWN:(c + 1) * OWN])
        m["x1i"] = _f(x1_g[c * OWN:(c + 1) * OWN])
        in_maps.append(m)
    resD = run_bass_kernel_spmd(ncD, in_maps, core_ids=list(range(NCORE)))
    out = np.concatenate([np.asarray(r["fin"]) for r in resD.results], axis=0)
    DBG.clear()
    DBG.update(dict(x1=x1_g, xb=xb_g, gmax8=gmax8, qx=qx_g, dtT=dtT_g, udT=udT_g,
                    dxT=dxT_g, szT=szT_g, BT=BT_g, CT=CT_g, ym=ym_g))
    return out.reshape(1, T, D).astype(np.float32)


# version 03
# speedup vs baseline: 1.2382x; 1.2382x over previous
from contextlib import ExitStack

import numpy as np
import ml_dtypes
import concourse.bass as bass
import concourse.mybir as mybir
from concourse.bass_utils import run_bass_kernel_spmd

F32 = mybir.dt.float32
BF16 = mybir.dt.bfloat16
AF = mybir.ActivationFunctionType
ALU = mybir.AluOpType
AX = mybir.AxisListType

NCORE = 8
T = 16384
D = 1024
OWN = 2048
HALO = 512
WTOT = OWN + HALO
NT = WTOT // 128
L = 128
NCHUNK = WTOT // L
OWNCH0 = HALO // L
H = 16
DH = 64
DI = 2048
DS = 16
EPS = 1e-5
BHALO = 128
BW = OWN + BHALO

ENGS = ["sync", "scalar", "vector", "gpsimd", "tensor"]


class Prog:

    def __init__(self, nc, sem):
        self.nc, self.sem, self.items, self.cnt = nc, sem, [], 0
        self.last_eng = None
        self.pe_floor = 0

    def op(self, eng, meth, *a, inc=1, **kw):
        if eng == "tensor" and self.last_eng == "tensor":
            wait = self.pe_floor
        else:
            wait = self.cnt
            if eng == "tensor":
                self.pe_floor = self.cnt
        self.last_eng = eng
        self.items.append((eng, meth, a, kw, wait, inc))
        self.cnt += inc

    def dma(self, out, in_, eng="sync", **kw):
        self.op(eng, "dma_start", inc=16, out=out, in_=in_, **kw)

    def flush(self, final=False):
        if not self.items and not final:
            return
        with self.nc.Block() as block:
            for eng_name in ENGS:
                mine = [it for it in self.items if it[0] == eng_name]
                if not mine and not final:
                    continue

                def body(eng, mine=mine):
                    for (_, meth, a, kw, wait, inc) in mine:
                        if wait > 0:
                            eng.wait_ge(self.sem, wait)
                        getattr(eng, meth)(*a, **kw).then_inc(self.sem, inc)
                    eng.wait_ge(self.sem, self.cnt)

                getattr(block, eng_name)(body)
        self.items = []


def f2(ap):
    return ap.rearrange("p a b -> p (a b)")


def sl(t, i, a, b):
    return f2(t[:, i:i + 1, a:b])


def emit_ident(p, nc, irow, icol, identf, identb, maskut):
    p.op("gpsimd", "iota", irow[:, :], [[1, 128]], channel_multiplier=0,
         allow_small_or_imprecise_dtypes=True)
    p.op("gpsimd", "iota", icol[:, :], [[0, 1]], channel_multiplier=1,
         allow_small_or_imprecise_dtypes=True)
    p.op("vector", "tensor_scalar", out=identf[:, :], in0=irow[:, :],
         scalar1=icol[:, :], scalar2=None, op0=ALU.is_equal)
    p.op("vector", "tensor_copy", out=identb[:, :], in_=identf[:, :])
    if maskut is not None:
        p.op("vector", "tensor_scalar", out=maskut[:, :], in0=irow[:, :],
             scalar1=icol[:, :], scalar2=None, op0=ALU.is_gt)


def emit_ln(p, xin, xout, scr, cols, n=1024, eps=EPS):
    p.op("vector", "tensor_reduce", out=cols[:, 0:1], in_=xin, axis=AX.X, op=ALU.add)
    p.op("vector", "tensor_scalar", out=cols[:, 1:2], in0=cols[:, 0:1],
         scalar1=-1.0 / n, scalar2=None, op0=ALU.mult)
    p.op("vector", "tensor_scalar", out=xout, in0=xin, scalar1=cols[:, 1:2],
         scalar2=None, op0=ALU.add)
    p.op("scalar", "activation", out=scr, in_=xout, func=AF.Square,
         accum_out=cols[:, 2:3])
    p.op("vector", "tensor_scalar", out=cols[:, 3:4], in0=cols[:, 2:3],
         scalar1=1.0 / n, scalar2=eps, op0=ALU.mult, op1=ALU.add)
    p.op("scalar", "activation", out=cols[:, 3:4], in_=cols[:, 3:4], func=AF.Ln)
    p.op("scalar", "activation", out=cols[:, 2:3], in_=cols[:, 3:4], func=AF.Exp,
         scale=-0.5)
    p.op("vector", "tensor_scalar", out=xout, in0=xout, scalar1=cols[:, 2:3],
         scalar2=None, op0=ALU.mult)


def emit_softplus(p, zin, out, f1, f2_, bias_col=None):
    if bias_col is not None:
        p.op("vector", "tensor_scalar", out=f1, in0=zin, scalar1=bias_col,
             scalar2=None, op0=ALU.add)
        zsrc = f1
    else:
        zsrc = zin
    p.op("scalar", "activation", out=f2_, in_=zsrc, func=AF.Abs)
    p.op("scalar", "activation", out=f2_, in_=f2_, func=AF.Exp, scale=-1.0)
    p.op("scalar", "activation", out=f2_, in_=f2_, func=AF.Ln, bias=1.0)
    p.op("vector", "tensor_scalar", out=out, in0=zsrc, scalar1=0.0, scalar2=None,
         op0=ALU.max)
    p.op("vector", "tensor_tensor", out=out, in0=out, in1=f2_, op=ALU.add)



def build_A():
    nc = bass.Bass("TRN2", target_bir_lowering=False)
    xs = nc.dram_tensor("xs", [WTOT, D], F32, kind="ExternalInput").ap()
    wrT = nc.dram_tensor("wrT", [D, D], F32, kind="ExternalInput").ap()
    wkT = nc.dram_tensor("wkT", [D, D], F32, kind="ExternalInput").ap()
    wvT = nc.dram_tensor("wvT", [D, D], F32, kind="ExternalInput").ap()
    wwT = nc.dram_tensor("wwT", [D, D], F32, kind="ExternalInput").ap()
    woT = nc.dram_tensor("woT", [D, D], F32, kind="ExternalInput").ap()
    tmc = nc.dram_tensor("tmc", [128, 32], F32, kind="ExternalInput").ap()
    ucol = nc.dram_tensor("ucol", [128, 8], F32, kind="ExternalInput").ap()
    lncol = nc.dram_tensor("lncol", [128, 16], F32, kind="ExternalInput").ap()
    fw1 = nc.dram_tensor("fw1", [D, 4096], F32, kind="ExternalInput").ap()
    fb1 = nc.dram_tensor("fb1", [128, 32], F32, kind="ExternalInput").ap()
    fw2 = nc.dram_tensor("fw2", [4096, D], F32, kind="ExternalInput").ap()
    fb2 = nc.dram_tensor("fb2", [1, D], F32, kind="ExternalInput").ap()
    bW1T = nc.dram_tensor("bW1T", [D, DI], F32, kind="ExternalInput").ap()
    bb1 = nc.dram_tensor("bb1", [128, 16], F32, kind="ExternalInput").ap()
    bW2T = nc.dram_tensor("bW2T", [DI, D], F32, kind="ExternalInput").ap()
    bb2 = nc.dram_tensor("bb2", [1, D], F32, kind="ExternalInput").ap()
    l1g = nc.dram_tensor("l1g", [1, D], F32, kind="ExternalInput").ap()
    l1b = nc.dram_tensor("l1b", [1, D], F32, kind="ExternalInput").ap()

    x1 = nc.dram_tensor("x1", [OWN, D], F32, kind="ExternalOutput").ap()
    xbo = nc.dram_tensor("xbo", [OWN, D], F32, kind="ExternalOutput").ap()
    xn1T = nc.dram_tensor("xn1T", [D, OWN], BF16, kind="ExternalOutput").ap()
    gmax = nc.dram_tensor("gmax", [1, 1], F32, kind="ExternalOutput").ap()

    rTs = nc.dram_tensor("rTs", [D, WTOT], BF16, kind="Internal").ap()
    kTs = nc.dram_tensor("kTs", [D, WTOT], BF16, kind="Internal").ap()
    wTs = nc.dram_tensor("wTs", [D, WTOT], F32, kind="Internal").ap()
    vs = nc.dram_tensor("vs", [WTOT, D], BF16, kind="Internal").ap()
    p1s = nc.dram_tensor("p1s", [OWN, D], F32, kind="Internal").ap()

    with ExitStack() as _st:
        S = _st.enter_context(nc.semaphore("S"))
        PA = _st.enter_context(nc.psum_tensor("PA", [128, 1024], F32))
        PB = _st.enter_context(nc.psum_tensor("PB", [128, 512], F32))
        PC = _st.enter_context(nc.psum_tensor("PC", [128, 128], F32))
        PD = _st.enter_context(nc.psum_tensor("PD", [128, 16], F32))
        PE = _st.enter_context(nc.psum_tensor("PE", [128, 128], BF16))
        PF = _st.enter_context(nc.psum_tensor("PF", [128, 128], F32))
        irow = _st.enter_context(nc.sbuf_tensor("irow", [128, 128], F32))
        icol = _st.enter_context(nc.sbuf_tensor("icol", [128, 1], F32))
        identf = _st.enter_context(nc.sbuf_tensor("identf", [128, 128], F32))
        identb = _st.enter_context(nc.sbuf_tensor("identb", [128, 128], BF16))
        maskut = _st.enter_context(nc.sbuf_tensor("maskut", [128, 128], F32))
        ones1 = _st.enter_context(nc.sbuf_tensor("ones1", [1, 128], F32))
        ones64 = _st.enter_context(nc.sbuf_tensor("ones64", [128, 1], F32))
        tmsb = _st.enter_context(nc.sbuf_tensor("tmsb", [128, 8, 4], F32))
        usb = _st.enter_context(nc.sbuf_tensor("usb", [128, 8, 1], F32))
        lnsb = _st.enter_context(nc.sbuf_tensor("lnsb", [128, 8, 2], F32))
        wosb = _st.enter_context(nc.sbuf_tensor("wosb", [128, 8, 1024], BF16))
        Ssb = _st.enter_context(nc.sbuf_tensor("Ssb", [128, 8, 64], F32))
        p = Prog(nc, S)
        emit_ident(p, nc, irow, icol, identf, identb, maskut)
        p.op("gpsimd", "memset", ones1[:, :], 1.0)
        p.op("gpsimd", "memset", ones64[:, :], 1.0)
        p.op("gpsimd", "memset", Ssb[:, :, :], 0.0)
        p.dma(tmsb[:, :, :].rearrange("p a b -> p (a b)"), tmc[:, :])
        p.dma(usb[:, :, :].rearrange("p a b -> p (a b)"), ucol[:, :])
        p.dma(lnsb[:, :, :].rearrange("p a b -> p (a b)"), lncol[:, :])

        with ExitStack() as _st:
            xnt = _st.enter_context(nc.sbuf_tensor("xnt", [128, 8, WTOT + 1], BF16))
            stg = _st.enter_context(nc.sbuf_tensor("stg", [128, 8, 512], F32))
            with ExitStack() as _st:
                xtile = _st.enter_context(nc.sbuf_tensor("xtile", [128, 1024], F32))
                lnx = _st.enter_context(nc.sbuf_tensor("lnx", [128, 1024], F32))
                xnb = _st.enter_context(nc.sbuf_tensor("xnb", [128, 1024], BF16))
                cols = _st.enter_context(nc.sbuf_tensor("cols", [128, 4], F32))
                for g in range(8):
                    p.op("gpsimd", "memset", sl(xnt, g, 0, 1), 0.0)
                for i in range(NT):
                    p.dma(xtile[:, :], xs[i * 128:(i + 1) * 128, :])
                    emit_ln(p, xtile[:, :], lnx[:, :], xtile[:, :], cols)
                    p.op("vector", "tensor_copy", out=xnb[:, :], in_=lnx[:, :])
                    for g in range(8):
                        p.op("tensor", "transpose", PE[:, :],
                             xnb[:, g * 128:(g + 1) * 128], identb[:, :])
                        p.op("scalar", "copy",
                             out=sl(xnt, g, 1 + i * 128, 1 + (i + 1) * 128),
                             in_=PE[:, :])
                p.flush()

            with ExitStack() as _st:
                wrsb = _st.enter_context(nc.sbuf_tensor("wrsb", [128, 8, 1024], BF16))
                wksb = _st.enter_context(nc.sbuf_tensor("wksb", [128, 8, 1024], BF16))
                wvsb = _st.enter_context(nc.sbuf_tensor("wvsb", [128, 8, 1024], BF16))
                wwsb = _st.enter_context(nc.sbuf_tensor("wwsb", [128, 8, 1024], BF16))
                mixsb = _st.enter_context(nc.sbuf_tensor("mixsb", [128, 32, 512], BF16))
                dstrip = _st.enter_context(nc.sbuf_tensor("dstrip", [128, 512], F32))
                xpf = _st.enter_context(nc.sbuf_tensor("xpf", [128, 512], F32))
                ev1 = _st.enter_context(nc.sbuf_tensor("ev1", [128, 512], F32))
                ev2 = _st.enter_context(nc.sbuf_tensor("ev2", [128, 512], F32))
                evb = _st.enter_context(nc.sbuf_tensor("evb", [128, 512], BF16))
                for wdram, wsb in ((wrT, wrsb), (wkT, wksb), (wvT, wvsb),
                                   (wwT, wwsb), (woT, wosb)):
                    wv3 = wdram.rearrange("(g p) c -> p g c", p=128)
                    for half in range(2):
                        p.dma(stg[:, :, :], wv3[:, :, half * 512:(half + 1) * 512])
                        for g in range(8):
                            p.op("vector", "tensor_copy",
                                 out=sl(wsb, g, half * 512, (half + 1) * 512),
                                 in_=sl(stg, g, 0, 512))
                for tc in range(5):
                    c0 = tc * 512
                    for ci in range(8):
                        p.op("vector", "tensor_tensor", out=dstrip[:, :],
                             in0=sl(xnt, ci, 1 + c0, 1 + c0 + 512),
                             in1=sl(xnt, ci, c0, c0 + 512), op=ALU.subtract)
                        p.op("vector", "tensor_copy", out=xpf[:, :],
                             in_=sl(xnt, ci, c0, c0 + 512))
                        for ai in range(4):
                            p.op("vector", "scalar_tensor_tensor",
                                 out=sl(mixsb, ai * 8 + ci, 0, 512),
                                 in0=dstrip[:, :], scalar=f2(tmsb[:, ci:ci + 1, ai:ai + 1]),
                                 in1=xpf[:, :], op0=ALU.mult, op1=ALU.add)
                    for g in range(8):
                        for ai, wsb, spill in ((0, wrsb, rTs), (1, wksb, kTs),
                                               (3, wwsb, wTs)):
                            for ci in range(8):
                                p.op("tensor", "matmul", PB[:, :],
                                     sl(wsb, ci, g * 128, (g + 1) * 128),
                                     sl(mixsb, ai * 8 + ci, 0, 512),
                                     start=(ci == 0), stop=(ci == 7))
                            dst = spill[g * 128:(g + 1) * 128, c0:c0 + 512]
                            if ai == 0:
                                p.op("scalar", "activation", out=evb[:, :],
                                     in_=PB[:, :], func=AF.Sigmoid)
                                p.dma(dst, evb[:, :])
                            elif ai == 1:
                                p.op("vector", "tensor_copy", out=evb[:, :],
                                     in_=PB[:, :])
                                p.dma(dst, evb[:, :])
                            else:
                                p.op("scalar", "activation", out=ev1[:, :],
                                     in_=PB[:, :], func=AF.Abs)
                                p.op("scalar", "activation", out=ev1[:, :],
                                     in_=ev1[:, :], func=AF.Exp, scale=-1.0)
                                p.op("scalar", "activation", out=ev1[:, :],
                                     in_=ev1[:, :], func=AF.Ln, bias=1.0)
                                p.op("vector", "tensor_scalar", out=ev2[:, :],
                                     in0=PB[:, :], scalar1=0.0, scalar2=-1e-4,
                                     op0=ALU.min, op1=ALU.add)
                                p.op("vector", "tensor_tensor", out=ev2[:, :],
                                     in0=ev2[:, :], in1=ev1[:, :], op=ALU.subtract)
                                p.dma(dst, ev2[:, :])
                    for i128 in range(4):
                        ti = tc * 4 + i128
                        for half in range(2):
                            for ci in range(8):
                                p.op("tensor", "matmul", PB[:, :],
                                     f2(mixsb[:, 2 * 8 + ci:2 * 8 + ci + 1,
                                              i128 * 128:(i128 + 1) * 128]),
                                     sl(wvsb, ci, half * 512, (half + 1) * 512),
                                     start=(ci == 0), stop=(ci == 7))
                            p.op("vector", "tensor_copy", out=evb[:, :], in_=PB[:, :])
                            p.dma(vs[ti * 128:(ti + 1) * 128,
                                     half * 512:(half + 1) * 512], evb[:, :])
                    p.flush()

            with ExitStack() as _st:
                fw1sb = _st.enter_context(nc.sbuf_tensor("fw1sb", [128, 8, 4096], BF16))
                fbsb = _st.enter_context(nc.sbuf_tensor("fbsb", [128, 32, 1], F32))
                b2row = _st.enter_context(nc.sbuf_tensor("b2row", [1, 1024], F32))
                b2rep = _st.enter_context(nc.sbuf_tensor("b2rep", [128, 1024], F32))
                hsb = _st.enter_context(nc.sbuf_tensor("hsb", [128, 32, 512], BF16))
                fw2c = _st.enter_context(nc.sbuf_tensor("fw2c", [128, 1024], BF16))
                oacc = _st.enter_context(nc.sbuf_tensor("oacc", [128, 4, 1024], F32))
                xrow = _st.enter_context(nc.sbuf_tensor("xrow", [128, 1024], F32))
                fw13 = fw1.rearrange("(g p) c -> p g c", p=128)
                for piece in range(8):
                    p.dma(stg[:, :, :], fw13[:, :, piece * 512:(piece + 1) * 512])
                    for g in range(8):
                        p.op("vector", "tensor_copy",
                             out=sl(fw1sb, g, piece * 512, (piece + 1) * 512),
                             in_=sl(stg, g, 0, 512))
                p.dma(fbsb[:, :, :].rearrange("p a b -> p (a b)"), fb1[:, :])
                p.dma(b2row[:, :], fb2[:, :])
                for half in range(2):
                    p.op("tensor", "matmul", PB[:, :], ones1[0:1, :],
                         b2row[0:1, half * 512:(half + 1) * 512], start=True, stop=True)
                    p.op("scalar", "copy",
                         out=b2rep[:, half * 512:(half + 1) * 512], in_=PB[:, :])
                for tco in range(4):
                    c0 = 1 + HALO + tco * 512
                    p.op("gpsimd", "memset", oacc[:, :, :], 0.0)
                    for hc in range(32):
                        for ci in range(8):
                            p.op("tensor", "matmul", PB[:, :],
                                 sl(fw1sb, ci, hc * 128, (hc + 1) * 128),
                                 sl(xnt, ci, c0, c0 + 512),
                                 start=(ci == 0), stop=(ci == 7))
                        p.op("scalar", "activation", out=sl(hsb, hc, 0, 512),
                             in_=PB[:, :], func=AF.Gelu,
                             bias=f2(fbsb[:, hc:hc + 1, 0:1]))
                    for hc in range(32):
                        p.dma(stg[:, 0:2, :],
                              fw2[hc * 128:(hc + 1) * 128, :].rearrange(
                                  "p (a c) -> p a c", a=2))
                        p.op("vector", "tensor_copy", out=fw2c[:, :],
                             in_=stg[:, 0:2, :].rearrange("p a c -> p (a c)"))
                        for i128 in range(4):
                            for half in range(2):
                                p.op("tensor", "matmul",
                                     PA[:, half * 512:(half + 1) * 512],
                                     f2(hsb[:, hc:hc + 1, i128 * 128:(i128 + 1) * 128]),
                                     fw2c[:, half * 512:(half + 1) * 512],
                                     start=True, stop=True)
                            p.op("vector", "tensor_tensor",
                                 out=f2(oacc[:, i128:i128 + 1, :]),
                                 in0=f2(oacc[:, i128:i128 + 1, :]),
                                 in1=PA[:, :], op=ALU.add)
                    for i128 in range(4):
                        r0 = HALO + tco * 512 + i128 * 128
                        p.dma(xrow[:, :], xs[r0:r0 + 128, :])
                        p.op("vector", "tensor_tensor", out=xrow[:, :], in0=xrow[:, :],
                             in1=b2rep[:, :], op=ALU.add)
                        p.op("vector", "tensor_tensor", out=xrow[:, :], in0=xrow[:, :],
                             in1=f2(oacc[:, i128:i128 + 1, :]), op=ALU.add)
                        p.dma(p1s[tco * 512 + i128 * 128:tco * 512 + (i128 + 1) * 128, :],
                              xrow[:, :])
                    p.flush()

        with ExitStack() as _st:
            rsp = _st.enter_context(nc.sbuf_tensor("rsp", [128, 8, 512], BF16))
            ksp = _st.enter_context(nc.sbuf_tensor("ksp", [128, 8, 512], BF16))
            wsp = _st.enter_context(nc.sbuf_tensor("wsp", [128, 8, 512], F32))
            vsp = _st.enter_context(nc.sbuf_tensor("vsp", [128, 4, 1024], BF16))
            zero128 = _st.enter_context(nc.sbuf_tensor("zero128", [128, 128], F32))
            cumw = _st.enter_context(nc.sbuf_tensor("cumw", [128, 128], F32))
            cw1 = _st.enter_context(nc.sbuf_tensor("cw1", [128, 128], F32))
            tf1 = _st.enter_context(nc.sbuf_tensor("tf1", [128, 128], F32))
            tf2 = _st.enter_context(nc.sbuf_tensor("tf2", [128, 128], F32))
            tf4 = _st.enter_context(nc.sbuf_tensor("tf4", [128, 128], F32))
            rf = _st.enter_context(nc.sbuf_tensor("rf", [128, 128], F32))
            kf = _st.enter_context(nc.sbuf_tensor("kf", [128, 128], F32))
            rtm = _st.enter_context(nc.sbuf_tensor("rtm", [128, 128], BF16))
            ktm = _st.enter_context(nc.sbuf_tensor("ktm", [128, 128], BF16))
            ksb = _st.enter_context(nc.sbuf_tensor("ksb", [128, 128], BF16))
            rsb = _st.enter_context(nc.sbuf_tensor("rsb", [128, 128], BF16))
            Gm0 = _st.enter_context(nc.sbuf_tensor("Gm0", [128, 128], BF16))
            Gm1 = _st.enter_context(nc.sbuf_tensor("Gm1", [128, 128], BF16))
            Sbb = _st.enter_context(nc.sbuf_tensor("Sbb", [128, 64], BF16))
            kstsb = _st.enter_context(nc.sbuf_tensor("kstsb", [128, 128], BF16))
            osb = _st.enter_context(nc.sbuf_tensor("osb", [128, 128], F32))
            o2b = _st.enter_context(nc.sbuf_tensor("o2b", [128, 128], BF16))
            dcsb = _st.enter_context(nc.sbuf_tensor("dcsb", [128, 2], F32))
            ccol = _st.enter_context(nc.sbuf_tensor("ccol", [128, 1], F32))
            gcol = _st.enter_context(nc.sbuf_tensor("gcol", [128, 4], F32))
            p1row = _st.enter_context(nc.sbuf_tensor("p1row", [128, 1024], F32))
            p.op("gpsimd", "memset", zero128[:, :], 0.0)
            for c in range(NCHUNK):
                span, vidx = c // 4, c % 4
                if vidx == 0:
                    s0 = span * 4 * 128
                    for dram, sb in ((rTs, rsp), (kTs, ksp), (wTs, wsp)):
                        p.dma(sb[:, :, :],
                              dram.rearrange("(g p) t -> p g t", p=128)[
                                  :, :, s0:s0 + 512])
                    p.dma(vsp[:, :, :],
                          vs.rearrange("(n p) c -> p n c", p=128)[
                              :, span * 4:span * 4 + 4, :])
                t0 = vidx * 128
                for g in range(8):
                    wch = sl(wsp, g, t0, t0 + 128)
                    p.op("vector", "tensor_tensor_scan", out=cumw[:, :], data0=wch,
                         data1=zero128[:, :], initial=0.0, op0=ALU.add, op1=ALU.add)
                    p.op("vector", "tensor_tensor", out=cw1[:, :], in0=cumw[:, :],
                         in1=wch, op=ALU.subtract)
                    cmid = cumw[:, 63:64]
                    dlast = cumw[:, 127:128]
                    p.op("vector", "tensor_copy", out=rf[:, :], in_=sl(rsp, g, t0, t0 + 128))
                    p.op("vector", "tensor_copy", out=kf[:, :], in_=sl(ksp, g, t0, t0 + 128))
                    p.op("vector", "tensor_scalar", out=tf1[:, :], in0=cw1[:, :],
                         scalar1=cmid, scalar2=None, op0=ALU.subtract)
                    p.op("scalar", "activation", out=tf1[:, :], in_=tf1[:, :], func=AF.Exp)
                    p.op("vector", "tensor_tensor", out=rtm[:, :], in0=tf1[:, :],
                         in1=rf[:, :], op=ALU.mult)
                    p.op("vector", "tensor_scalar", out=tf2[:, :], in0=cumw[:, :],
                         scalar1=-1.0, scalar2=cmid, op0=ALU.mult, op1=ALU.add)
                    p.op("scalar", "activation", out=tf2[:, :], in_=tf2[:, :], func=AF.Exp)
                    p.op("vector", "tensor_tensor", out=ktm[:, :], in0=tf2[:, :],
                         in1=kf[:, :], op=ALU.mult)
                    p.op("vector", "tensor_scalar", out=tf2[:, :], in0=cumw[:, :],
                         scalar1=-1.0, scalar2=dlast, op0=ALU.mult, op1=ALU.add)
                    p.op("scalar", "activation", out=tf2[:, :], in_=tf2[:, :], func=AF.Exp)
                    p.op("vector", "tensor_tensor", out=ksb[:, :], in0=tf2[:, :],
                         in1=kf[:, :], op=ALU.mult)
                    p.op("scalar", "activation", out=tf4[:, :], in_=cw1[:, :], func=AF.Exp)
                    p.op("vector", "tensor_tensor", out=rsb[:, :], in0=tf4[:, :],
                         in1=rf[:, :], op=ALU.mult)
                    p.op("scalar", "activation", out=tf4[:, :],
                         in_=sl(ksp, g, t0, t0 + 128), func=AF.Exp,
                         bias=f2(usb[:, g:g + 1, 0:1]))
                    p.op("vector", "tensor_tensor", out=tf4[:, :], in0=tf4[:, :],
                         in1=rf[:, :], op=ALU.mult)
                    p.op("scalar", "activation", out=ccol[:, 0:1], in_=dlast, func=AF.Exp)
                    for h in range(2):
                        p.op("tensor", "matmul", PD[:, h:h + 1],
                             tf4[h * 64:(h + 1) * 64, :],
                             ones64[h * 64:(h + 1) * 64, :],
                             start=True, stop=True)
                    p.op("scalar", "copy", out=dcsb[:, 0:2], in_=PD[:, 0:2])
                    for h, Gm in ((0, Gm0), (1, Gm1)):
                        p.op("tensor", "matmul", PC[:, :],
                             ktm[h * 64:(h + 1) * 64, :], rtm[h * 64:(h + 1) * 64, :],
                             start=True, stop=True)
                        p.op("vector", "tensor_scalar", out=tf1[:, :], in0=PC[:, :],
                             scalar1=1e30, scalar2=-1e30, op0=ALU.min, op1=ALU.max)
                        p.op("vector", "tensor_tensor", out=Gm[:, :], in0=tf1[:, :],
                             in1=maskut[:, :], op=ALU.mult)
                    p.op("vector", "tensor_copy", out=Sbb[:, :], in_=f2(Ssb[:, g:g + 1, :]))
                    for h, Gm in ((0, Gm0), (1, Gm1)):
                        vch = f2(vsp[:, vidx:vidx + 1, g * 128 + h * 64:g * 128 + (h + 1) * 64])
                        p.op("tensor", "matmul", PF[:, h * 64:(h + 1) * 64],
                             Gm[:, :], vch, start=True, stop=False)
                        p.op("tensor", "matmul", PF[:, h * 64:(h + 1) * 64],
                             rsb[h * 64:(h + 1) * 64, :], Sbb[h * 64:(h + 1) * 64, :],
                             start=False, stop=True)
                    for h in range(2):
                        vch = f2(vsp[:, vidx:vidx + 1, g * 128 + h * 64:g * 128 + (h + 1) * 64])
                        p.op("vector", "scalar_tensor_tensor",
                             out=osb[:, h * 64:(h + 1) * 64], in0=vch,
                             scalar=dcsb[:, h:h + 1], in1=PF[:, h * 64:(h + 1) * 64],
                             op0=ALU.mult, op1=ALU.add)
                    p.op("vector", "tensor_reduce", out=gcol[:, 0:2],
                         in_=osb[:, :].rearrange("p (a b) -> p a b", a=2),
                         axis=AX.X, op=ALU.add)
                    p.op("vector", "tensor_scalar", out=gcol[:, 2:4], in0=gcol[:, 0:2],
                         scalar1=-1.0 / 64, scalar2=None, op0=ALU.mult)
                    for h in range(2):
                        p.op("vector", "tensor_scalar", out=osb[:, h * 64:(h + 1) * 64],
                             in0=osb[:, h * 64:(h + 1) * 64],
                             scalar1=gcol[:, 2 + h:3 + h], scalar2=None, op0=ALU.add)
                        p.op("scalar", "activation", out=tf4[:, h * 64:(h + 1) * 64],
                             in_=osb[:, h * 64:(h + 1) * 64], func=AF.Square,
                             accum_out=gcol[:, h:h + 1])
                    p.op("vector", "tensor_scalar", out=gcol[:, 2:4], in0=gcol[:, 0:2],
                         scalar1=1.0 / 64, scalar2=EPS, op0=ALU.mult, op1=ALU.add)
                    p.op("scalar", "activation", out=gcol[:, 2:4], in_=gcol[:, 2:4],
                         func=AF.Ln)
                    p.op("scalar", "activation", out=gcol[:, 0:2], in_=gcol[:, 2:4],
                         func=AF.Exp, scale=-0.5)
                    for h in range(2):
                        p.op("vector", "tensor_scalar", out=osb[:, h * 64:(h + 1) * 64],
                             in0=osb[:, h * 64:(h + 1) * 64],
                             scalar1=gcol[:, h:h + 1], scalar2=None, op0=ALU.mult)
                    p.op("tensor", "transpose", PC[:, :], osb[:, :], identf[:, :])
                    p.op("vector", "tensor_scalar", out=tf4[:, :], in0=PC[:, :],
                         scalar1=f2(lnsb[:, g:g + 1, 0:1]),
                         scalar2=f2(lnsb[:, g:g + 1, 1:2]), op0=ALU.mult, op1=ALU.add)
                    p.op("vector", "tensor_tensor", out=o2b[:, :], in0=tf4[:, :],
                         in1=rf[:, :], op=ALU.mult)
                    for half in range(2):
                        p.op("tensor", "matmul", PA[:, half * 512:(half + 1) * 512],
                             o2b[:, :], sl(wosb, g, half * 512, (half + 1) * 512),
                             start=(g == 0), stop=(g == 7), skip_group_check=True)
                    p.op("tensor", "transpose", PE[:, :], ksb[:, :], identb[:, :])
                    p.op("scalar", "copy", out=kstsb[:, :], in_=PE[:, :])
                    for h in range(2):
                        vch = f2(vsp[:, vidx:vidx + 1, g * 128 + h * 64:g * 128 + (h + 1) * 64])
                        p.op("tensor", "matmul", PF[:, h * 64:(h + 1) * 64],
                             kstsb[:, :], vch, start=True, stop=True)
                    for h in range(2):
                        p.op("vector", "scalar_tensor_tensor",
                             out=f2(Ssb[h * 64:(h + 1) * 64, g:g + 1, :]),
                             in0=f2(Ssb[h * 64:(h + 1) * 64, g:g + 1, :]),
                             scalar=ccol[h * 64:(h + 1) * 64, 0:1],
                             in1=PF[h * 64:(h + 1) * 64, h * 64:(h + 1) * 64],
                             op0=ALU.mult, op1=ALU.add)
                if c >= OWNCH0:
                    r0 = (c - OWNCH0) * 128
                    p.dma(p1row[:, :], p1s[r0:r0 + 128, :])
                    p.op("vector", "tensor_tensor", out=p1row[:, :], in0=p1row[:, :],
                         in1=PA[:, :], op=ALU.add)
                    p.dma(x1[r0:r0 + 128, :], p1row[:, :])
                p.flush()

        with ExitStack() as _st:
            sw1sb = _st.enter_context(nc.sbuf_tensor("sw1sb", [128, 8, 2048], BF16))
            sw2sb = _st.enter_context(nc.sbuf_tensor("sw2sb", [128, 16, 1024], BF16))
            stg2 = _st.enter_context(nc.sbuf_tensor("stg2", [128, 8, 1024], F32))
            bbsb = _st.enter_context(nc.sbuf_tensor("bbsb", [128, 16, 1], F32))
            bb2rep = _st.enter_context(nc.sbuf_tensor("bb2rep", [128, 1024], F32))
            g1rep = _st.enter_context(nc.sbuf_tensor("g1rep", [128, 1024], F32))
            b1rep = _st.enter_context(nc.sbuf_tensor("b1rep", [128, 1024], F32))
            rowt = _st.enter_context(nc.sbuf_tensor("rowt", [1, 1024], F32))
            x1t = _st.enter_context(nc.sbuf_tensor("x1t", [128, 8, 512], BF16))
            u2t = _st.enter_context(nc.sbuf_tensor("u2t", [128, 8, 512], BF16))
            gatb = _st.enter_context(nc.sbuf_tensor("gatb", [128, 16, 512], BF16))
            xn1f = _st.enter_context(nc.sbuf_tensor("xn1f", [128, 4, 1024], F32))
            x1row = _st.enter_context(nc.sbuf_tensor("x1row", [128, 1024], F32))
            lscr = _st.enter_context(nc.sbuf_tensor("lscr", [128, 1024], F32))
            lscr2 = _st.enter_context(nc.sbuf_tensor("lscr2", [128, 1024], F32))
            xbb = _st.enter_context(nc.sbuf_tensor("xbb", [128, 1024], BF16))
            bcol = _st.enter_context(nc.sbuf_tensor("bcol", [128, 4], F32))
            gmcol = _st.enter_context(nc.sbuf_tensor("gmcol", [128, 1], F32))
            gmrow = _st.enter_context(nc.sbuf_tensor("gmrow", [1, 128], F32))
            w13 = bW1T.rearrange("(g p) c -> p g c", p=128)
            for half in range(2):
                p.dma(stg2[:, :, :], w13[:, :, half * 1024:(half + 1) * 1024])
                p.op("scalar", "activation",
                     out=sw1sb[:, :, half * 1024:(half + 1) * 1024],
                     in_=stg2[:, :, :], func=AF.Sign)
            w23 = bW2T.rearrange("(g p) c -> p g c", p=128)
            for half in range(2):
                p.dma(stg2[:, :, :], w23[:, half * 8:(half + 1) * 8, :])
                p.op("scalar", "activation",
                     out=sw2sb[:, half * 8:(half + 1) * 8, :].rearrange(
                         "p a b -> p (a b)"),
                     in_=stg2[:, :, :].rearrange("p a b -> p (a b)"), func=AF.Sign)
            p.dma(bbsb[:, :, :].rearrange("p a b -> p (a b)"), bb1[:, :])
            for name_row, rep in ((bb2, bb2rep), (l1g, g1rep), (l1b, b1rep)):
                p.dma(rowt[:, :], name_row[:, :])
                for half in range(2):
                    p.op("tensor", "matmul", PB[:, :], ones1[0:1, :],
                         rowt[0:1, half * 512:(half + 1) * 512], start=True, stop=True)
                    p.op("scalar", "copy", out=rep[:, half * 512:(half + 1) * 512],
                         in_=PB[:, :])
            p.op("gpsimd", "memset", gmcol[:, :], 0.0)
            for tco in range(4):
                for i128 in range(4):
                    r0 = tco * 512 + i128 * 128
                    p.dma(x1row[:, :], x1[r0:r0 + 128, :])
                    emit_ln(p, x1row[:, :], f2(xn1f[:, i128:i128 + 1, :]), lscr[:, :], bcol)
                    p.op("vector", "tensor_copy", out=xbb[:, :],
                         in_=f2(xn1f[:, i128:i128 + 1, :]))
                    for g in range(8):
                        p.op("tensor", "transpose", PE[:, :],
                             xbb[:, g * 128:(g + 1) * 128], identb[:, :])
                        p.op("scalar", "copy",
                             out=sl(x1t, g, i128 * 128, (i128 + 1) * 128), in_=PE[:, :])
                    emit_ln(p, f2(xn1f[:, i128:i128 + 1, :]), lscr[:, :], lscr2[:, :], bcol)
                    p.op("vector", "tensor_tensor", out=lscr[:, :], in0=lscr[:, :],
                         in1=g1rep[:, :], op=ALU.mult)
                    p.op("vector", "tensor_tensor", out=lscr[:, :], in0=lscr[:, :],
                         in1=b1rep[:, :], op=ALU.add)
                    p.op("vector", "tensor_copy", out=xbb[:, :], in_=lscr[:, :])
                    for g in range(8):
                        p.op("tensor", "transpose", PE[:, :],
                             xbb[:, g * 128:(g + 1) * 128], identb[:, :])
                        p.op("scalar", "copy",
                             out=sl(u2t, g, i128 * 128, (i128 + 1) * 128), in_=PE[:, :])
                for g in range(8):
                    p.dma(xn1T[g * 128:(g + 1) * 128, tco * 512:(tco + 1) * 512],
                          sl(x1t, g, 0, 512))
                for oc in range(16):
                    for ci in range(8):
                        p.op("tensor", "matmul", PB[:, :],
                             sl(sw1sb, ci, oc * 128, (oc + 1) * 128),
                             sl(u2t, ci, 0, 512), start=(ci == 0), stop=(ci == 7))
                    p.op("vector", "tensor_scalar", out=lscr[:, 0:512], in0=PB[:, :],
                         scalar1=f2(bbsb[:, oc:oc + 1, 0:1]), scalar2=1.0,
                         op0=ALU.add, op1=ALU.min)
                    p.op("vector", "tensor_scalar", out=sl(gatb, oc, 0, 512),
                         in0=lscr[:, 0:512], scalar1=-1.0, scalar2=None, op0=ALU.max)
                for i128 in range(4):
                    for hc in range(16):
                        for half in range(2):
                            p.op("tensor", "matmul",
                                 PA[:, half * 512:(half + 1) * 512],
                                 f2(gatb[:, hc:hc + 1, i128 * 128:(i128 + 1) * 128]),
                                 sl(sw2sb, hc, half * 512, (half + 1) * 512),
                                 start=(hc == 0), stop=(hc == 15),
                                 skip_group_check=True)
                    p.op("vector", "tensor_tensor", out=lscr[:, :],
                         in0=f2(xn1f[:, i128:i128 + 1, :]), in1=bb2rep[:, :], op=ALU.add)
                    p.op("vector", "tensor_tensor", out=lscr[:, :], in0=lscr[:, :],
                         in1=PA[:, :], op=ALU.add)
                    r0 = tco * 512 + i128 * 128
                    p.dma(xbo[r0:r0 + 128, :], lscr[:, :])
                    p.op("vector", "tensor_reduce", out=bcol[:, 0:1], in_=lscr[:, :],
                         axis=AX.X, op=ALU.max, apply_absolute_value=True)
                    p.op("vector", "tensor_tensor", out=gmcol[:, :], in0=gmcol[:, :],
                         in1=bcol[:, 0:1], op=ALU.max)
                p.flush()
            p.op("tensor", "transpose", PC[0:1, :], gmcol[:, :], identf[:, :])
            p.op("scalar", "copy", out=gmrow[0:1, :], in_=PC[0:1, :])
            p.op("vector", "tensor_reduce", out=gmrow[0:1, 0:1], in_=gmrow[0:1, :],
                 axis=AX.X, op=ALU.max)
            p.dma(gmax[:, :], gmrow[0:1, 0:1])
            p.flush(final=True)
    return nc



def build_B():
    nc = bass.Bass("TRN2", target_bir_lowering=False)
    xbh = nc.dram_tensor("xbh", [BW, D], F32, kind="ExternalInput").ap()
    gmax8 = nc.dram_tensor("gmax8", [8, 1], F32, kind="ExternalInput").ap()
    gsel = nc.dram_tensor("gsel", [8, 2], F32, kind="ExternalInput").ap()
    hmask = nc.dram_tensor("hmask", [128, 1], F32, kind="ExternalInput").ap()
    l2g = nc.dram_tensor("l2g", [1, D], F32, kind="ExternalInput").ap()
    l2b = nc.dram_tensor("l2b", [1, D], F32, kind="ExternalInput").ap()
    ipT = nc.dram_tensor("ipT", [D, 2 * DI], F32, kind="ExternalInput").ap()
    convw = nc.dram_tensor("convw", [128, 64], F32, kind="ExternalInput").ap()
    convb = nc.dram_tensor("convb", [128, 16], F32, kind="ExternalInput").ap()
    dparam = nc.dram_tensor("dparam", [128, 16], F32, kind="ExternalInput").ap()
    xpT = nc.dram_tensor("xpT", [DI, 96], F32, kind="ExternalInput").ap()
    dpT = nc.dram_tensor("dpT", [64, DI], F32, kind="ExternalInput").ap()
    dpb = nc.dram_tensor("dpb", [128, 16], F32, kind="ExternalInput").ap()

    qxo = nc.dram_tensor("qxo", [OWN, D], F32, kind="ExternalOutput").ap()
    dtT = nc.dram_tensor("dtT", [DI, OWN], F32, kind="ExternalOutput").ap()
    udT = nc.dram_tensor("udT", [DI, OWN], F32, kind="ExternalOutput").ap()
    dxT = nc.dram_tensor("dxT", [DI, OWN], F32, kind="ExternalOutput").ap()
    szT = nc.dram_tensor("szT", [DI, OWN], F32, kind="ExternalOutput").ap()
    BTo = nc.dram_tensor("BTo", [DS, OWN], F32, kind="ExternalOutput").ap()
    CTo = nc.dram_tensor("CTo", [DS, OWN], F32, kind="ExternalOutput").ap()

    xsf = nc.dram_tensor("xsf", [DI, OWN], F32, kind="Internal").ap()

    with ExitStack() as _st:
        S = _st.enter_context(nc.semaphore("S"))
        PA = _st.enter_context(nc.psum_tensor("PA", [128, 1024], F32))
        PB = _st.enter_context(nc.psum_tensor("PB", [128, 512], F32))
        PC = _st.enter_context(nc.psum_tensor("PC", [128, 128], F32))
        PE = _st.enter_context(nc.psum_tensor("PE", [128, 128], BF16))
        irow = _st.enter_context(nc.sbuf_tensor("irow", [128, 128], F32))
        icol = _st.enter_context(nc.sbuf_tensor("icol", [128, 1], F32))
        identf = _st.enter_context(nc.sbuf_tensor("identf", [128, 128], F32))
        identb = _st.enter_context(nc.sbuf_tensor("identb", [128, 128], BF16))
        ones1 = _st.enter_context(nc.sbuf_tensor("ones1", [1, 128], F32))
        grow = _st.enter_context(nc.sbuf_tensor("grow", [8, 2], F32))
        gm8 = _st.enter_context(nc.sbuf_tensor("gm8", [8, 1], F32))
        gsl = _st.enter_context(nc.sbuf_tensor("gsl", [8, 2], F32))
        rc2g = _st.enter_context(nc.sbuf_tensor("rc2g", [2, 8], F32))
        rc2 = _st.enter_context(nc.sbuf_tensor("rc2", [2, 2], F32))
        sel2 = _st.enter_context(nc.sbuf_tensor("sel2", [2, 2, 128], F32))
        ir2 = _st.enter_context(nc.sbuf_tensor("ir2", [2, 128], F32))
        rco = _st.enter_context(nc.sbuf_tensor("rco", [128, 1], F32))
        rch = _st.enter_context(nc.sbuf_tensor("rch", [128, 1], F32))
        hmsk = _st.enter_context(nc.sbuf_tensor("hmsk", [128, 1], F32))
        grep = _st.enter_context(nc.sbuf_tensor("grep", [128, 1024], F32))
        brep = _st.enter_context(nc.sbuf_tensor("brep", [128, 1024], F32))
        rowt = _st.enter_context(nc.sbuf_tensor("rowt", [1, 1024], F32))
        cwsb = _st.enter_context(nc.sbuf_tensor("cwsb", [128, 16, 4], F32))
        cbsb = _st.enter_context(nc.sbuf_tensor("cbsb", [128, 16, 1], F32))
        dsb = _st.enter_context(nc.sbuf_tensor("dsb", [128, 16, 1], F32))
        dpbsb = _st.enter_context(nc.sbuf_tensor("dpbsb", [128, 16, 1], F32))
        p = Prog(nc, S)
        emit_ident(p, nc, irow, icol, identf, identb, None)
        p.op("gpsimd", "memset", ones1[:, :], 1.0)
        p.dma(gm8[:, :], gmax8[:, :])
        p.dma(gsl[:, :], gsel[:, :])
        p.dma(hmsk[:, :], hmask[:, :])
        p.dma(cwsb[:, :, :].rearrange("p a b -> p (a b)"), convw[:, :])
        p.dma(cbsb[:, :, :].rearrange("p a b -> p (a b)"), convb[:, :])
        p.dma(dsb[:, :, :].rearrange("p a b -> p (a b)"), dparam[:, :])
        p.dma(dpbsb[:, :, :].rearrange("p a b -> p (a b)"), dpb[:, :])
        p.op("vector", "tensor_scalar", out=grow[0:8, :], in0=gsl[0:8, :],
             scalar1=gm8[0:8, 0:1], scalar2=None, op0=ALU.mult)
        p.op("tensor", "transpose", PC[0:2, 0:8], grow[0:8, :], identf[0:8, 0:8])
        p.op("scalar", "copy", out=rc2g[0:2, :], in_=PC[0:2, 0:8])
        p.op("vector", "tensor_reduce", out=rc2[0:2, 0:1], in_=rc2g[0:2, :],
             axis=AX.X, op=ALU.max)
        p.op("vector", "tensor_scalar", out=rc2[0:2, 1:2], in0=rc2[0:2, 0:1],
             scalar1=EPS, scalar2=None, op0=ALU.add)
        p.op("vector", "reciprocal", out=rc2[0:2, 0:1], in_=rc2[0:2, 1:2])
        p.op("vector", "tensor_scalar", out=rc2[0:2, 0:1], in0=rc2[0:2, 0:1],
             scalar1=128.0, scalar2=None, op0=ALU.mult)
        p.op("gpsimd", "iota", ir2[0:2, :], [[0, 128]], channel_multiplier=1,
             allow_small_or_imprecise_dtypes=True)
        for j in range(2):
            p.op("vector", "tensor_scalar", out=f2(sel2[0:2, j:j + 1, :]),
                 in0=ir2[0:2, :], scalar1=float(j), scalar2=None, op0=ALU.is_equal)
        for j, dst in ((0, rco), (1, rch)):
            p.op("tensor", "matmul", PC[:, 0:1], f2(sel2[0:2, j:j + 1, :]),
                 rc2[0:2, 0:1], start=True, stop=True)
            p.op("scalar", "copy", out=dst[:, :], in_=PC[:, 0:1])
        for dram_row, rep in ((l2g, grep), (l2b, brep)):
            p.dma(rowt[:, :], dram_row[:, :])
            for half in range(2):
                p.op("tensor", "matmul", PB[:, :], ones1[0:1, :],
                     rowt[0:1, half * 512:(half + 1) * 512], start=True, stop=True)
                p.op("scalar", "copy", out=rep[:, half * 512:(half + 1) * 512],
                     in_=PB[:, :])
        p.flush()

        UB = 128.0 - EPS
        LB = -128.0 + EPS
        with ExitStack() as _st:
            uT = _st.enter_context(nc.sbuf_tensor("uT", [128, 8, BW], BF16))
            xscb = _st.enter_context(nc.sbuf_tensor("xscb", [128, 16, 2048], BF16))
            xbt = _st.enter_context(nc.sbuf_tensor("xbt", [128, 1024], F32))
            q1 = _st.enter_context(nc.sbuf_tensor("q1", [128, 1024], F32))
            uf = _st.enter_context(nc.sbuf_tensor("uf", [128, 1024], F32))
            lscr = _st.enter_context(nc.sbuf_tensor("lscr", [128, 1024], F32))
            ubb = _st.enter_context(nc.sbuf_tensor("ubb", [128, 1024], BF16))
            bcol = _st.enter_context(nc.sbuf_tensor("bcol", [128, 4], F32))
            for i in range(17):
                p.dma(xbt[:, :], xbh[i * 128:(i + 1) * 128, :])
                rc = rch if i == 0 else rco
                p.op("vector", "tensor_scalar", out=q1[:, :], in0=xbt[:, :],
                     scalar1=rc[:, 0:1], scalar2=UB, op0=ALU.mult, op1=ALU.min)
                p.op("vector", "tensor_scalar", out=q1[:, :], in0=q1[:, :],
                     scalar1=LB, scalar2=None, op0=ALU.max)
                if i > 0:
                    p.dma(qxo[(i - 1) * 128:i * 128, :], q1[:, :])
                emit_ln(p, q1[:, :], uf[:, :], lscr[:, :], bcol)
                p.op("vector", "tensor_tensor", out=uf[:, :], in0=uf[:, :],
                     in1=grep[:, :], op=ALU.mult)
                p.op("vector", "tensor_tensor", out=uf[:, :], in0=uf[:, :],
                     in1=brep[:, :], op=ALU.add)
                if i == 0:
                    p.op("vector", "tensor_scalar", out=uf[:, :], in0=uf[:, :],
                         scalar1=hmsk[:, 0:1], scalar2=None, op0=ALU.mult)
                p.op("vector", "tensor_copy", out=ubb[:, :], in_=uf[:, :])
                for g in range(8):
                    p.op("tensor", "transpose", PE[:, :],
                         ubb[:, g * 128:(g + 1) * 128], identb[:, :])
                    p.op("scalar", "copy", out=sl(uT, g, i * 128, (i + 1) * 128),
                         in_=PE[:, :])
                p.flush()

            ip3 = ipT.rearrange("(ci p) oc -> p ci oc", p=128)
            tchunks = [(0, 512), (512, 1024), (1024, 1536), (1536, 2048), (2048, BW)]
            with ExitStack() as _st:
                ipc = _st.enter_context(nc.sbuf_tensor("ipc", [128, 8, 128], BF16))
                ipstg = _st.enter_context(nc.sbuf_tensor("ipstg", [128, 8, 128], F32))
                xrowf = _st.enter_context(nc.sbuf_tensor("xrowf", [128, BW], F32))
                cv = _st.enter_context(nc.sbuf_tensor("cv", [128, 2048], F32))
                xscf = _st.enter_context(nc.sbuf_tensor("xscf", [128, 2048], F32))
                for oc in range(32):
                    p.dma(ipstg[:, :, :], ip3[:, :, oc * 128:(oc + 1) * 128])
                    p.op("vector", "tensor_copy",
                         out=ipc[:, :, :].rearrange("p a b -> p (a b)"),
                         in_=ipstg[:, :, :].rearrange("p a b -> p (a b)"))
                    for (a, b) in tchunks:
                        for ci in range(8):
                            p.op("tensor", "matmul", PB[:, 0:b - a],
                                 sl(ipc, ci, 0, 128), sl(uT, ci, a, b),
                                 start=(ci == 0), stop=(ci == 7))
                        p.op("scalar", "copy", out=xrowf[:, a:b], in_=PB[:, 0:b - a])
                    if oc < 16:
                        p.op("vector", "tensor_scalar", out=cv[:, :],
                             in0=xrowf[:, BHALO - 3:BHALO - 3 + 2048],
                             scalar1=f2(cwsb[:, oc:oc + 1, 0:1]), scalar2=None,
                             op0=ALU.mult)
                        for tau in range(1, 4):
                            p.op("vector", "scalar_tensor_tensor", out=cv[:, :],
                                 in0=xrowf[:, BHALO - 3 + tau:BHALO - 3 + tau + 2048],
                                 scalar=f2(cwsb[:, oc:oc + 1, tau:tau + 1]),
                                 in1=cv[:, :], op0=ALU.mult, op1=ALU.add)
                        p.op("scalar", "activation", out=xscf[:, :], in_=cv[:, :],
                             func=AF.Silu, bias=f2(cbsb[:, oc:oc + 1, 0:1]))
                        p.dma(xsf[oc * 128:(oc + 1) * 128, :], xscf[:, :])
                        p.op("vector", "tensor_scalar", out=cv[:, :], in0=xscf[:, :],
                             scalar1=f2(dsb[:, oc:oc + 1, 0:1]), scalar2=None,
                             op0=ALU.mult)
                        p.dma(dxT[oc * 128:(oc + 1) * 128, :], cv[:, :])
                        p.op("vector", "tensor_copy", out=sl(xscb, oc, 0, 2048),
                             in_=xscf[:, :])
                    else:
                        zc = oc - 16
                        p.op("scalar", "activation", out=cv[:, :],
                             in_=xrowf[:, BHALO:BW], func=AF.Silu)
                        p.dma(szT[zc * 128:(zc + 1) * 128, :], cv[:, :])
                    p.flush()

            with ExitStack() as _st:
                xpsb = _st.enter_context(nc.sbuf_tensor("xpsb", [128, 16, 96], BF16))
                xpstg = _st.enter_context(nc.sbuf_tensor("xpstg", [128, 16, 96], F32))
                xdb = _st.enter_context(nc.sbuf_tensor("xdb", [128, 96], F32))
                BTsb = _st.enter_context(nc.sbuf_tensor("BTsb", [16, 2048], F32))
                CTsb = _st.enter_context(nc.sbuf_tensor("CTsb", [16, 2048], F32))
                dtl = _st.enter_context(nc.sbuf_tensor("dtl", [64, 2048], F32))
                dtlb = _st.enter_context(nc.sbuf_tensor("dtlb", [64, 2048], BF16))
                dpsb = _st.enter_context(nc.sbuf_tensor("dpsb", [64, 16, 128], BF16))
                dpstg = _st.enter_context(nc.sbuf_tensor("dpstg", [64, 16, 128], F32))
                dtf = _st.enter_context(nc.sbuf_tensor("dtf", [128, 512], F32))
                sf1 = _st.enter_context(nc.sbuf_tensor("sf1", [128, 512], F32))
                sf2 = _st.enter_context(nc.sbuf_tensor("sf2", [128, 512], F32))
                xsrow = _st.enter_context(nc.sbuf_tensor("xsrow", [128, 512], F32))
                p.dma(xpstg[:, :, :], xpT.rearrange("(g p) c -> p g c", p=128))
                p.op("vector", "tensor_copy",
                     out=xpsb[:, :, :].rearrange("p a b -> p (a b)"),
                     in_=xpstg[:, :, :].rearrange("p a b -> p (a b)"))
                for t128 in range(16):
                    for dc in range(16):
                        p.op("tensor", "matmul", PC[:, 0:96],
                             f2(xscb[:, dc:dc + 1, t128 * 128:(t128 + 1) * 128]),
                             sl(xpsb, dc, 0, 96), start=(dc == 0), stop=(dc == 15))
                    p.op("scalar", "copy", out=xdb[:, :], in_=PC[:, 0:96])
                    p.op("tensor", "transpose", PC[0:64, :], xdb[:, 0:64],
                         identf[:, :])
                    p.op("scalar", "copy", out=dtl[:, t128 * 128:(t128 + 1) * 128],
                         in_=PC[0:64, 0:128])
                    p.op("tensor", "transpose", PC[0:16, :], xdb[:, 64:80],
                         identf[:, :])
                    p.op("scalar", "copy", out=BTsb[:, t128 * 128:(t128 + 1) * 128],
                         in_=PC[0:16, 0:128])
                    p.op("tensor", "transpose", PC[0:16, :], xdb[:, 80:96],
                         identf[:, :])
                    p.op("scalar", "copy", out=CTsb[:, t128 * 128:(t128 + 1) * 128],
                         in_=PC[0:16, 0:128])
                p.dma(BTo[:, :], BTsb[:, :])
                p.dma(CTo[:, :], CTsb[:, :])
                p.op("vector", "tensor_copy", out=dtlb[:, :], in_=dtl[:, :])
                p.dma(dpstg[:, :, :], dpT.rearrange("p (g c) -> p g c", g=16))
                p.op("vector", "tensor_copy",
                     out=dpsb[:, :, :].rearrange("p a b -> p (a b)"),
                     in_=dpstg[:, :, :].rearrange("p a b -> p (a b)"))
                p.flush()
                for dc in range(16):
                    for tch in range(4):
                        a = tch * 512
                        p.op("tensor", "matmul", PB[:, :],
                             sl(dpsb, dc, 0, 128), dtlb[:, a:a + 512],
                             start=True, stop=True)
                        emit_softplus(p, PB[:, :], dtf[:, :], sf1[:, :], sf2[:, :],
                                      bias_col=f2(dpbsb[:, dc:dc + 1, 0:1]))
                        p.dma(dtT[dc * 128:(dc + 1) * 128, a:a + 512], dtf[:, :])
                        p.dma(xsrow[:, :], xsf[dc * 128:(dc + 1) * 128, a:a + 512])
                        p.op("vector", "tensor_tensor", out=sf1[:, :],
                             in0=dtf[:, :], in1=xsrow[:, :], op=ALU.mult)
                        p.dma(udT[dc * 128:(dc + 1) * 128, a:a + 512], sf1[:, :])
                    p.flush()
        p.flush(final=True)
    return nc



def build_C():
    nc = bass.Bass("TRN2", target_bir_lowering=False)
    dts = nc.dram_tensor("dts", [256, T], F32, kind="ExternalInput").ap()
    uds = nc.dram_tensor("uds", [256, T], F32, kind="ExternalInput").ap()
    dxs = nc.dram_tensor("dxs", [256, T], F32, kind="ExternalInput").ap()
    szs = nc.dram_tensor("szs", [256, T], F32, kind="ExternalInput").ap()
    alog = nc.dram_tensor("alog", [256, DS], F32, kind="ExternalInput").ap()
    BTg = nc.dram_tensor("BTg", [DS, T], F32, kind="ExternalInput").ap()
    CTg = nc.dram_tensor("CTg", [DS, T], F32, kind="ExternalInput").ap()
    ymo = nc.dram_tensor("ymo", [256, T], F32, kind="ExternalOutput").ap()

    NCK = 32
    CK = T // NCK
    with ExitStack() as _st:
        S = _st.enter_context(nc.semaphore("S"))
        PA = _st.enter_context(nc.psum_tensor("PA", [128, 1024], F32))
        ones1 = _st.enter_context(nc.sbuf_tensor("ones1", [1, 128], F32))
        asb = _st.enter_context(nc.sbuf_tensor("asb", [128, 2, DS], F32))
        hst = _st.enter_context(nc.sbuf_tensor("hst", [128, 2, DS], F32))
        dtc = _st.enter_context(nc.sbuf_tensor("dtc", [128, 2, CK], F32))
        udc = _st.enter_context(nc.sbuf_tensor("udc", [128, 2, CK], F32))
        dxc = _st.enter_context(nc.sbuf_tensor("dxc", [128, 2, CK], F32))
        szc = _st.enter_context(nc.sbuf_tensor("szc", [128, 2, CK], F32))
        BTc = _st.enter_context(nc.sbuf_tensor("BTc", [16, CK], F32))
        CTc = _st.enter_context(nc.sbuf_tensor("CTc", [16, CK], F32))
        da = _st.enter_context(nc.sbuf_tensor("da", [128, CK], F32))
        bb = _st.enter_context(nc.sbuf_tensor("bb", [128, CK], F32))
        hh = _st.enter_context(nc.sbuf_tensor("hh", [128, CK], F32))
        cm = _st.enter_context(nc.sbuf_tensor("cm", [128, CK], F32))
        yac = _st.enter_context(nc.sbuf_tensor("yac", [128, 2, CK], F32))
        sel16 = _st.enter_context(nc.sbuf_tensor("sel16", [16, 16, 128], F32))
        ir16 = _st.enter_context(nc.sbuf_tensor("ir16", [16, 128], F32))
        p = Prog(nc, S)
        p.op("gpsimd", "memset", ones1[:, :], 1.0)
        p.op("gpsimd", "memset", hst[:, :, :], 0.0)
        p.op("gpsimd", "iota", ir16[0:16, :], [[0, 128]], channel_multiplier=1,
             allow_small_or_imprecise_dtypes=True)
        for s in range(DS):
            p.op("vector", "tensor_scalar", out=f2(sel16[0:16, s:s + 1, :]),
                 in0=ir16[0:16, :], scalar1=float(s), scalar2=None, op0=ALU.is_equal)
        p.dma(asb[:, :, :], alog.rearrange("(g p) s -> p g s", p=128))
        p.op("scalar", "activation", out=asb[:, :, :].rearrange("p a b -> p (a b)"),
             in_=asb[:, :, :].rearrange("p a b -> p (a b)"), func=AF.Exp)
        p.op("vector", "tensor_scalar",
             out=asb[:, :, :].rearrange("p a b -> p (a b)"),
             in0=asb[:, :, :].rearrange("p a b -> p (a b)"),
             scalar1=-1.0, scalar2=None, op0=ALU.mult)
        p.flush()
        for ck in range(NCK):
            a = ck * CK
            for dram, sb in ((dts, dtc), (uds, udc), (dxs, dxc), (szs, szc)):
                p.dma(sb[:, :, :],
                      dram.rearrange("(g p) t -> p g t", p=128)[:, :, a:a + CK])
            p.dma(BTc[:, :], BTg[:, a:a + CK])
            p.dma(CTc[:, :], CTg[:, a:a + CK])
            for s in range(DS):
                p.op("tensor", "matmul", PA[:, 0:CK], f2(sel16[:, s:s + 1, :]),
                     BTc[0:16, :], start=True, stop=True)
                p.op("tensor", "matmul", PA[:, CK:2 * CK], f2(sel16[:, s:s + 1, :]),
                     CTc[0:16, :], start=True, stop=True)
                for g in range(2):
                    p.op("scalar", "activation", out=da[:, :],
                         in_=f2(dtc[:, g:g + 1, :]), func=AF.Exp,
                         scale=f2(asb[:, g:g + 1, s:s + 1]))
                    p.op("vector", "tensor_tensor", out=bb[:, :],
                         in0=f2(udc[:, g:g + 1, :]), in1=PA[:, 0:CK], op=ALU.mult)
                    p.op("vector", "tensor_tensor_scan", out=hh[:, :], data0=da[:, :],
                         data1=bb[:, :], initial=f2(hst[:, g:g + 1, s:s + 1]),
                         op0=ALU.mult, op1=ALU.add)
                    p.op("vector", "tensor_copy", out=f2(hst[:, g:g + 1, s:s + 1]),
                         in_=hh[:, CK - 1:CK])
                    p.op("vector", "tensor_tensor", out=cm[:, :], in0=hh[:, :],
                         in1=PA[:, CK:2 * CK], op=ALU.mult)
                    if s == 0:
                        p.op("gpsimd", "tensor_copy", out=f2(yac[:, g:g + 1, :]),
                             in_=cm[:, :])
                    else:
                        p.op("gpsimd", "tensor_add", out=f2(yac[:, g:g + 1, :]),
                             in0=f2(yac[:, g:g + 1, :]), in1=cm[:, :])
            for g in range(2):
                p.op("vector", "tensor_tensor", out=f2(yac[:, g:g + 1, :]),
                     in0=f2(yac[:, g:g + 1, :]), in1=f2(dxc[:, g:g + 1, :]), op=ALU.add)
                p.op("vector", "tensor_tensor", out=f2(yac[:, g:g + 1, :]),
                     in0=f2(yac[:, g:g + 1, :]), in1=f2(szc[:, g:g + 1, :]), op=ALU.mult)
                p.dma(ymo[g * 128:(g + 1) * 128, a:a + CK], f2(yac[:, g:g + 1, :]))
            p.flush()
        p.flush(final=True)
    return nc



def build_D():
    nc = bass.Bass("TRN2", target_bir_lowering=False)
    ymc = nc.dram_tensor("ymc", [DI, OWN], F32, kind="ExternalInput").ap()
    opT = nc.dram_tensor("opT", [DI, D], F32, kind="ExternalInput").ap()
    xn1Ti = nc.dram_tensor("xn1Ti", [D, OWN], BF16, kind="ExternalInput").ap()
    qxi = nc.dram_tensor("qxi", [OWN, D], F32, kind="ExternalInput").ap()
    x1i = nc.dram_tensor("x1i", [OWN, D], F32, kind="ExternalInput").ap()
    gw = nc.dram_tensor("gw", [128, 64], F32, kind="ExternalInput").ap()
    gb = nc.dram_tensor("gb", [1, 8], F32, kind="ExternalInput").ap()
    w1 = nc.dram_tensor("w1", [8, D, 2048], F32, kind="ExternalInput").ap()
    b1c = nc.dram_tensor("b1c", [128, 128], F32, kind="ExternalInput").ap()
    w2 = nc.dram_tensor("w2", [8, 2048, D], F32, kind="ExternalInput").ap()
    b2 = nc.dram_tensor("b2", [8, D], F32, kind="ExternalInput").ap()
    fin = nc.dram_tensor("fin", [OWN, D], F32, kind="ExternalOutput").ap()

    with ExitStack() as _st:
        S = _st.enter_context(nc.semaphore("S"))
        PA = _st.enter_context(nc.psum_tensor("PA", [128, 1024], F32))
        PB = _st.enter_context(nc.psum_tensor("PB", [128, 512], F32))
        PC = _st.enter_context(nc.psum_tensor("PC", [128, 128], F32))
        PF = _st.enter_context(nc.psum_tensor("PF", [128, 128], F32))
        irow = _st.enter_context(nc.sbuf_tensor("irow", [128, 128], F32))
        icol = _st.enter_context(nc.sbuf_tensor("icol", [128, 1], F32))
        identf = _st.enter_context(nc.sbuf_tensor("identf", [128, 128], F32))
        identb = _st.enter_context(nc.sbuf_tensor("identb", [128, 128], BF16))
        ones1 = _st.enter_context(nc.sbuf_tensor("ones1", [1, 128], F32))
        oacc = _st.enter_context(nc.sbuf_tensor("oacc", [128, 16, 1024], F32))
        xnsb = _st.enter_context(nc.sbuf_tensor("xnsb", [128, 8, OWN], BF16))
        wmT = _st.enter_context(nc.sbuf_tensor("wmT", [8, OWN], F32))
        b1sb = _st.enter_context(nc.sbuf_tensor("b1sb", [128, 16, 8], F32))
        b2sb = _st.enter_context(nc.sbuf_tensor("b2sb", [8, 1024], F32))
        gbrep = _st.enter_context(nc.sbuf_tensor("gbrep", [128, 8], F32))
        gwsb = _st.enter_context(nc.sbuf_tensor("gwsb", [128, 8, 8], BF16))
        gwstg = _st.enter_context(nc.sbuf_tensor("gwstg", [128, 8, 8], F32))
        sel8 = _st.enter_context(nc.sbuf_tensor("sel8", [8, 8, 128], F32))
        ir8 = _st.enter_context(nc.sbuf_tensor("ir8", [8, 128], F32))
        p = Prog(nc, S)
        emit_ident(p, nc, irow, icol, identf, identb, None)
        p.op("gpsimd", "memset", ones1[:, :], 1.0)
        p.op("gpsimd", "memset", oacc[:, :, :], 0.0)
        p.op("gpsimd", "iota", ir8[0:8, :], [[0, 128]], channel_multiplier=1,
             allow_small_or_imprecise_dtypes=True)
        for e in range(8):
            p.op("vector", "tensor_scalar", out=f2(sel8[0:8, e:e + 1, :]),
                 in0=ir8[0:8, :], scalar1=float(e), scalar2=None, op0=ALU.is_equal)
        p.dma(xnsb[:, :, :], xn1Ti.rearrange("(g p) t -> p g t", p=128))
        p.dma(b1sb[:, :, :].rearrange("p a b -> p (a b)"), b1c[:, :])
        p.dma(b2sb[:, :], b2[:, :])
        p.dma(gwstg[:, :, :].rearrange("p a b -> p (a b)"), gw[:, :])
        p.op("vector", "tensor_copy",
             out=gwsb[:, :, :].rearrange("p a b -> p (a b)"),
             in_=gwstg[:, :, :].rearrange("p a b -> p (a b)"))
        with ExitStack() as _st:
            gbr = _st.enter_context(nc.sbuf_tensor("gbr", [1, 8], F32))
            p.dma(gbr[:, :], gb[:, :])
            p.op("tensor", "matmul", PC[:, 0:8], ones1[0:1, :], gbr[0:1, :],
                 start=True, stop=True)
            p.op("scalar", "copy", out=gbrep[:, :], in_=PC[:, 0:8])
            p.flush()

            with ExitStack() as _st:
                pf_ = _st.enter_context(nc.sbuf_tensor("pf_", [128, 8], F32))
                prob = _st.enter_context(nc.sbuf_tensor("prob", [128, 8], F32))
                mcol = _st.enter_context(nc.sbuf_tensor("mcol", [128, 4], F32))
                wm = _st.enter_context(nc.sbuf_tensor("wm", [128, 8], F32))
                for t128 in range(16):
                    for ci in range(8):
                        p.op("tensor", "matmul", PC[:, 0:8],
                             f2(xnsb[:, ci:ci + 1, t128 * 128:(t128 + 1) * 128]),
                             sl(gwsb, ci, 0, 8), start=(ci == 0), stop=(ci == 7))
                    p.op("vector", "tensor_tensor", out=pf_[:, :], in0=PC[:, 0:8],
                         in1=gbrep[:, :], op=ALU.add)
                    p.op("vector", "tensor_reduce", out=mcol[:, 0:1], in_=pf_[:, :],
                         axis=AX.X, op=ALU.max)
                    p.op("vector", "tensor_scalar", out=mcol[:, 1:2], in0=mcol[:, 0:1],
                         scalar1=-1.0, scalar2=None, op0=ALU.mult)
                    p.op("scalar", "activation", out=prob[:, :], in_=pf_[:, :],
                         func=AF.Exp, bias=mcol[:, 1:2])
                    p.op("vector", "tensor_reduce", out=mcol[:, 2:3], in_=prob[:, :],
                         axis=AX.X, op=ALU.add)
                    p.op("vector", "reciprocal", out=mcol[:, 3:4], in_=mcol[:, 2:3])
                    p.op("vector", "tensor_scalar", out=prob[:, :], in0=prob[:, :],
                         scalar1=mcol[:, 3:4], scalar2=None, op0=ALU.mult)
                    p.op("vector", "tensor_reduce", out=mcol[:, 0:1], in_=prob[:, :],
                         axis=AX.X, op=ALU.max)
                    p.op("vector", "tensor_scalar", out=wm[:, :], in0=prob[:, :],
                         scalar1=mcol[:, 0:1], scalar2=None, op0=ALU.is_ge)
                    p.op("vector", "scalar_tensor_tensor", out=wm[:, :], in0=wm[:, :],
                         scalar=-2.0, in1=prob[:, :], op0=ALU.mult, op1=ALU.add)
                    p.op("vector", "tensor_reduce", out=mcol[:, 1:2], in_=wm[:, :],
                         axis=AX.X, op=ALU.max)
                    p.op("vector", "tensor_scalar", out=wm[:, :], in0=prob[:, :],
                         scalar1=mcol[:, 1:2], scalar2=None, op0=ALU.is_ge)
                    p.op("vector", "tensor_tensor", out=wm[:, :], in0=wm[:, :],
                         in1=prob[:, :], op=ALU.mult)
                    p.op("tensor", "transpose", PF[0:8, :], wm[:, :], identf[:, :])
                    p.op("scalar", "copy", out=wmT[:, t128 * 128:(t128 + 1) * 128],
                         in_=PF[0:8, 0:128])
                p.flush()

            with ExitStack() as _st:
                w1b = _st.enter_context(nc.sbuf_tensor("w1b", [128, 8, 2048], BF16))
                w2b = _st.enter_context(nc.sbuf_tensor("w2b", [128, 16, 1024], BF16))
                stg = _st.enter_context(nc.sbuf_tensor("stg", [128, 8, 256], F32))
                hTb = _st.enter_context(nc.sbuf_tensor("hTb", [128, 16, 512], BF16))
                hscr = _st.enter_context(nc.sbuf_tensor("hscr", [128, 512], F32))
                wmf = _st.enter_context(nc.sbuf_tensor("wmf", [128, 512], F32))
                for e in range(8):
                    w1e = w1[e:e + 1, :, :].rearrange("e (ci p) h -> p (e ci) h", p=128)
                    for piece in range(8):
                        p.dma(stg[:, :, :], w1e[:, :, piece * 256:(piece + 1) * 256])
                        for ci in range(8):
                            p.op("vector", "tensor_copy",
                                 out=sl(w1b, ci, piece * 256, (piece + 1) * 256),
                                 in_=sl(stg, ci, 0, 256))
                    w2e = w2[e:e + 1, :, :].rearrange("e (hc p) c -> p (e hc) c", p=128)
                    for piece in range(8):
                        p.dma(stg[:, :, :].rearrange("p (a u) c -> p a u c", a=2),
                              w2e[:, piece * 2:(piece + 1) * 2, :].rearrange(
                                  "p a (u c) -> p a u c", u=4))
                        for j in range(8):
                            hc = piece * 2 + j // 4
                            q = j % 4
                            p.op("vector", "tensor_copy",
                                 out=f2(w2b[:, hc:hc + 1, q * 256:(q + 1) * 256]),
                                 in_=sl(stg, j, 0, 256))
                    for tco in range(4):
                        a = tco * 512
                        p.op("tensor", "matmul", PB[:, :], f2(sel8[:, e:e + 1, :]),
                             wmT[0:8, a:a + 512], start=True, stop=True)
                        p.op("scalar", "copy", out=wmf[:, :], in_=PB[:, :])
                        for hc in range(16):
                            for ci in range(8):
                                p.op("tensor", "matmul", PB[:, :],
                                     sl(w1b, ci, hc * 128, (hc + 1) * 128),
                                     sl(xnsb, ci, a, a + 512),
                                     start=(ci == 0), stop=(ci == 7))
                            p.op("scalar", "activation", out=hscr[:, :], in_=PB[:, :],
                                 func=AF.Relu, bias=f2(b1sb[:, hc:hc + 1, e:e + 1]))
                            p.op("vector", "tensor_tensor", out=sl(hTb, hc, 0, 512),
                                 in0=hscr[:, :], in1=wmf[:, :], op=ALU.mult)
                        for i128 in range(4):
                            ti = tco * 4 + i128
                            for hc in range(16):
                                for half in range(2):
                                    p.op("tensor", "matmul",
                                         PA[:, half * 512:(half + 1) * 512],
                                         f2(hTb[:, hc:hc + 1, i128 * 128:(i128 + 1) * 128]),
                                         f2(w2b[:, hc:hc + 1, half * 512:(half + 1) * 512]),
                                         start=(hc == 0), stop=(hc == 15),
                                         skip_group_check=True)
                            p.op("vector", "tensor_tensor",
                                 out=f2(oacc[:, ti:ti + 1, :]),
                                 in0=f2(oacc[:, ti:ti + 1, :]), in1=PA[:, :], op=ALU.add)
                        p.flush()

            with ExitStack() as _st:
                ymstg = _st.enter_context(nc.sbuf_tensor("ymstg", [128, 2048], F32))
                ymbb = _st.enter_context(nc.sbuf_tensor("ymbb", [128, 2048], BF16))
                opstg = _st.enter_context(nc.sbuf_tensor("opstg", [128, 1024], F32))
                opb = _st.enter_context(nc.sbuf_tensor("opb", [128, 1024], BF16))
                b2bb = _st.enter_context(nc.sbuf_tensor("b2bb", [8, 1024], BF16))
                wmbb = _st.enter_context(nc.sbuf_tensor("wmbb", [8, OWN], BF16))
                resrow = _st.enter_context(nc.sbuf_tensor("resrow", [128, 1024], F32))
                p.op("vector", "tensor_copy", out=b2bb[:, :], in_=b2sb[:, :])
                p.op("vector", "tensor_copy", out=wmbb[:, :], in_=wmT[:, :])
                for dc in range(16):
                    p.dma(ymstg[:, :], ymc[dc * 128:(dc + 1) * 128, :])
                    p.op("vector", "tensor_copy", out=ymbb[:, :], in_=ymstg[:, :])
                    p.dma(opstg[:, :], opT[dc * 128:(dc + 1) * 128, :])
                    p.op("vector", "tensor_copy", out=opb[:, :], in_=opstg[:, :])
                    for ti in range(16):
                        for half in range(2):
                            p.op("tensor", "matmul", PA[:, half * 512:(half + 1) * 512],
                                 ymbb[:, ti * 128:(ti + 1) * 128],
                                 opb[:, half * 512:(half + 1) * 512],
                                 start=True, stop=True)
                        p.op("vector", "tensor_tensor", out=f2(oacc[:, ti:ti + 1, :]),
                             in0=f2(oacc[:, ti:ti + 1, :]), in1=PA[:, :], op=ALU.add)
                    p.flush()
                for ti in range(16):
                    for half in range(2):
                        p.op("tensor", "matmul", PA[:, half * 512:(half + 1) * 512],
                             wmbb[:, ti * 128:(ti + 1) * 128],
                             b2bb[:, half * 512:(half + 1) * 512],
                             start=True, stop=True)
                    p.op("vector", "tensor_tensor", out=f2(oacc[:, ti:ti + 1, :]),
                         in0=f2(oacc[:, ti:ti + 1, :]), in1=PA[:, :], op=ALU.add)
                    p.dma(resrow[:, :], x1i[ti * 128:(ti + 1) * 128, :])
                    p.op("vector", "tensor_tensor", out=f2(oacc[:, ti:ti + 1, :]),
                         in0=f2(oacc[:, ti:ti + 1, :]), in1=resrow[:, :], op=ALU.add)
                    p.dma(resrow[:, :], qxi[ti * 128:(ti + 1) * 128, :])
                    p.op("vector", "tensor_tensor", out=f2(oacc[:, ti:ti + 1, :]),
                         in0=f2(oacc[:, ti:ti + 1, :]), in1=resrow[:, :], op=ALU.add)
                    p.dma(fin[ti * 128:(ti + 1) * 128, :], f2(oacc[:, ti:ti + 1, :]))
                p.flush(final=True)
    return nc



def _f(a):
    return np.ascontiguousarray(a, dtype=np.float32)


def swz(a):
    a = np.asarray(a, dtype=np.float32)
    if a.ndim == 1:
        a = a.reshape(-1, 1)
    G = a.shape[0] // 128
    return np.ascontiguousarray(a.reshape(G, 128, a.shape[1]).transpose(1, 0, 2)
                                .reshape(128, G * a.shape[1]))


DBG = {}


def kernel(**inp):
    x = _f(inp["L0_attn_x"]).reshape(T, D)
    xpad = np.concatenate([np.zeros((HALO, D), np.float32), x], axis=0)

    ncA = build_A()
    common_A = dict(
        wrT=_f(inp["L0_attn_W_r"].T), wkT=_f(inp["L0_attn_W_k"].T),
        wvT=_f(inp["L0_attn_W_v"].T), wwT=_f(inp["L0_attn_W_w"].T),
        woT=_f(inp["L0_attn_W_o"].T),
        tmc=swz(np.stack([_f(inp["L0_attn_tm_r"]), _f(inp["L0_attn_tm_k"]),
                          _f(inp["L0_attn_tm_v"]), _f(inp["L0_attn_tm_w"])], axis=1)),
        ucol=swz(np.asarray(inp["L0_attn_u"]).reshape(D, 1)),
        lncol=swz(np.stack([np.tile(_f(inp["L0_attn_ln_w"]), H),
                            np.tile(_f(inp["L0_attn_ln_b"]), H)], axis=1)),
        fw1=_f(inp["L0_ffn_w1"]), fb1=swz(inp["L0_ffn_b1"]),
        fw2=_f(inp["L0_ffn_w2"]), fb2=_f(inp["L0_ffn_b2"]).reshape(1, D),
        bW1T=_f(inp["L1_attn_W1"].T), bb1=swz(inp["L1_attn_b1"]),
        bW2T=_f(inp["L1_attn_W2"].T), bb2=_f(inp["L1_attn_b2"]).reshape(1, D),
        l1g=_f(inp["L1_attn_ln1_g"]).reshape(1, D),
        l1b=_f(inp["L1_attn_ln1_b"]).reshape(1, D),
    )
    in_maps = []
    for c in range(NCORE):
        m = dict(common_A)
        m["xs"] = _f(xpad[c * OWN:c * OWN + WTOT])
        in_maps.append(m)
    resA = run_bass_kernel_spmd(ncA, in_maps, core_ids=list(range(NCORE)))
    x1_g = np.concatenate([np.asarray(r["x1"]) for r in resA.results], axis=0)
    xb_g = np.concatenate([np.asarray(r["xbo"]) for r in resA.results], axis=0)
    gmax8 = np.concatenate([np.asarray(r["gmax"]) for r in resA.results],
                           axis=0).reshape(8, 1).astype(np.float32)

    ncB = build_B()
    xbpad = np.concatenate([np.zeros((BHALO, D), np.float32), xb_g], axis=0)
    common_B = dict(
        gmax8=gmax8,
        l2g=_f(inp["L1_attn_ln2_g"]).reshape(1, D),
        l2b=_f(inp["L1_attn_ln2_b"]).reshape(1, D),
        ipT=_f(inp["L1_attn_in_proj_w"].T),
        convw=swz(np.asarray(inp["L1_attn_conv_w"]).reshape(DI, 4)),
        convb=swz(inp["L1_attn_conv_b"]),
        dparam=swz(inp["L1_attn_D_param"]),
        xpT=_f(inp["L1_attn_x_proj_w"].T),
        dpT=_f(inp["L1_attn_dt_proj_w"].T),
        dpb=swz(inp["L1_attn_dt_proj_b"]),
    )
    in_maps = []
    for c in range(NCORE):
        m = dict(common_B)
        m["xbh"] = _f(xbpad[c * OWN:c * OWN + BW])
        own_grp = (c * OWN) // 4096
        halo_grp = max(c * OWN - 1, 0) // 4096
        gs = np.zeros((8, 2), np.float32)
        gs[2 * own_grp:2 * own_grp + 2, 0] = 1.0
        gs[2 * halo_grp:2 * halo_grp + 2, 1] = 1.0
        m["gsel"] = gs
        m["hmask"] = np.ones((128, 1), np.float32) if c > 0 else np.zeros(
            (128, 1), np.float32)
        in_maps.append(m)
    resB = run_bass_kernel_spmd(ncB, in_maps, core_ids=list(range(NCORE)))
    qx_g = np.concatenate([np.asarray(r["qxo"]) for r in resB.results], axis=0)
    dtT_g = np.concatenate([np.asarray(r["dtT"]) for r in resB.results], axis=1)
    udT_g = np.concatenate([np.asarray(r["udT"]) for r in resB.results], axis=1)
    dxT_g = np.concatenate([np.asarray(r["dxT"]) for r in resB.results], axis=1)
    szT_g = np.concatenate([np.asarray(r["szT"]) for r in resB.results], axis=1)
    BT_g = np.concatenate([np.asarray(r["BTo"]) for r in resB.results], axis=1)
    CT_g = np.concatenate([np.asarray(r["CTo"]) for r in resB.results], axis=1)

    ncC = build_C()
    alog_full = _f(inp["L1_attn_A_log"])
    in_maps = []
    for c in range(NCORE):
        sl_ = slice(c * 256, (c + 1) * 256)
        in_maps.append(dict(
            dts=_f(dtT_g[sl_]), uds=_f(udT_g[sl_]), dxs=_f(dxT_g[sl_]),
            szs=_f(szT_g[sl_]), alog=_f(alog_full[sl_]),
            BTg=_f(BT_g), CTg=_f(CT_g)))
    resC = run_bass_kernel_spmd(ncC, in_maps, core_ids=list(range(NCORE)))
    ym_g = np.concatenate([np.asarray(r["ymo"]) for r in resC.results], axis=0)

    ncD = build_D()
    common_D = dict(
        opT=_f(inp["L1_attn_out_proj_w"].T),
        gw=swz(inp["L1_ffn_gate_w"]),
        gb=_f(inp["L1_ffn_gate_b"]).reshape(1, 8),
        w1=_f(inp["L1_ffn_w1"]), b1c=swz(np.asarray(inp["L1_ffn_b1"]).T),
        w2=_f(inp["L1_ffn_w2"]), b2=_f(inp["L1_ffn_b2"]),
    )
    in_maps = []
    for c in range(NCORE):
        m = dict(common_D)
        m["ymc"] = _f(ym_g[:, c * OWN:(c + 1) * OWN])
        m["xn1Ti"] = np.ascontiguousarray(
            np.asarray(resA.results[c]["xn1T"]), dtype=ml_dtypes.bfloat16)
        m["qxi"] = _f(qx_g[c * OWN:(c + 1) * OWN])
        m["x1i"] = _f(x1_g[c * OWN:(c + 1) * OWN])
        in_maps.append(m)
    resD = run_bass_kernel_spmd(ncD, in_maps, core_ids=list(range(NCORE)))
    out = np.concatenate([np.asarray(r["fin"]) for r in resD.results], axis=0)
    DBG.clear()
    DBG.update(dict(x1=x1_g, xb=xb_g, gmax8=gmax8, qx=qx_g, dtT=dtT_g, udT=udT_g,
                    dxT=dxT_g, szT=szT_g, BT=BT_g, CT=CT_g, ym=ym_g))
    return out.reshape(1, T, D).astype(np.float32)


# version 04 (final)
# speedup vs baseline: 1.2652x; 1.0218x over previous
from contextlib import ExitStack

import numpy as np
import ml_dtypes
import concourse.bass as bass
import concourse.mybir as mybir
from concourse.bass_utils import run_bass_kernel_spmd

F32 = mybir.dt.float32
BF16 = mybir.dt.bfloat16
AF = mybir.ActivationFunctionType
ALU = mybir.AluOpType
AX = mybir.AxisListType

NCORE = 8
T = 16384
D = 1024
OWN = 2048
HALO = 512
WTOT = OWN + HALO
NT = WTOT // 128
L = 128
NCHUNK = WTOT // L
OWNCH0 = HALO // L
H = 16
DH = 64
DI = 2048
DS = 16
EPS = 1e-5
BHALO = 128
BW = OWN + BHALO

ENGS = ["sync", "scalar", "vector", "gpsimd", "tensor"]


class Prog:

    def __init__(self, nc, sem):
        self.nc, self.sem, self.items, self.cnt = nc, sem, [], 0
        self.last_eng = None
        self.pe_floor = 0

    def op(self, eng, meth, *a, inc=1, **kw):
        if eng in ("tensor", "sync") and self.last_eng == eng:
            wait = self.pe_floor
        else:
            wait = self.cnt
            if eng in ("tensor", "sync"):
                self.pe_floor = self.cnt
        self.last_eng = eng
        self.items.append((eng, meth, a, kw, wait, inc))
        self.cnt += inc

    def dma(self, out, in_, eng="sync", **kw):
        self.op(eng, "dma_start", inc=16, out=out, in_=in_, **kw)

    def flush(self, final=False):
        if not self.items and not final:
            return
        with self.nc.Block() as block:
            for eng_name in ENGS:
                mine = [it for it in self.items if it[0] == eng_name]
                if not mine and not final:
                    continue

                def body(eng, mine=mine):
                    for (_, meth, a, kw, wait, inc) in mine:
                        if wait > 0:
                            eng.wait_ge(self.sem, wait)
                        getattr(eng, meth)(*a, **kw).then_inc(self.sem, inc)
                    eng.wait_ge(self.sem, self.cnt)

                getattr(block, eng_name)(body)
        self.items = []


def f2(ap):
    return ap.rearrange("p a b -> p (a b)")


def sl(t, i, a, b):
    return f2(t[:, i:i + 1, a:b])


def emit_ident(p, nc, irow, icol, identf, identb, maskut):
    p.op("gpsimd", "iota", irow[:, :], [[1, 128]], channel_multiplier=0,
         allow_small_or_imprecise_dtypes=True)
    p.op("gpsimd", "iota", icol[:, :], [[0, 1]], channel_multiplier=1,
         allow_small_or_imprecise_dtypes=True)
    p.op("vector", "tensor_scalar", out=identf[:, :], in0=irow[:, :],
         scalar1=icol[:, :], scalar2=None, op0=ALU.is_equal)
    p.op("vector", "tensor_copy", out=identb[:, :], in_=identf[:, :])
    if maskut is not None:
        p.op("vector", "tensor_scalar", out=maskut[:, :], in0=irow[:, :],
             scalar1=icol[:, :], scalar2=None, op0=ALU.is_gt)


def emit_ln(p, xin, xout, scr, cols, n=1024, eps=EPS):
    p.op("vector", "tensor_reduce", out=cols[:, 0:1], in_=xin, axis=AX.X, op=ALU.add)
    p.op("vector", "tensor_scalar", out=cols[:, 1:2], in0=cols[:, 0:1],
         scalar1=-1.0 / n, scalar2=None, op0=ALU.mult)
    p.op("vector", "tensor_scalar", out=xout, in0=xin, scalar1=cols[:, 1:2],
         scalar2=None, op0=ALU.add)
    p.op("scalar", "activation", out=scr, in_=xout, func=AF.Square,
         accum_out=cols[:, 2:3])
    p.op("vector", "tensor_scalar", out=cols[:, 3:4], in0=cols[:, 2:3],
         scalar1=1.0 / n, scalar2=eps, op0=ALU.mult, op1=ALU.add)
    p.op("scalar", "activation", out=cols[:, 3:4], in_=cols[:, 3:4], func=AF.Ln)
    p.op("scalar", "activation", out=cols[:, 2:3], in_=cols[:, 3:4], func=AF.Exp,
         scale=-0.5)
    p.op("vector", "tensor_scalar", out=xout, in0=xout, scalar1=cols[:, 2:3],
         scalar2=None, op0=ALU.mult)


def emit_softplus(p, zin, out, f1, f2_, bias_col=None):
    if bias_col is not None:
        p.op("vector", "tensor_scalar", out=f1, in0=zin, scalar1=bias_col,
             scalar2=None, op0=ALU.add)
        zsrc = f1
    else:
        zsrc = zin
    p.op("scalar", "activation", out=f2_, in_=zsrc, func=AF.Abs)
    p.op("scalar", "activation", out=f2_, in_=f2_, func=AF.Exp, scale=-1.0)
    p.op("scalar", "activation", out=f2_, in_=f2_, func=AF.Ln, bias=1.0)
    p.op("vector", "tensor_scalar", out=out, in0=zsrc, scalar1=0.0, scalar2=None,
         op0=ALU.max)
    p.op("vector", "tensor_tensor", out=out, in0=out, in1=f2_, op=ALU.add)



def build_A():
    nc = bass.Bass("TRN2", target_bir_lowering=False)
    xs = nc.dram_tensor("xs", [WTOT, D], F32, kind="ExternalInput").ap()
    wrT = nc.dram_tensor("wrT", [D, D], F32, kind="ExternalInput").ap()
    wkT = nc.dram_tensor("wkT", [D, D], F32, kind="ExternalInput").ap()
    wvT = nc.dram_tensor("wvT", [D, D], F32, kind="ExternalInput").ap()
    wwT = nc.dram_tensor("wwT", [D, D], F32, kind="ExternalInput").ap()
    woT = nc.dram_tensor("woT", [D, D], F32, kind="ExternalInput").ap()
    tmc = nc.dram_tensor("tmc", [128, 32], F32, kind="ExternalInput").ap()
    ucol = nc.dram_tensor("ucol", [128, 8], F32, kind="ExternalInput").ap()
    lncol = nc.dram_tensor("lncol", [128, 16], F32, kind="ExternalInput").ap()
    fw1 = nc.dram_tensor("fw1", [D, 4096], F32, kind="ExternalInput").ap()
    fb1 = nc.dram_tensor("fb1", [128, 32], F32, kind="ExternalInput").ap()
    fw2 = nc.dram_tensor("fw2", [4096, D], F32, kind="ExternalInput").ap()
    fb2 = nc.dram_tensor("fb2", [1, D], F32, kind="ExternalInput").ap()
    bW1T = nc.dram_tensor("bW1T", [D, DI], F32, kind="ExternalInput").ap()
    bb1 = nc.dram_tensor("bb1", [128, 16], F32, kind="ExternalInput").ap()
    bW2T = nc.dram_tensor("bW2T", [DI, D], F32, kind="ExternalInput").ap()
    bb2 = nc.dram_tensor("bb2", [1, D], F32, kind="ExternalInput").ap()
    l1g = nc.dram_tensor("l1g", [1, D], F32, kind="ExternalInput").ap()
    l1b = nc.dram_tensor("l1b", [1, D], F32, kind="ExternalInput").ap()

    x1 = nc.dram_tensor("x1", [OWN, D], F32, kind="ExternalOutput").ap()
    xbo = nc.dram_tensor("xbo", [OWN, D], F32, kind="ExternalOutput").ap()
    xn1T = nc.dram_tensor("xn1T", [D, OWN], BF16, kind="ExternalOutput").ap()
    gmax = nc.dram_tensor("gmax", [1, 1], F32, kind="ExternalOutput").ap()

    rTs = nc.dram_tensor("rTs", [D, WTOT], BF16, kind="Internal").ap()
    kTs = nc.dram_tensor("kTs", [D, WTOT], BF16, kind="Internal").ap()
    wTs = nc.dram_tensor("wTs", [D, WTOT], F32, kind="Internal").ap()
    vs = nc.dram_tensor("vs", [WTOT, D], BF16, kind="Internal").ap()
    p1s = nc.dram_tensor("p1s", [OWN, D], F32, kind="Internal").ap()

    with ExitStack() as _st:
        S = _st.enter_context(nc.semaphore("S"))
        PA = _st.enter_context(nc.psum_tensor("PA", [128, 1024], F32))
        PB = _st.enter_context(nc.psum_tensor("PB", [128, 512], F32))
        PC = _st.enter_context(nc.psum_tensor("PC", [128, 128], F32))
        PD = _st.enter_context(nc.psum_tensor("PD", [128, 16], F32))
        PE = _st.enter_context(nc.psum_tensor("PE", [128, 128], BF16))
        PF = _st.enter_context(nc.psum_tensor("PF", [128, 128], F32))
        irow = _st.enter_context(nc.sbuf_tensor("irow", [128, 128], F32))
        icol = _st.enter_context(nc.sbuf_tensor("icol", [128, 1], F32))
        identf = _st.enter_context(nc.sbuf_tensor("identf", [128, 128], F32))
        identb = _st.enter_context(nc.sbuf_tensor("identb", [128, 128], BF16))
        maskut = _st.enter_context(nc.sbuf_tensor("maskut", [128, 128], F32))
        ones1 = _st.enter_context(nc.sbuf_tensor("ones1", [1, 128], F32))
        ones64 = _st.enter_context(nc.sbuf_tensor("ones64", [128, 1], F32))
        tmsb = _st.enter_context(nc.sbuf_tensor("tmsb", [128, 8, 4], F32))
        usb = _st.enter_context(nc.sbuf_tensor("usb", [128, 8, 1], F32))
        lnsb = _st.enter_context(nc.sbuf_tensor("lnsb", [128, 8, 2], F32))
        wosb = _st.enter_context(nc.sbuf_tensor("wosb", [128, 8, 1024], BF16))
        Ssb = _st.enter_context(nc.sbuf_tensor("Ssb", [128, 8, 64], F32))
        p = Prog(nc, S)
        emit_ident(p, nc, irow, icol, identf, identb, maskut)
        p.op("gpsimd", "memset", ones1[:, :], 1.0)
        p.op("gpsimd", "memset", ones64[:, :], 1.0)
        p.op("gpsimd", "memset", Ssb[:, :, :], 0.0)
        p.dma(tmsb[:, :, :].rearrange("p a b -> p (a b)"), tmc[:, :])
        p.dma(usb[:, :, :].rearrange("p a b -> p (a b)"), ucol[:, :])
        p.dma(lnsb[:, :, :].rearrange("p a b -> p (a b)"), lncol[:, :])

        with ExitStack() as _st:
            xnt = _st.enter_context(nc.sbuf_tensor("xnt", [128, 8, WTOT + 1], BF16))
            stg = _st.enter_context(nc.sbuf_tensor("stg", [128, 8, 512], F32))
            with ExitStack() as _st:
                xtile = _st.enter_context(nc.sbuf_tensor("xtile", [128, 1024], F32))
                lnx = _st.enter_context(nc.sbuf_tensor("lnx", [128, 1024], F32))
                xnb = _st.enter_context(nc.sbuf_tensor("xnb", [128, 1024], BF16))
                cols = _st.enter_context(nc.sbuf_tensor("cols", [128, 4], F32))
                for g in range(8):
                    p.op("gpsimd", "memset", sl(xnt, g, 0, 1), 0.0)
                for i in range(NT):
                    p.dma(xtile[:, :], xs[i * 128:(i + 1) * 128, :])
                    emit_ln(p, xtile[:, :], lnx[:, :], xtile[:, :], cols)
                    p.op("vector", "tensor_copy", out=xnb[:, :], in_=lnx[:, :])
                    for g in range(8):
                        p.op("tensor", "transpose", PE[:, :],
                             xnb[:, g * 128:(g + 1) * 128], identb[:, :])
                        p.op("scalar", "copy",
                             out=sl(xnt, g, 1 + i * 128, 1 + (i + 1) * 128),
                             in_=PE[:, :])
                p.flush()

            with ExitStack() as _st:
                wrsb = _st.enter_context(nc.sbuf_tensor("wrsb", [128, 8, 1024], BF16))
                wksb = _st.enter_context(nc.sbuf_tensor("wksb", [128, 8, 1024], BF16))
                wvsb = _st.enter_context(nc.sbuf_tensor("wvsb", [128, 8, 1024], BF16))
                wwsb = _st.enter_context(nc.sbuf_tensor("wwsb", [128, 8, 1024], BF16))
                mixsb = _st.enter_context(nc.sbuf_tensor("mixsb", [128, 32, 512], BF16))
                dstrip = _st.enter_context(nc.sbuf_tensor("dstrip", [128, 512], F32))
                xpf = _st.enter_context(nc.sbuf_tensor("xpf", [128, 512], F32))
                ev1 = _st.enter_context(nc.sbuf_tensor("ev1", [128, 512], F32))
                ev2 = _st.enter_context(nc.sbuf_tensor("ev2", [128, 512], F32))
                evb = _st.enter_context(nc.sbuf_tensor("evb", [128, 512], BF16))
                for wdram, wsb in ((wrT, wrsb), (wkT, wksb), (wvT, wvsb),
                                   (wwT, wwsb), (woT, wosb)):
                    wv3 = wdram.rearrange("(g p) c -> p g c", p=128)
                    for half in range(2):
                        p.dma(stg[:, :, :], wv3[:, :, half * 512:(half + 1) * 512])
                        for g in range(8):
                            p.op("vector", "tensor_copy",
                                 out=sl(wsb, g, half * 512, (half + 1) * 512),
                                 in_=sl(stg, g, 0, 512))
                for tc in range(5):
                    c0 = tc * 512
                    for ci in range(8):
                        p.op("vector", "tensor_tensor", out=dstrip[:, :],
                             in0=sl(xnt, ci, 1 + c0, 1 + c0 + 512),
                             in1=sl(xnt, ci, c0, c0 + 512), op=ALU.subtract)
                        p.op("vector", "tensor_copy", out=xpf[:, :],
                             in_=sl(xnt, ci, c0, c0 + 512))
                        for ai in range(4):
                            p.op("vector", "scalar_tensor_tensor",
                                 out=sl(mixsb, ai * 8 + ci, 0, 512),
                                 in0=dstrip[:, :], scalar=f2(tmsb[:, ci:ci + 1, ai:ai + 1]),
                                 in1=xpf[:, :], op0=ALU.mult, op1=ALU.add)
                    for g in range(8):
                        for ai, wsb, spill in ((0, wrsb, rTs), (1, wksb, kTs),
                                               (3, wwsb, wTs)):
                            for ci in range(8):
                                p.op("tensor", "matmul", PB[:, :],
                                     sl(wsb, ci, g * 128, (g + 1) * 128),
                                     sl(mixsb, ai * 8 + ci, 0, 512),
                                     start=(ci == 0), stop=(ci == 7))
                            dst = spill[g * 128:(g + 1) * 128, c0:c0 + 512]
                            if ai == 0:
                                p.op("scalar", "activation", out=evb[:, :],
                                     in_=PB[:, :], func=AF.Sigmoid)
                                p.dma(dst, evb[:, :])
                            elif ai == 1:
                                p.op("vector", "tensor_copy", out=evb[:, :],
                                     in_=PB[:, :])
                                p.dma(dst, evb[:, :])
                            else:
                                p.op("scalar", "activation", out=ev1[:, :],
                                     in_=PB[:, :], func=AF.Abs)
                                p.op("scalar", "activation", out=ev1[:, :],
                                     in_=ev1[:, :], func=AF.Exp, scale=-1.0)
                                p.op("scalar", "activation", out=ev1[:, :],
                                     in_=ev1[:, :], func=AF.Ln, bias=1.0)
                                p.op("vector", "tensor_scalar", out=ev2[:, :],
                                     in0=PB[:, :], scalar1=0.0, scalar2=-1e-4,
                                     op0=ALU.min, op1=ALU.add)
                                p.op("vector", "tensor_tensor", out=ev2[:, :],
                                     in0=ev2[:, :], in1=ev1[:, :], op=ALU.subtract)
                                p.dma(dst, ev2[:, :])
                    for i128 in range(4):
                        ti = tc * 4 + i128
                        for half in range(2):
                            for ci in range(8):
                                p.op("tensor", "matmul", PB[:, :],
                                     f2(mixsb[:, 2 * 8 + ci:2 * 8 + ci + 1,
                                              i128 * 128:(i128 + 1) * 128]),
                                     sl(wvsb, ci, half * 512, (half + 1) * 512),
                                     start=(ci == 0), stop=(ci == 7))
                            p.op("vector", "tensor_copy", out=evb[:, :], in_=PB[:, :])
                            p.dma(vs[ti * 128:(ti + 1) * 128,
                                     half * 512:(half + 1) * 512], evb[:, :])
                    p.flush()

            with ExitStack() as _st:
                fw1sb = _st.enter_context(nc.sbuf_tensor("fw1sb", [128, 8, 4096], BF16))
                fbsb = _st.enter_context(nc.sbuf_tensor("fbsb", [128, 32, 1], F32))
                b2row = _st.enter_context(nc.sbuf_tensor("b2row", [1, 1024], F32))
                b2rep = _st.enter_context(nc.sbuf_tensor("b2rep", [128, 1024], F32))
                hsb = _st.enter_context(nc.sbuf_tensor("hsb", [128, 32, 512], BF16))
                fw2c = _st.enter_context(nc.sbuf_tensor("fw2c", [128, 1024], BF16))
                oacc = _st.enter_context(nc.sbuf_tensor("oacc", [128, 4, 1024], F32))
                xrow = _st.enter_context(nc.sbuf_tensor("xrow", [128, 1024], F32))
                fw13 = fw1.rearrange("(g p) c -> p g c", p=128)
                for piece in range(8):
                    p.dma(stg[:, :, :], fw13[:, :, piece * 512:(piece + 1) * 512])
                    for g in range(8):
                        p.op("vector", "tensor_copy",
                             out=sl(fw1sb, g, piece * 512, (piece + 1) * 512),
                             in_=sl(stg, g, 0, 512))
                p.dma(fbsb[:, :, :].rearrange("p a b -> p (a b)"), fb1[:, :])
                p.dma(b2row[:, :], fb2[:, :])
                for half in range(2):
                    p.op("tensor", "matmul", PB[:, :], ones1[0:1, :],
                         b2row[0:1, half * 512:(half + 1) * 512], start=True, stop=True)
                    p.op("scalar", "copy",
                         out=b2rep[:, half * 512:(half + 1) * 512], in_=PB[:, :])
                for tco in range(4):
                    c0 = 1 + HALO + tco * 512
                    p.op("gpsimd", "memset", oacc[:, :, :], 0.0)
                    for hc in range(32):
                        for ci in range(8):
                            p.op("tensor", "matmul", PB[:, :],
                                 sl(fw1sb, ci, hc * 128, (hc + 1) * 128),
                                 sl(xnt, ci, c0, c0 + 512),
                                 start=(ci == 0), stop=(ci == 7))
                        p.op("scalar", "activation", out=sl(hsb, hc, 0, 512),
                             in_=PB[:, :], func=AF.Gelu,
                             bias=f2(fbsb[:, hc:hc + 1, 0:1]))
                    for hc in range(32):
                        p.dma(stg[:, 0:2, :],
                              fw2[hc * 128:(hc + 1) * 128, :].rearrange(
                                  "p (a c) -> p a c", a=2))
                        p.op("vector", "tensor_copy", out=fw2c[:, :],
                             in_=stg[:, 0:2, :].rearrange("p a c -> p (a c)"))
                        for i128 in range(4):
                            for half in range(2):
                                p.op("tensor", "matmul",
                                     PA[:, half * 512:(half + 1) * 512],
                                     f2(hsb[:, hc:hc + 1, i128 * 128:(i128 + 1) * 128]),
                                     fw2c[:, half * 512:(half + 1) * 512],
                                     start=True, stop=True)
                            p.op("vector", "tensor_tensor",
                                 out=f2(oacc[:, i128:i128 + 1, :]),
                                 in0=f2(oacc[:, i128:i128 + 1, :]),
                                 in1=PA[:, :], op=ALU.add)
                    for i128 in range(4):
                        r0 = HALO + tco * 512 + i128 * 128
                        p.dma(xrow[:, :], xs[r0:r0 + 128, :])
                        p.op("vector", "tensor_tensor", out=xrow[:, :], in0=xrow[:, :],
                             in1=b2rep[:, :], op=ALU.add)
                        p.op("vector", "tensor_tensor", out=xrow[:, :], in0=xrow[:, :],
                             in1=f2(oacc[:, i128:i128 + 1, :]), op=ALU.add)
                        p.dma(p1s[tco * 512 + i128 * 128:tco * 512 + (i128 + 1) * 128, :],
                              xrow[:, :])
                    p.flush()

        with ExitStack() as _st:
            rsp = _st.enter_context(nc.sbuf_tensor("rsp", [128, 8, 512], BF16))
            ksp = _st.enter_context(nc.sbuf_tensor("ksp", [128, 8, 512], BF16))
            wsp = _st.enter_context(nc.sbuf_tensor("wsp", [128, 8, 512], F32))
            vsp = _st.enter_context(nc.sbuf_tensor("vsp", [128, 4, 1024], BF16))
            zero128 = _st.enter_context(nc.sbuf_tensor("zero128", [128, 128], F32))
            cumw = _st.enter_context(nc.sbuf_tensor("cumw", [128, 128], F32))
            cw1 = _st.enter_context(nc.sbuf_tensor("cw1", [128, 128], F32))
            tf1 = _st.enter_context(nc.sbuf_tensor("tf1", [128, 128], F32))
            tf2 = _st.enter_context(nc.sbuf_tensor("tf2", [128, 128], F32))
            tf4 = _st.enter_context(nc.sbuf_tensor("tf4", [128, 128], F32))
            rf = _st.enter_context(nc.sbuf_tensor("rf", [128, 128], F32))
            kf = _st.enter_context(nc.sbuf_tensor("kf", [128, 128], F32))
            rtm = _st.enter_context(nc.sbuf_tensor("rtm", [128, 128], BF16))
            ktm = _st.enter_context(nc.sbuf_tensor("ktm", [128, 128], BF16))
            ksb = _st.enter_context(nc.sbuf_tensor("ksb", [128, 128], BF16))
            rsb = _st.enter_context(nc.sbuf_tensor("rsb", [128, 128], BF16))
            Gm0 = _st.enter_context(nc.sbuf_tensor("Gm0", [128, 128], BF16))
            Gm1 = _st.enter_context(nc.sbuf_tensor("Gm1", [128, 128], BF16))
            Sbb = _st.enter_context(nc.sbuf_tensor("Sbb", [128, 64], BF16))
            kstsb = _st.enter_context(nc.sbuf_tensor("kstsb", [128, 128], BF16))
            osb = _st.enter_context(nc.sbuf_tensor("osb", [128, 128], F32))
            o2b = _st.enter_context(nc.sbuf_tensor("o2b", [128, 128], BF16))
            dcsb = _st.enter_context(nc.sbuf_tensor("dcsb", [128, 2], F32))
            ccol = _st.enter_context(nc.sbuf_tensor("ccol", [128, 1], F32))
            gcol = _st.enter_context(nc.sbuf_tensor("gcol", [128, 4], F32))
            p1row = _st.enter_context(nc.sbuf_tensor("p1row", [128, 1024], F32))
            p.op("gpsimd", "memset", zero128[:, :], 0.0)
            for c in range(NCHUNK):
                span, vidx = c // 4, c % 4
                if vidx == 0:
                    s0 = span * 4 * 128
                    for dram, sb in ((rTs, rsp), (kTs, ksp), (wTs, wsp)):
                        p.dma(sb[:, :, :],
                              dram.rearrange("(g p) t -> p g t", p=128)[
                                  :, :, s0:s0 + 512])
                    p.dma(vsp[:, :, :],
                          vs.rearrange("(n p) c -> p n c", p=128)[
                              :, span * 4:span * 4 + 4, :])
                t0 = vidx * 128
                for g in range(8):
                    wch = sl(wsp, g, t0, t0 + 128)
                    p.op("vector", "tensor_tensor_scan", out=cumw[:, :], data0=wch,
                         data1=zero128[:, :], initial=0.0, op0=ALU.add, op1=ALU.add)
                    p.op("vector", "tensor_tensor", out=cw1[:, :], in0=cumw[:, :],
                         in1=wch, op=ALU.subtract)
                    cmid = cumw[:, 63:64]
                    dlast = cumw[:, 127:128]
                    p.op("vector", "tensor_copy", out=rf[:, :], in_=sl(rsp, g, t0, t0 + 128))
                    p.op("vector", "tensor_copy", out=kf[:, :], in_=sl(ksp, g, t0, t0 + 128))
                    p.op("vector", "tensor_scalar", out=tf1[:, :], in0=cw1[:, :],
                         scalar1=cmid, scalar2=None, op0=ALU.subtract)
                    p.op("scalar", "activation", out=tf1[:, :], in_=tf1[:, :], func=AF.Exp)
                    p.op("vector", "tensor_tensor", out=rtm[:, :], in0=tf1[:, :],
                         in1=rf[:, :], op=ALU.mult)
                    p.op("vector", "tensor_scalar", out=tf2[:, :], in0=cumw[:, :],
                         scalar1=-1.0, scalar2=cmid, op0=ALU.mult, op1=ALU.add)
                    p.op("scalar", "activation", out=tf2[:, :], in_=tf2[:, :], func=AF.Exp)
                    p.op("vector", "tensor_tensor", out=ktm[:, :], in0=tf2[:, :],
                         in1=kf[:, :], op=ALU.mult)
                    p.op("vector", "tensor_scalar", out=tf2[:, :], in0=cumw[:, :],
                         scalar1=-1.0, scalar2=dlast, op0=ALU.mult, op1=ALU.add)
                    p.op("scalar", "activation", out=tf2[:, :], in_=tf2[:, :], func=AF.Exp)
                    p.op("vector", "tensor_tensor", out=ksb[:, :], in0=tf2[:, :],
                         in1=kf[:, :], op=ALU.mult)
                    p.op("scalar", "activation", out=tf4[:, :], in_=cw1[:, :], func=AF.Exp)
                    p.op("vector", "tensor_tensor", out=rsb[:, :], in0=tf4[:, :],
                         in1=rf[:, :], op=ALU.mult)
                    p.op("scalar", "activation", out=tf4[:, :],
                         in_=sl(ksp, g, t0, t0 + 128), func=AF.Exp,
                         bias=f2(usb[:, g:g + 1, 0:1]))
                    p.op("vector", "tensor_tensor", out=tf4[:, :], in0=tf4[:, :],
                         in1=rf[:, :], op=ALU.mult)
                    p.op("scalar", "activation", out=ccol[:, 0:1], in_=dlast, func=AF.Exp)
                    for h in range(2):
                        p.op("tensor", "matmul", PD[:, h:h + 1],
                             tf4[h * 64:(h + 1) * 64, :],
                             ones64[h * 64:(h + 1) * 64, :],
                             start=True, stop=True)
                    p.op("scalar", "copy", out=dcsb[:, 0:2], in_=PD[:, 0:2])
                    for h, Gm in ((0, Gm0), (1, Gm1)):
                        p.op("tensor", "matmul", PC[:, :],
                             ktm[h * 64:(h + 1) * 64, :], rtm[h * 64:(h + 1) * 64, :],
                             start=True, stop=True)
                        p.op("vector", "tensor_scalar", out=tf1[:, :], in0=PC[:, :],
                             scalar1=1e30, scalar2=-1e30, op0=ALU.min, op1=ALU.max)
                        p.op("vector", "tensor_tensor", out=Gm[:, :], in0=tf1[:, :],
                             in1=maskut[:, :], op=ALU.mult)
                    p.op("vector", "tensor_copy", out=Sbb[:, :], in_=f2(Ssb[:, g:g + 1, :]))
                    for h, Gm in ((0, Gm0), (1, Gm1)):
                        vch = f2(vsp[:, vidx:vidx + 1, g * 128 + h * 64:g * 128 + (h + 1) * 64])
                        p.op("tensor", "matmul", PF[:, h * 64:(h + 1) * 64],
                             Gm[:, :], vch, start=True, stop=False)
                        p.op("tensor", "matmul", PF[:, h * 64:(h + 1) * 64],
                             rsb[h * 64:(h + 1) * 64, :], Sbb[h * 64:(h + 1) * 64, :],
                             start=False, stop=True)
                    for h in range(2):
                        vch = f2(vsp[:, vidx:vidx + 1, g * 128 + h * 64:g * 128 + (h + 1) * 64])
                        p.op("vector", "scalar_tensor_tensor",
                             out=osb[:, h * 64:(h + 1) * 64], in0=vch,
                             scalar=dcsb[:, h:h + 1], in1=PF[:, h * 64:(h + 1) * 64],
                             op0=ALU.mult, op1=ALU.add)
                    p.op("vector", "tensor_reduce", out=gcol[:, 0:2],
                         in_=osb[:, :].rearrange("p (a b) -> p a b", a=2),
                         axis=AX.X, op=ALU.add)
                    p.op("vector", "tensor_scalar", out=gcol[:, 2:4], in0=gcol[:, 0:2],
                         scalar1=-1.0 / 64, scalar2=None, op0=ALU.mult)
                    for h in range(2):
                        p.op("vector", "tensor_scalar", out=osb[:, h * 64:(h + 1) * 64],
                             in0=osb[:, h * 64:(h + 1) * 64],
                             scalar1=gcol[:, 2 + h:3 + h], scalar2=None, op0=ALU.add)
                        p.op("scalar", "activation", out=tf4[:, h * 64:(h + 1) * 64],
                             in_=osb[:, h * 64:(h + 1) * 64], func=AF.Square,
                             accum_out=gcol[:, h:h + 1])
                    p.op("vector", "tensor_scalar", out=gcol[:, 2:4], in0=gcol[:, 0:2],
                         scalar1=1.0 / 64, scalar2=EPS, op0=ALU.mult, op1=ALU.add)
                    p.op("scalar", "activation", out=gcol[:, 2:4], in_=gcol[:, 2:4],
                         func=AF.Ln)
                    p.op("scalar", "activation", out=gcol[:, 0:2], in_=gcol[:, 2:4],
                         func=AF.Exp, scale=-0.5)
                    for h in range(2):
                        p.op("vector", "tensor_scalar", out=osb[:, h * 64:(h + 1) * 64],
                             in0=osb[:, h * 64:(h + 1) * 64],
                             scalar1=gcol[:, h:h + 1], scalar2=None, op0=ALU.mult)
                    p.op("tensor", "transpose", PC[:, :], osb[:, :], identf[:, :])
                    p.op("vector", "tensor_scalar", out=tf4[:, :], in0=PC[:, :],
                         scalar1=f2(lnsb[:, g:g + 1, 0:1]),
                         scalar2=f2(lnsb[:, g:g + 1, 1:2]), op0=ALU.mult, op1=ALU.add)
                    p.op("vector", "tensor_tensor", out=o2b[:, :], in0=tf4[:, :],
                         in1=rf[:, :], op=ALU.mult)
                    for half in range(2):
                        p.op("tensor", "matmul", PA[:, half * 512:(half + 1) * 512],
                             o2b[:, :], sl(wosb, g, half * 512, (half + 1) * 512),
                             start=(g == 0), stop=(g == 7), skip_group_check=True)
                    p.op("tensor", "transpose", PE[:, :], ksb[:, :], identb[:, :])
                    p.op("scalar", "copy", out=kstsb[:, :], in_=PE[:, :])
                    for h in range(2):
                        vch = f2(vsp[:, vidx:vidx + 1, g * 128 + h * 64:g * 128 + (h + 1) * 64])
                        p.op("tensor", "matmul", PF[:, h * 64:(h + 1) * 64],
                             kstsb[:, :], vch, start=True, stop=True)
                    for h in range(2):
                        p.op("vector", "scalar_tensor_tensor",
                             out=f2(Ssb[h * 64:(h + 1) * 64, g:g + 1, :]),
                             in0=f2(Ssb[h * 64:(h + 1) * 64, g:g + 1, :]),
                             scalar=ccol[h * 64:(h + 1) * 64, 0:1],
                             in1=PF[h * 64:(h + 1) * 64, h * 64:(h + 1) * 64],
                             op0=ALU.mult, op1=ALU.add)
                if c >= OWNCH0:
                    r0 = (c - OWNCH0) * 128
                    p.dma(p1row[:, :], p1s[r0:r0 + 128, :])
                    p.op("vector", "tensor_tensor", out=p1row[:, :], in0=p1row[:, :],
                         in1=PA[:, :], op=ALU.add)
                    p.dma(x1[r0:r0 + 128, :], p1row[:, :])
                p.flush()

        with ExitStack() as _st:
            sw1sb = _st.enter_context(nc.sbuf_tensor("sw1sb", [128, 8, 2048], BF16))
            sw2sb = _st.enter_context(nc.sbuf_tensor("sw2sb", [128, 16, 1024], BF16))
            stg2 = _st.enter_context(nc.sbuf_tensor("stg2", [128, 8, 1024], F32))
            bbsb = _st.enter_context(nc.sbuf_tensor("bbsb", [128, 16, 1], F32))
            bb2rep = _st.enter_context(nc.sbuf_tensor("bb2rep", [128, 1024], F32))
            g1rep = _st.enter_context(nc.sbuf_tensor("g1rep", [128, 1024], F32))
            b1rep = _st.enter_context(nc.sbuf_tensor("b1rep", [128, 1024], F32))
            rowt = _st.enter_context(nc.sbuf_tensor("rowt", [1, 1024], F32))
            x1t = _st.enter_context(nc.sbuf_tensor("x1t", [128, 8, 512], BF16))
            u2t = _st.enter_context(nc.sbuf_tensor("u2t", [128, 8, 512], BF16))
            gatb = _st.enter_context(nc.sbuf_tensor("gatb", [128, 16, 512], BF16))
            xn1f = _st.enter_context(nc.sbuf_tensor("xn1f", [128, 4, 1024], F32))
            x1row = _st.enter_context(nc.sbuf_tensor("x1row", [128, 1024], F32))
            lscr = _st.enter_context(nc.sbuf_tensor("lscr", [128, 1024], F32))
            lscr2 = _st.enter_context(nc.sbuf_tensor("lscr2", [128, 1024], F32))
            xbb = _st.enter_context(nc.sbuf_tensor("xbb", [128, 1024], BF16))
            bcol = _st.enter_context(nc.sbuf_tensor("bcol", [128, 4], F32))
            gmcol = _st.enter_context(nc.sbuf_tensor("gmcol", [128, 1], F32))
            gmrow = _st.enter_context(nc.sbuf_tensor("gmrow", [1, 128], F32))
            w13 = bW1T.rearrange("(g p) c -> p g c", p=128)
            for half in range(2):
                p.dma(stg2[:, :, :], w13[:, :, half * 1024:(half + 1) * 1024])
                p.op("scalar", "activation",
                     out=sw1sb[:, :, half * 1024:(half + 1) * 1024],
                     in_=stg2[:, :, :], func=AF.Sign)
            w23 = bW2T.rearrange("(g p) c -> p g c", p=128)
            for half in range(2):
                p.dma(stg2[:, :, :], w23[:, half * 8:(half + 1) * 8, :])
                p.op("scalar", "activation",
                     out=sw2sb[:, half * 8:(half + 1) * 8, :].rearrange(
                         "p a b -> p (a b)"),
                     in_=stg2[:, :, :].rearrange("p a b -> p (a b)"), func=AF.Sign)
            p.dma(bbsb[:, :, :].rearrange("p a b -> p (a b)"), bb1[:, :])
            for name_row, rep in ((bb2, bb2rep), (l1g, g1rep), (l1b, b1rep)):
                p.dma(rowt[:, :], name_row[:, :])
                for half in range(2):
                    p.op("tensor", "matmul", PB[:, :], ones1[0:1, :],
                         rowt[0:1, half * 512:(half + 1) * 512], start=True, stop=True)
                    p.op("scalar", "copy", out=rep[:, half * 512:(half + 1) * 512],
                         in_=PB[:, :])
            p.op("gpsimd", "memset", gmcol[:, :], 0.0)
            for tco in range(4):
                for i128 in range(4):
                    r0 = tco * 512 + i128 * 128
                    p.dma(x1row[:, :], x1[r0:r0 + 128, :])
                    emit_ln(p, x1row[:, :], f2(xn1f[:, i128:i128 + 1, :]), lscr[:, :], bcol)
                    p.op("vector", "tensor_copy", out=xbb[:, :],
                         in_=f2(xn1f[:, i128:i128 + 1, :]))
                    for g in range(8):
                        p.op("tensor", "transpose", PE[:, :],
                             xbb[:, g * 128:(g + 1) * 128], identb[:, :])
                        p.op("scalar", "copy",
                             out=sl(x1t, g, i128 * 128, (i128 + 1) * 128), in_=PE[:, :])
                    emit_ln(p, f2(xn1f[:, i128:i128 + 1, :]), lscr[:, :], lscr2[:, :], bcol)
                    p.op("vector", "tensor_tensor", out=lscr[:, :], in0=lscr[:, :],
                         in1=g1rep[:, :], op=ALU.mult)
                    p.op("vector", "tensor_tensor", out=lscr[:, :], in0=lscr[:, :],
                         in1=b1rep[:, :], op=ALU.add)
                    p.op("vector", "tensor_copy", out=xbb[:, :], in_=lscr[:, :])
                    for g in range(8):
                        p.op("tensor", "transpose", PE[:, :],
                             xbb[:, g * 128:(g + 1) * 128], identb[:, :])
                        p.op("scalar", "copy",
                             out=sl(u2t, g, i128 * 128, (i128 + 1) * 128), in_=PE[:, :])
                for g in range(8):
                    p.dma(xn1T[g * 128:(g + 1) * 128, tco * 512:(tco + 1) * 512],
                          sl(x1t, g, 0, 512))
                for oc in range(16):
                    for ci in range(8):
                        p.op("tensor", "matmul", PB[:, :],
                             sl(sw1sb, ci, oc * 128, (oc + 1) * 128),
                             sl(u2t, ci, 0, 512), start=(ci == 0), stop=(ci == 7))
                    p.op("vector", "tensor_scalar", out=lscr[:, 0:512], in0=PB[:, :],
                         scalar1=f2(bbsb[:, oc:oc + 1, 0:1]), scalar2=1.0,
                         op0=ALU.add, op1=ALU.min)
                    p.op("vector", "tensor_scalar", out=sl(gatb, oc, 0, 512),
                         in0=lscr[:, 0:512], scalar1=-1.0, scalar2=None, op0=ALU.max)
                for i128 in range(4):
                    for hc in range(16):
                        for half in range(2):
                            p.op("tensor", "matmul",
                                 PA[:, half * 512:(half + 1) * 512],
                                 f2(gatb[:, hc:hc + 1, i128 * 128:(i128 + 1) * 128]),
                                 sl(sw2sb, hc, half * 512, (half + 1) * 512),
                                 start=(hc == 0), stop=(hc == 15),
                                 skip_group_check=True)
                    p.op("vector", "tensor_tensor", out=lscr[:, :],
                         in0=f2(xn1f[:, i128:i128 + 1, :]), in1=bb2rep[:, :], op=ALU.add)
                    p.op("vector", "tensor_tensor", out=lscr[:, :], in0=lscr[:, :],
                         in1=PA[:, :], op=ALU.add)
                    r0 = tco * 512 + i128 * 128
                    p.dma(xbo[r0:r0 + 128, :], lscr[:, :])
                    p.op("vector", "tensor_reduce", out=bcol[:, 0:1], in_=lscr[:, :],
                         axis=AX.X, op=ALU.max, apply_absolute_value=True)
                    p.op("vector", "tensor_tensor", out=gmcol[:, :], in0=gmcol[:, :],
                         in1=bcol[:, 0:1], op=ALU.max)
                p.flush()
            p.op("tensor", "transpose", PC[0:1, :], gmcol[:, :], identf[:, :])
            p.op("scalar", "copy", out=gmrow[0:1, :], in_=PC[0:1, :])
            p.op("vector", "tensor_reduce", out=gmrow[0:1, 0:1], in_=gmrow[0:1, :],
                 axis=AX.X, op=ALU.max)
            p.dma(gmax[:, :], gmrow[0:1, 0:1])
            p.flush(final=True)
    return nc



def build_B():
    nc = bass.Bass("TRN2", target_bir_lowering=False)
    xbh = nc.dram_tensor("xbh", [BW, D], F32, kind="ExternalInput").ap()
    gmax8 = nc.dram_tensor("gmax8", [8, 1], F32, kind="ExternalInput").ap()
    gsel = nc.dram_tensor("gsel", [8, 2], F32, kind="ExternalInput").ap()
    hmask = nc.dram_tensor("hmask", [128, 1], F32, kind="ExternalInput").ap()
    l2g = nc.dram_tensor("l2g", [1, D], F32, kind="ExternalInput").ap()
    l2b = nc.dram_tensor("l2b", [1, D], F32, kind="ExternalInput").ap()
    ipT = nc.dram_tensor("ipT", [D, 2 * DI], F32, kind="ExternalInput").ap()
    convw = nc.dram_tensor("convw", [128, 64], F32, kind="ExternalInput").ap()
    convb = nc.dram_tensor("convb", [128, 16], F32, kind="ExternalInput").ap()
    dparam = nc.dram_tensor("dparam", [128, 16], F32, kind="ExternalInput").ap()
    xpT = nc.dram_tensor("xpT", [DI, 96], F32, kind="ExternalInput").ap()
    dpT = nc.dram_tensor("dpT", [64, DI], F32, kind="ExternalInput").ap()
    dpb = nc.dram_tensor("dpb", [128, 16], F32, kind="ExternalInput").ap()

    qxo = nc.dram_tensor("qxo", [OWN, D], F32, kind="ExternalOutput").ap()
    dtT = nc.dram_tensor("dtT", [DI, OWN], F32, kind="ExternalOutput").ap()
    udT = nc.dram_tensor("udT", [DI, OWN], F32, kind="ExternalOutput").ap()
    dxT = nc.dram_tensor("dxT", [DI, OWN], F32, kind="ExternalOutput").ap()
    szT = nc.dram_tensor("szT", [DI, OWN], F32, kind="ExternalOutput").ap()
    BTo = nc.dram_tensor("BTo", [DS, OWN], F32, kind="ExternalOutput").ap()
    CTo = nc.dram_tensor("CTo", [DS, OWN], F32, kind="ExternalOutput").ap()

    xsf = nc.dram_tensor("xsf", [DI, OWN], F32, kind="Internal").ap()

    with ExitStack() as _st:
        S = _st.enter_context(nc.semaphore("S"))
        PA = _st.enter_context(nc.psum_tensor("PA", [128, 1024], F32))
        PB = _st.enter_context(nc.psum_tensor("PB", [128, 512], F32))
        PC = _st.enter_context(nc.psum_tensor("PC", [128, 128], F32))
        PE = _st.enter_context(nc.psum_tensor("PE", [128, 128], BF16))
        irow = _st.enter_context(nc.sbuf_tensor("irow", [128, 128], F32))
        icol = _st.enter_context(nc.sbuf_tensor("icol", [128, 1], F32))
        identf = _st.enter_context(nc.sbuf_tensor("identf", [128, 128], F32))
        identb = _st.enter_context(nc.sbuf_tensor("identb", [128, 128], BF16))
        ones1 = _st.enter_context(nc.sbuf_tensor("ones1", [1, 128], F32))
        grow = _st.enter_context(nc.sbuf_tensor("grow", [8, 2], F32))
        gm8 = _st.enter_context(nc.sbuf_tensor("gm8", [8, 1], F32))
        gsl = _st.enter_context(nc.sbuf_tensor("gsl", [8, 2], F32))
        rc2g = _st.enter_context(nc.sbuf_tensor("rc2g", [2, 8], F32))
        rc2 = _st.enter_context(nc.sbuf_tensor("rc2", [2, 2], F32))
        sel2 = _st.enter_context(nc.sbuf_tensor("sel2", [2, 2, 128], F32))
        ir2 = _st.enter_context(nc.sbuf_tensor("ir2", [2, 128], F32))
        rco = _st.enter_context(nc.sbuf_tensor("rco", [128, 1], F32))
        rch = _st.enter_context(nc.sbuf_tensor("rch", [128, 1], F32))
        hmsk = _st.enter_context(nc.sbuf_tensor("hmsk", [128, 1], F32))
        grep = _st.enter_context(nc.sbuf_tensor("grep", [128, 1024], F32))
        brep = _st.enter_context(nc.sbuf_tensor("brep", [128, 1024], F32))
        rowt = _st.enter_context(nc.sbuf_tensor("rowt", [1, 1024], F32))
        cwsb = _st.enter_context(nc.sbuf_tensor("cwsb", [128, 16, 4], F32))
        cbsb = _st.enter_context(nc.sbuf_tensor("cbsb", [128, 16, 1], F32))
        dsb = _st.enter_context(nc.sbuf_tensor("dsb", [128, 16, 1], F32))
        dpbsb = _st.enter_context(nc.sbuf_tensor("dpbsb", [128, 16, 1], F32))
        p = Prog(nc, S)
        emit_ident(p, nc, irow, icol, identf, identb, None)
        p.op("gpsimd", "memset", ones1[:, :], 1.0)
        p.dma(gm8[:, :], gmax8[:, :])
        p.dma(gsl[:, :], gsel[:, :])
        p.dma(hmsk[:, :], hmask[:, :])
        p.dma(cwsb[:, :, :].rearrange("p a b -> p (a b)"), convw[:, :])
        p.dma(cbsb[:, :, :].rearrange("p a b -> p (a b)"), convb[:, :])
        p.dma(dsb[:, :, :].rearrange("p a b -> p (a b)"), dparam[:, :])
        p.dma(dpbsb[:, :, :].rearrange("p a b -> p (a b)"), dpb[:, :])
        p.op("vector", "tensor_scalar", out=grow[0:8, :], in0=gsl[0:8, :],
             scalar1=gm8[0:8, 0:1], scalar2=None, op0=ALU.mult)
        p.op("tensor", "transpose", PC[0:2, 0:8], grow[0:8, :], identf[0:8, 0:8])
        p.op("scalar", "copy", out=rc2g[0:2, :], in_=PC[0:2, 0:8])
        p.op("vector", "tensor_reduce", out=rc2[0:2, 0:1], in_=rc2g[0:2, :],
             axis=AX.X, op=ALU.max)
        p.op("vector", "tensor_scalar", out=rc2[0:2, 1:2], in0=rc2[0:2, 0:1],
             scalar1=EPS, scalar2=None, op0=ALU.add)
        p.op("vector", "reciprocal", out=rc2[0:2, 0:1], in_=rc2[0:2, 1:2])
        p.op("vector", "tensor_scalar", out=rc2[0:2, 0:1], in0=rc2[0:2, 0:1],
             scalar1=128.0, scalar2=None, op0=ALU.mult)
        p.op("gpsimd", "iota", ir2[0:2, :], [[0, 128]], channel_multiplier=1,
             allow_small_or_imprecise_dtypes=True)
        for j in range(2):
            p.op("vector", "tensor_scalar", out=f2(sel2[0:2, j:j + 1, :]),
                 in0=ir2[0:2, :], scalar1=float(j), scalar2=None, op0=ALU.is_equal)
        for j, dst in ((0, rco), (1, rch)):
            p.op("tensor", "matmul", PC[:, 0:1], f2(sel2[0:2, j:j + 1, :]),
                 rc2[0:2, 0:1], start=True, stop=True)
            p.op("scalar", "copy", out=dst[:, :], in_=PC[:, 0:1])
        for dram_row, rep in ((l2g, grep), (l2b, brep)):
            p.dma(rowt[:, :], dram_row[:, :])
            for half in range(2):
                p.op("tensor", "matmul", PB[:, :], ones1[0:1, :],
                     rowt[0:1, half * 512:(half + 1) * 512], start=True, stop=True)
                p.op("scalar", "copy", out=rep[:, half * 512:(half + 1) * 512],
                     in_=PB[:, :])
        p.flush()

        UB = 128.0 - EPS
        LB = -128.0 + EPS
        with ExitStack() as _st:
            uT = _st.enter_context(nc.sbuf_tensor("uT", [128, 8, BW], BF16))
            xscb = _st.enter_context(nc.sbuf_tensor("xscb", [128, 16, 2048], BF16))
            xbt = _st.enter_context(nc.sbuf_tensor("xbt", [128, 1024], F32))
            q1 = _st.enter_context(nc.sbuf_tensor("q1", [128, 1024], F32))
            uf = _st.enter_context(nc.sbuf_tensor("uf", [128, 1024], F32))
            lscr = _st.enter_context(nc.sbuf_tensor("lscr", [128, 1024], F32))
            ubb = _st.enter_context(nc.sbuf_tensor("ubb", [128, 1024], BF16))
            bcol = _st.enter_context(nc.sbuf_tensor("bcol", [128, 4], F32))
            for i in range(17):
                p.dma(xbt[:, :], xbh[i * 128:(i + 1) * 128, :])
                rc = rch if i == 0 else rco
                p.op("vector", "tensor_scalar", out=q1[:, :], in0=xbt[:, :],
                     scalar1=rc[:, 0:1], scalar2=UB, op0=ALU.mult, op1=ALU.min)
                p.op("vector", "tensor_scalar", out=q1[:, :], in0=q1[:, :],
                     scalar1=LB, scalar2=None, op0=ALU.max)
                if i > 0:
                    p.dma(qxo[(i - 1) * 128:i * 128, :], q1[:, :])
                emit_ln(p, q1[:, :], uf[:, :], lscr[:, :], bcol)
                p.op("vector", "tensor_tensor", out=uf[:, :], in0=uf[:, :],
                     in1=grep[:, :], op=ALU.mult)
                p.op("vector", "tensor_tensor", out=uf[:, :], in0=uf[:, :],
                     in1=brep[:, :], op=ALU.add)
                if i == 0:
                    p.op("vector", "tensor_scalar", out=uf[:, :], in0=uf[:, :],
                         scalar1=hmsk[:, 0:1], scalar2=None, op0=ALU.mult)
                p.op("vector", "tensor_copy", out=ubb[:, :], in_=uf[:, :])
                for g in range(8):
                    p.op("tensor", "transpose", PE[:, :],
                         ubb[:, g * 128:(g + 1) * 128], identb[:, :])
                    p.op("scalar", "copy", out=sl(uT, g, i * 128, (i + 1) * 128),
                         in_=PE[:, :])
                p.flush()

            ip3 = ipT.rearrange("(ci p) oc -> p ci oc", p=128)
            tchunks = [(0, 512), (512, 1024), (1024, 1536), (1536, 2048), (2048, BW)]
            with ExitStack() as _st:
                ipc = _st.enter_context(nc.sbuf_tensor("ipc", [128, 8, 128], BF16))
                ipstg = _st.enter_context(nc.sbuf_tensor("ipstg", [128, 8, 128], F32))
                xrowf = _st.enter_context(nc.sbuf_tensor("xrowf", [128, BW], F32))
                cv = _st.enter_context(nc.sbuf_tensor("cv", [128, 2048], F32))
                xscf = _st.enter_context(nc.sbuf_tensor("xscf", [128, 2048], F32))
                for oc in range(32):
                    p.dma(ipstg[:, :, :], ip3[:, :, oc * 128:(oc + 1) * 128])
                    p.op("vector", "tensor_copy",
                         out=ipc[:, :, :].rearrange("p a b -> p (a b)"),
                         in_=ipstg[:, :, :].rearrange("p a b -> p (a b)"))
                    for (a, b) in tchunks:
                        for ci in range(8):
                            p.op("tensor", "matmul", PB[:, 0:b - a],
                                 sl(ipc, ci, 0, 128), sl(uT, ci, a, b),
                                 start=(ci == 0), stop=(ci == 7))
                        p.op("scalar", "copy", out=xrowf[:, a:b], in_=PB[:, 0:b - a])
                    if oc < 16:
                        p.op("vector", "tensor_scalar", out=cv[:, :],
                             in0=xrowf[:, BHALO - 3:BHALO - 3 + 2048],
                             scalar1=f2(cwsb[:, oc:oc + 1, 0:1]), scalar2=None,
                             op0=ALU.mult)
                        for tau in range(1, 4):
                            p.op("vector", "scalar_tensor_tensor", out=cv[:, :],
                                 in0=xrowf[:, BHALO - 3 + tau:BHALO - 3 + tau + 2048],
                                 scalar=f2(cwsb[:, oc:oc + 1, tau:tau + 1]),
                                 in1=cv[:, :], op0=ALU.mult, op1=ALU.add)
                        p.op("scalar", "activation", out=xscf[:, :], in_=cv[:, :],
                             func=AF.Silu, bias=f2(cbsb[:, oc:oc + 1, 0:1]))
                        p.dma(xsf[oc * 128:(oc + 1) * 128, :], xscf[:, :])
                        p.op("vector", "tensor_scalar", out=cv[:, :], in0=xscf[:, :],
                             scalar1=f2(dsb[:, oc:oc + 1, 0:1]), scalar2=None,
                             op0=ALU.mult)
                        p.dma(dxT[oc * 128:(oc + 1) * 128, :], cv[:, :])
                        p.op("vector", "tensor_copy", out=sl(xscb, oc, 0, 2048),
                             in_=xscf[:, :])
                    else:
                        zc = oc - 16
                        p.op("scalar", "activation", out=cv[:, :],
                             in_=xrowf[:, BHALO:BW], func=AF.Silu)
                        p.dma(szT[zc * 128:(zc + 1) * 128, :], cv[:, :])
                    p.flush()

            with ExitStack() as _st:
                xpsb = _st.enter_context(nc.sbuf_tensor("xpsb", [128, 16, 96], BF16))
                xpstg = _st.enter_context(nc.sbuf_tensor("xpstg", [128, 16, 96], F32))
                xdb = _st.enter_context(nc.sbuf_tensor("xdb", [128, 96], F32))
                BTsb = _st.enter_context(nc.sbuf_tensor("BTsb", [16, 2048], F32))
                CTsb = _st.enter_context(nc.sbuf_tensor("CTsb", [16, 2048], F32))
                dtl = _st.enter_context(nc.sbuf_tensor("dtl", [64, 2048], F32))
                dtlb = _st.enter_context(nc.sbuf_tensor("dtlb", [64, 2048], BF16))
                dpsb = _st.enter_context(nc.sbuf_tensor("dpsb", [64, 16, 128], BF16))
                dpstg = _st.enter_context(nc.sbuf_tensor("dpstg", [64, 16, 128], F32))
                dtf = _st.enter_context(nc.sbuf_tensor("dtf", [128, 512], F32))
                sf1 = _st.enter_context(nc.sbuf_tensor("sf1", [128, 512], F32))
                sf2 = _st.enter_context(nc.sbuf_tensor("sf2", [128, 512], F32))
                xsrow = _st.enter_context(nc.sbuf_tensor("xsrow", [128, 512], F32))
                p.dma(xpstg[:, :, :], xpT.rearrange("(g p) c -> p g c", p=128))
                p.op("vector", "tensor_copy",
                     out=xpsb[:, :, :].rearrange("p a b -> p (a b)"),
                     in_=xpstg[:, :, :].rearrange("p a b -> p (a b)"))
                for t128 in range(16):
                    for dc in range(16):
                        p.op("tensor", "matmul", PC[:, 0:96],
                             f2(xscb[:, dc:dc + 1, t128 * 128:(t128 + 1) * 128]),
                             sl(xpsb, dc, 0, 96), start=(dc == 0), stop=(dc == 15))
                    p.op("scalar", "copy", out=xdb[:, :], in_=PC[:, 0:96])
                    p.op("tensor", "transpose", PC[0:64, :], xdb[:, 0:64],
                         identf[:, :])
                    p.op("scalar", "copy", out=dtl[:, t128 * 128:(t128 + 1) * 128],
                         in_=PC[0:64, 0:128])
                    p.op("tensor", "transpose", PC[0:16, :], xdb[:, 64:80],
                         identf[:, :])
                    p.op("scalar", "copy", out=BTsb[:, t128 * 128:(t128 + 1) * 128],
                         in_=PC[0:16, 0:128])
                    p.op("tensor", "transpose", PC[0:16, :], xdb[:, 80:96],
                         identf[:, :])
                    p.op("scalar", "copy", out=CTsb[:, t128 * 128:(t128 + 1) * 128],
                         in_=PC[0:16, 0:128])
                p.dma(BTo[:, :], BTsb[:, :])
                p.dma(CTo[:, :], CTsb[:, :])
                p.op("vector", "tensor_copy", out=dtlb[:, :], in_=dtl[:, :])
                p.dma(dpstg[:, :, :], dpT.rearrange("p (g c) -> p g c", g=16))
                p.op("vector", "tensor_copy",
                     out=dpsb[:, :, :].rearrange("p a b -> p (a b)"),
                     in_=dpstg[:, :, :].rearrange("p a b -> p (a b)"))
                p.flush()
                for dc in range(16):
                    for tch in range(4):
                        a = tch * 512
                        p.op("tensor", "matmul", PB[:, :],
                             sl(dpsb, dc, 0, 128), dtlb[:, a:a + 512],
                             start=True, stop=True)
                        emit_softplus(p, PB[:, :], dtf[:, :], sf1[:, :], sf2[:, :],
                                      bias_col=f2(dpbsb[:, dc:dc + 1, 0:1]))
                        p.dma(dtT[dc * 128:(dc + 1) * 128, a:a + 512], dtf[:, :])
                        p.dma(xsrow[:, :], xsf[dc * 128:(dc + 1) * 128, a:a + 512])
                        p.op("vector", "tensor_tensor", out=sf1[:, :],
                             in0=dtf[:, :], in1=xsrow[:, :], op=ALU.mult)
                        p.dma(udT[dc * 128:(dc + 1) * 128, a:a + 512], sf1[:, :])
                    p.flush()
        p.flush(final=True)
    return nc



def build_C():
    nc = bass.Bass("TRN2", target_bir_lowering=False)
    dts = nc.dram_tensor("dts", [256, T], F32, kind="ExternalInput").ap()
    uds = nc.dram_tensor("uds", [256, T], F32, kind="ExternalInput").ap()
    dxs = nc.dram_tensor("dxs", [256, T], F32, kind="ExternalInput").ap()
    szs = nc.dram_tensor("szs", [256, T], F32, kind="ExternalInput").ap()
    alog = nc.dram_tensor("alog", [256, DS], F32, kind="ExternalInput").ap()
    BTg = nc.dram_tensor("BTg", [DS, T], F32, kind="ExternalInput").ap()
    CTg = nc.dram_tensor("CTg", [DS, T], F32, kind="ExternalInput").ap()
    ymo = nc.dram_tensor("ymo", [256, T], F32, kind="ExternalOutput").ap()

    NCK = 32
    CK = T // NCK
    with ExitStack() as _st:
        S = _st.enter_context(nc.semaphore("S"))
        PA = _st.enter_context(nc.psum_tensor("PA", [128, 1024], F32))
        ones1 = _st.enter_context(nc.sbuf_tensor("ones1", [1, 128], F32))
        asb = _st.enter_context(nc.sbuf_tensor("asb", [128, 2, DS], F32))
        hst = _st.enter_context(nc.sbuf_tensor("hst", [128, 2, DS], F32))
        dtc = _st.enter_context(nc.sbuf_tensor("dtc", [128, 2, CK], F32))
        udc = _st.enter_context(nc.sbuf_tensor("udc", [128, 2, CK], F32))
        dxc = _st.enter_context(nc.sbuf_tensor("dxc", [128, 2, CK], F32))
        szc = _st.enter_context(nc.sbuf_tensor("szc", [128, 2, CK], F32))
        BTc = _st.enter_context(nc.sbuf_tensor("BTc", [16, CK], F32))
        CTc = _st.enter_context(nc.sbuf_tensor("CTc", [16, CK], F32))
        da = _st.enter_context(nc.sbuf_tensor("da", [128, CK], F32))
        bb = _st.enter_context(nc.sbuf_tensor("bb", [128, CK], F32))
        hh = _st.enter_context(nc.sbuf_tensor("hh", [128, CK], F32))
        cm = _st.enter_context(nc.sbuf_tensor("cm", [128, CK], F32))
        yac = _st.enter_context(nc.sbuf_tensor("yac", [128, 2, CK], F32))
        sel16 = _st.enter_context(nc.sbuf_tensor("sel16", [16, 16, 128], F32))
        ir16 = _st.enter_context(nc.sbuf_tensor("ir16", [16, 128], F32))
        p = Prog(nc, S)
        p.op("gpsimd", "memset", ones1[:, :], 1.0)
        p.op("gpsimd", "memset", hst[:, :, :], 0.0)
        p.op("gpsimd", "iota", ir16[0:16, :], [[0, 128]], channel_multiplier=1,
             allow_small_or_imprecise_dtypes=True)
        for s in range(DS):
            p.op("vector", "tensor_scalar", out=f2(sel16[0:16, s:s + 1, :]),
                 in0=ir16[0:16, :], scalar1=float(s), scalar2=None, op0=ALU.is_equal)
        p.dma(asb[:, :, :], alog.rearrange("(g p) s -> p g s", p=128))
        p.op("scalar", "activation", out=asb[:, :, :].rearrange("p a b -> p (a b)"),
             in_=asb[:, :, :].rearrange("p a b -> p (a b)"), func=AF.Exp)
        p.op("vector", "tensor_scalar",
             out=asb[:, :, :].rearrange("p a b -> p (a b)"),
             in0=asb[:, :, :].rearrange("p a b -> p (a b)"),
             scalar1=-1.0, scalar2=None, op0=ALU.mult)
        p.flush()
        for ck in range(NCK):
            a = ck * CK
            for dram, sb in ((dts, dtc), (uds, udc), (dxs, dxc), (szs, szc)):
                p.dma(sb[:, :, :],
                      dram.rearrange("(g p) t -> p g t", p=128)[:, :, a:a + CK])
            p.dma(BTc[:, :], BTg[:, a:a + CK])
            p.dma(CTc[:, :], CTg[:, a:a + CK])
            for s in range(DS):
                p.op("tensor", "matmul", PA[:, 0:CK], f2(sel16[:, s:s + 1, :]),
                     BTc[0:16, :], start=True, stop=True)
                p.op("tensor", "matmul", PA[:, CK:2 * CK], f2(sel16[:, s:s + 1, :]),
                     CTc[0:16, :], start=True, stop=True)
                for g in range(2):
                    p.op("scalar", "activation", out=da[:, :],
                         in_=f2(dtc[:, g:g + 1, :]), func=AF.Exp,
                         scale=f2(asb[:, g:g + 1, s:s + 1]))
                    p.op("vector", "tensor_tensor", out=bb[:, :],
                         in0=f2(udc[:, g:g + 1, :]), in1=PA[:, 0:CK], op=ALU.mult)
                    p.op("vector", "tensor_tensor_scan", out=hh[:, :], data0=da[:, :],
                         data1=bb[:, :], initial=f2(hst[:, g:g + 1, s:s + 1]),
                         op0=ALU.mult, op1=ALU.add)
                    p.op("vector", "tensor_copy", out=f2(hst[:, g:g + 1, s:s + 1]),
                         in_=hh[:, CK - 1:CK])
                    p.op("vector", "tensor_tensor", out=cm[:, :], in0=hh[:, :],
                         in1=PA[:, CK:2 * CK], op=ALU.mult)
                    if s == 0:
                        p.op("gpsimd", "tensor_copy", out=f2(yac[:, g:g + 1, :]),
                             in_=cm[:, :])
                    else:
                        p.op("gpsimd", "tensor_add", out=f2(yac[:, g:g + 1, :]),
                             in0=f2(yac[:, g:g + 1, :]), in1=cm[:, :])
            for g in range(2):
                p.op("vector", "tensor_tensor", out=f2(yac[:, g:g + 1, :]),
                     in0=f2(yac[:, g:g + 1, :]), in1=f2(dxc[:, g:g + 1, :]), op=ALU.add)
                p.op("vector", "tensor_tensor", out=f2(yac[:, g:g + 1, :]),
                     in0=f2(yac[:, g:g + 1, :]), in1=f2(szc[:, g:g + 1, :]), op=ALU.mult)
                p.dma(ymo[g * 128:(g + 1) * 128, a:a + CK], f2(yac[:, g:g + 1, :]))
            p.flush()
        p.flush(final=True)
    return nc



def build_D():
    nc = bass.Bass("TRN2", target_bir_lowering=False)
    ymc = nc.dram_tensor("ymc", [DI, OWN], F32, kind="ExternalInput").ap()
    opT = nc.dram_tensor("opT", [DI, D], F32, kind="ExternalInput").ap()
    xn1Ti = nc.dram_tensor("xn1Ti", [D, OWN], BF16, kind="ExternalInput").ap()
    qxi = nc.dram_tensor("qxi", [OWN, D], F32, kind="ExternalInput").ap()
    x1i = nc.dram_tensor("x1i", [OWN, D], F32, kind="ExternalInput").ap()
    gw = nc.dram_tensor("gw", [128, 64], F32, kind="ExternalInput").ap()
    gb = nc.dram_tensor("gb", [1, 8], F32, kind="ExternalInput").ap()
    w1 = nc.dram_tensor("w1", [8, D, 2048], F32, kind="ExternalInput").ap()
    b1c = nc.dram_tensor("b1c", [128, 128], F32, kind="ExternalInput").ap()
    w2 = nc.dram_tensor("w2", [8, 2048, D], F32, kind="ExternalInput").ap()
    b2 = nc.dram_tensor("b2", [8, D], F32, kind="ExternalInput").ap()
    fin = nc.dram_tensor("fin", [OWN, D], F32, kind="ExternalOutput").ap()

    with ExitStack() as _st:
        S = _st.enter_context(nc.semaphore("S"))
        PA = _st.enter_context(nc.psum_tensor("PA", [128, 1024], F32))
        PB = _st.enter_context(nc.psum_tensor("PB", [128, 512], F32))
        PC = _st.enter_context(nc.psum_tensor("PC", [128, 128], F32))
        PF = _st.enter_context(nc.psum_tensor("PF", [128, 128], F32))
        irow = _st.enter_context(nc.sbuf_tensor("irow", [128, 128], F32))
        icol = _st.enter_context(nc.sbuf_tensor("icol", [128, 1], F32))
        identf = _st.enter_context(nc.sbuf_tensor("identf", [128, 128], F32))
        identb = _st.enter_context(nc.sbuf_tensor("identb", [128, 128], BF16))
        ones1 = _st.enter_context(nc.sbuf_tensor("ones1", [1, 128], F32))
        oacc = _st.enter_context(nc.sbuf_tensor("oacc", [128, 16, 1024], F32))
        xnsb = _st.enter_context(nc.sbuf_tensor("xnsb", [128, 8, OWN], BF16))
        wmT = _st.enter_context(nc.sbuf_tensor("wmT", [8, OWN], F32))
        b1sb = _st.enter_context(nc.sbuf_tensor("b1sb", [128, 16, 8], F32))
        b2sb = _st.enter_context(nc.sbuf_tensor("b2sb", [8, 1024], F32))
        gbrep = _st.enter_context(nc.sbuf_tensor("gbrep", [128, 8], F32))
        gwsb = _st.enter_context(nc.sbuf_tensor("gwsb", [128, 8, 8], BF16))
        gwstg = _st.enter_context(nc.sbuf_tensor("gwstg", [128, 8, 8], F32))
        sel8 = _st.enter_context(nc.sbuf_tensor("sel8", [8, 8, 128], F32))
        ir8 = _st.enter_context(nc.sbuf_tensor("ir8", [8, 128], F32))
        p = Prog(nc, S)
        emit_ident(p, nc, irow, icol, identf, identb, None)
        p.op("gpsimd", "memset", ones1[:, :], 1.0)
        p.op("gpsimd", "memset", oacc[:, :, :], 0.0)
        p.op("gpsimd", "iota", ir8[0:8, :], [[0, 128]], channel_multiplier=1,
             allow_small_or_imprecise_dtypes=True)
        for e in range(8):
            p.op("vector", "tensor_scalar", out=f2(sel8[0:8, e:e + 1, :]),
                 in0=ir8[0:8, :], scalar1=float(e), scalar2=None, op0=ALU.is_equal)
        p.dma(xnsb[:, :, :], xn1Ti.rearrange("(g p) t -> p g t", p=128))
        p.dma(b1sb[:, :, :].rearrange("p a b -> p (a b)"), b1c[:, :])
        p.dma(b2sb[:, :], b2[:, :])
        p.dma(gwstg[:, :, :].rearrange("p a b -> p (a b)"), gw[:, :])
        p.op("vector", "tensor_copy",
             out=gwsb[:, :, :].rearrange("p a b -> p (a b)"),
             in_=gwstg[:, :, :].rearrange("p a b -> p (a b)"))
        with ExitStack() as _st:
            gbr = _st.enter_context(nc.sbuf_tensor("gbr", [1, 8], F32))
            p.dma(gbr[:, :], gb[:, :])
            p.op("tensor", "matmul", PC[:, 0:8], ones1[0:1, :], gbr[0:1, :],
                 start=True, stop=True)
            p.op("scalar", "copy", out=gbrep[:, :], in_=PC[:, 0:8])
            p.flush()

            with ExitStack() as _st:
                pf_ = _st.enter_context(nc.sbuf_tensor("pf_", [128, 8], F32))
                prob = _st.enter_context(nc.sbuf_tensor("prob", [128, 8], F32))
                mcol = _st.enter_context(nc.sbuf_tensor("mcol", [128, 4], F32))
                wm = _st.enter_context(nc.sbuf_tensor("wm", [128, 8], F32))
                for t128 in range(16):
                    for ci in range(8):
                        p.op("tensor", "matmul", PC[:, 0:8],
                             f2(xnsb[:, ci:ci + 1, t128 * 128:(t128 + 1) * 128]),
                             sl(gwsb, ci, 0, 8), start=(ci == 0), stop=(ci == 7))
                    p.op("vector", "tensor_tensor", out=pf_[:, :], in0=PC[:, 0:8],
                         in1=gbrep[:, :], op=ALU.add)
                    p.op("vector", "tensor_reduce", out=mcol[:, 0:1], in_=pf_[:, :],
                         axis=AX.X, op=ALU.max)
                    p.op("vector", "tensor_scalar", out=mcol[:, 1:2], in0=mcol[:, 0:1],
                         scalar1=-1.0, scalar2=None, op0=ALU.mult)
                    p.op("scalar", "activation", out=prob[:, :], in_=pf_[:, :],
                         func=AF.Exp, bias=mcol[:, 1:2])
                    p.op("vector", "tensor_reduce", out=mcol[:, 2:3], in_=prob[:, :],
                         axis=AX.X, op=ALU.add)
                    p.op("vector", "reciprocal", out=mcol[:, 3:4], in_=mcol[:, 2:3])
                    p.op("vector", "tensor_scalar", out=prob[:, :], in0=prob[:, :],
                         scalar1=mcol[:, 3:4], scalar2=None, op0=ALU.mult)
                    p.op("vector", "tensor_reduce", out=mcol[:, 0:1], in_=prob[:, :],
                         axis=AX.X, op=ALU.max)
                    p.op("vector", "tensor_scalar", out=wm[:, :], in0=prob[:, :],
                         scalar1=mcol[:, 0:1], scalar2=None, op0=ALU.is_ge)
                    p.op("vector", "scalar_tensor_tensor", out=wm[:, :], in0=wm[:, :],
                         scalar=-2.0, in1=prob[:, :], op0=ALU.mult, op1=ALU.add)
                    p.op("vector", "tensor_reduce", out=mcol[:, 1:2], in_=wm[:, :],
                         axis=AX.X, op=ALU.max)
                    p.op("vector", "tensor_scalar", out=wm[:, :], in0=prob[:, :],
                         scalar1=mcol[:, 1:2], scalar2=None, op0=ALU.is_ge)
                    p.op("vector", "tensor_tensor", out=wm[:, :], in0=wm[:, :],
                         in1=prob[:, :], op=ALU.mult)
                    p.op("tensor", "transpose", PF[0:8, :], wm[:, :], identf[:, :])
                    p.op("scalar", "copy", out=wmT[:, t128 * 128:(t128 + 1) * 128],
                         in_=PF[0:8, 0:128])
                p.flush()

            with ExitStack() as _st:
                w1b = _st.enter_context(nc.sbuf_tensor("w1b", [128, 8, 2048], BF16))
                w2b = _st.enter_context(nc.sbuf_tensor("w2b", [128, 16, 1024], BF16))
                stg = _st.enter_context(nc.sbuf_tensor("stg", [128, 8, 256], F32))
                hTb = _st.enter_context(nc.sbuf_tensor("hTb", [128, 16, 512], BF16))
                hscr = _st.enter_context(nc.sbuf_tensor("hscr", [128, 512], F32))
                wmf = _st.enter_context(nc.sbuf_tensor("wmf", [128, 512], F32))
                for e in range(8):
                    w1e = w1[e:e + 1, :, :].rearrange("e (ci p) h -> p (e ci) h", p=128)
                    for piece in range(8):
                        p.dma(stg[:, :, :], w1e[:, :, piece * 256:(piece + 1) * 256])
                        for ci in range(8):
                            p.op("vector", "tensor_copy",
                                 out=sl(w1b, ci, piece * 256, (piece + 1) * 256),
                                 in_=sl(stg, ci, 0, 256))
                    w2e = w2[e:e + 1, :, :].rearrange("e (hc p) c -> p (e hc) c", p=128)
                    for piece in range(8):
                        p.dma(stg[:, :, :].rearrange("p (a u) c -> p a u c", a=2),
                              w2e[:, piece * 2:(piece + 1) * 2, :].rearrange(
                                  "p a (u c) -> p a u c", u=4))
                        for j in range(8):
                            hc = piece * 2 + j // 4
                            q = j % 4
                            p.op("vector", "tensor_copy",
                                 out=f2(w2b[:, hc:hc + 1, q * 256:(q + 1) * 256]),
                                 in_=sl(stg, j, 0, 256))
                    for tco in range(4):
                        a = tco * 512
                        p.op("tensor", "matmul", PB[:, :], f2(sel8[:, e:e + 1, :]),
                             wmT[0:8, a:a + 512], start=True, stop=True)
                        p.op("scalar", "copy", out=wmf[:, :], in_=PB[:, :])
                        for hc in range(16):
                            for ci in range(8):
                                p.op("tensor", "matmul", PB[:, :],
                                     sl(w1b, ci, hc * 128, (hc + 1) * 128),
                                     sl(xnsb, ci, a, a + 512),
                                     start=(ci == 0), stop=(ci == 7))
                            p.op("scalar", "activation", out=hscr[:, :], in_=PB[:, :],
                                 func=AF.Relu, bias=f2(b1sb[:, hc:hc + 1, e:e + 1]))
                            p.op("vector", "tensor_tensor", out=sl(hTb, hc, 0, 512),
                                 in0=hscr[:, :], in1=wmf[:, :], op=ALU.mult)
                        for i128 in range(4):
                            ti = tco * 4 + i128
                            for hc in range(16):
                                for half in range(2):
                                    p.op("tensor", "matmul",
                                         PA[:, half * 512:(half + 1) * 512],
                                         f2(hTb[:, hc:hc + 1, i128 * 128:(i128 + 1) * 128]),
                                         f2(w2b[:, hc:hc + 1, half * 512:(half + 1) * 512]),
                                         start=(hc == 0), stop=(hc == 15),
                                         skip_group_check=True)
                            p.op("vector", "tensor_tensor",
                                 out=f2(oacc[:, ti:ti + 1, :]),
                                 in0=f2(oacc[:, ti:ti + 1, :]), in1=PA[:, :], op=ALU.add)
                        p.flush()

            with ExitStack() as _st:
                ymstg = _st.enter_context(nc.sbuf_tensor("ymstg", [128, 2048], F32))
                ymbb = _st.enter_context(nc.sbuf_tensor("ymbb", [128, 2048], BF16))
                opstg = _st.enter_context(nc.sbuf_tensor("opstg", [128, 1024], F32))
                opb = _st.enter_context(nc.sbuf_tensor("opb", [128, 1024], BF16))
                b2bb = _st.enter_context(nc.sbuf_tensor("b2bb", [8, 1024], BF16))
                wmbb = _st.enter_context(nc.sbuf_tensor("wmbb", [8, OWN], BF16))
                resrow = _st.enter_context(nc.sbuf_tensor("resrow", [128, 1024], F32))
                p.op("vector", "tensor_copy", out=b2bb[:, :], in_=b2sb[:, :])
                p.op("vector", "tensor_copy", out=wmbb[:, :], in_=wmT[:, :])
                for dc in range(16):
                    p.dma(ymstg[:, :], ymc[dc * 128:(dc + 1) * 128, :])
                    p.op("vector", "tensor_copy", out=ymbb[:, :], in_=ymstg[:, :])
                    p.dma(opstg[:, :], opT[dc * 128:(dc + 1) * 128, :])
                    p.op("vector", "tensor_copy", out=opb[:, :], in_=opstg[:, :])
                    for ti in range(16):
                        for half in range(2):
                            p.op("tensor", "matmul", PA[:, half * 512:(half + 1) * 512],
                                 ymbb[:, ti * 128:(ti + 1) * 128],
                                 opb[:, half * 512:(half + 1) * 512],
                                 start=True, stop=True)
                        p.op("vector", "tensor_tensor", out=f2(oacc[:, ti:ti + 1, :]),
                             in0=f2(oacc[:, ti:ti + 1, :]), in1=PA[:, :], op=ALU.add)
                    p.flush()
                for ti in range(16):
                    for half in range(2):
                        p.op("tensor", "matmul", PA[:, half * 512:(half + 1) * 512],
                             wmbb[:, ti * 128:(ti + 1) * 128],
                             b2bb[:, half * 512:(half + 1) * 512],
                             start=True, stop=True)
                    p.op("vector", "tensor_tensor", out=f2(oacc[:, ti:ti + 1, :]),
                         in0=f2(oacc[:, ti:ti + 1, :]), in1=PA[:, :], op=ALU.add)
                    p.dma(resrow[:, :], x1i[ti * 128:(ti + 1) * 128, :])
                    p.op("vector", "tensor_tensor", out=f2(oacc[:, ti:ti + 1, :]),
                         in0=f2(oacc[:, ti:ti + 1, :]), in1=resrow[:, :], op=ALU.add)
                    p.dma(resrow[:, :], qxi[ti * 128:(ti + 1) * 128, :])
                    p.op("vector", "tensor_tensor", out=f2(oacc[:, ti:ti + 1, :]),
                         in0=f2(oacc[:, ti:ti + 1, :]), in1=resrow[:, :], op=ALU.add)
                    p.dma(fin[ti * 128:(ti + 1) * 128, :], f2(oacc[:, ti:ti + 1, :]))
                p.flush(final=True)
    return nc



def _f(a):
    return np.ascontiguousarray(a, dtype=np.float32)


def swz(a):
    a = np.asarray(a, dtype=np.float32)
    if a.ndim == 1:
        a = a.reshape(-1, 1)
    G = a.shape[0] // 128
    return np.ascontiguousarray(a.reshape(G, 128, a.shape[1]).transpose(1, 0, 2)
                                .reshape(128, G * a.shape[1]))


DBG = {}


def kernel(**inp):
    x = _f(inp["L0_attn_x"]).reshape(T, D)
    xpad = np.concatenate([np.zeros((HALO, D), np.float32), x], axis=0)

    ncA = build_A()
    common_A = dict(
        wrT=_f(inp["L0_attn_W_r"].T), wkT=_f(inp["L0_attn_W_k"].T),
        wvT=_f(inp["L0_attn_W_v"].T), wwT=_f(inp["L0_attn_W_w"].T),
        woT=_f(inp["L0_attn_W_o"].T),
        tmc=swz(np.stack([_f(inp["L0_attn_tm_r"]), _f(inp["L0_attn_tm_k"]),
                          _f(inp["L0_attn_tm_v"]), _f(inp["L0_attn_tm_w"])], axis=1)),
        ucol=swz(np.asarray(inp["L0_attn_u"]).reshape(D, 1)),
        lncol=swz(np.stack([np.tile(_f(inp["L0_attn_ln_w"]), H),
                            np.tile(_f(inp["L0_attn_ln_b"]), H)], axis=1)),
        fw1=_f(inp["L0_ffn_w1"]), fb1=swz(inp["L0_ffn_b1"]),
        fw2=_f(inp["L0_ffn_w2"]), fb2=_f(inp["L0_ffn_b2"]).reshape(1, D),
        bW1T=_f(inp["L1_attn_W1"].T), bb1=swz(inp["L1_attn_b1"]),
        bW2T=_f(inp["L1_attn_W2"].T), bb2=_f(inp["L1_attn_b2"]).reshape(1, D),
        l1g=_f(inp["L1_attn_ln1_g"]).reshape(1, D),
        l1b=_f(inp["L1_attn_ln1_b"]).reshape(1, D),
    )
    in_maps = []
    for c in range(NCORE):
        m = dict(common_A)
        m["xs"] = _f(xpad[c * OWN:c * OWN + WTOT])
        in_maps.append(m)
    resA = run_bass_kernel_spmd(ncA, in_maps, core_ids=list(range(NCORE)))
    x1_g = np.concatenate([np.asarray(r["x1"]) for r in resA.results], axis=0)
    xb_g = np.concatenate([np.asarray(r["xbo"]) for r in resA.results], axis=0)
    gmax8 = np.concatenate([np.asarray(r["gmax"]) for r in resA.results],
                           axis=0).reshape(8, 1).astype(np.float32)

    ncB = build_B()
    xbpad = np.concatenate([np.zeros((BHALO, D), np.float32), xb_g], axis=0)
    common_B = dict(
        gmax8=gmax8,
        l2g=_f(inp["L1_attn_ln2_g"]).reshape(1, D),
        l2b=_f(inp["L1_attn_ln2_b"]).reshape(1, D),
        ipT=_f(inp["L1_attn_in_proj_w"].T),
        convw=swz(np.asarray(inp["L1_attn_conv_w"]).reshape(DI, 4)),
        convb=swz(inp["L1_attn_conv_b"]),
        dparam=swz(inp["L1_attn_D_param"]),
        xpT=_f(inp["L1_attn_x_proj_w"].T),
        dpT=_f(inp["L1_attn_dt_proj_w"].T),
        dpb=swz(inp["L1_attn_dt_proj_b"]),
    )
    in_maps = []
    for c in range(NCORE):
        m = dict(common_B)
        m["xbh"] = _f(xbpad[c * OWN:c * OWN + BW])
        own_grp = (c * OWN) // 4096
        halo_grp = max(c * OWN - 1, 0) // 4096
        gs = np.zeros((8, 2), np.float32)
        gs[2 * own_grp:2 * own_grp + 2, 0] = 1.0
        gs[2 * halo_grp:2 * halo_grp + 2, 1] = 1.0
        m["gsel"] = gs
        m["hmask"] = np.ones((128, 1), np.float32) if c > 0 else np.zeros(
            (128, 1), np.float32)
        in_maps.append(m)
    resB = run_bass_kernel_spmd(ncB, in_maps, core_ids=list(range(NCORE)))
    qx_g = np.concatenate([np.asarray(r["qxo"]) for r in resB.results], axis=0)
    dtT_g = np.concatenate([np.asarray(r["dtT"]) for r in resB.results], axis=1)
    udT_g = np.concatenate([np.asarray(r["udT"]) for r in resB.results], axis=1)
    dxT_g = np.concatenate([np.asarray(r["dxT"]) for r in resB.results], axis=1)
    szT_g = np.concatenate([np.asarray(r["szT"]) for r in resB.results], axis=1)
    BT_g = np.concatenate([np.asarray(r["BTo"]) for r in resB.results], axis=1)
    CT_g = np.concatenate([np.asarray(r["CTo"]) for r in resB.results], axis=1)

    ncC = build_C()
    alog_full = _f(inp["L1_attn_A_log"])
    in_maps = []
    for c in range(NCORE):
        sl_ = slice(c * 256, (c + 1) * 256)
        in_maps.append(dict(
            dts=_f(dtT_g[sl_]), uds=_f(udT_g[sl_]), dxs=_f(dxT_g[sl_]),
            szs=_f(szT_g[sl_]), alog=_f(alog_full[sl_]),
            BTg=_f(BT_g), CTg=_f(CT_g)))
    resC = run_bass_kernel_spmd(ncC, in_maps, core_ids=list(range(NCORE)))
    ym_g = np.concatenate([np.asarray(r["ymo"]) for r in resC.results], axis=0)

    ncD = build_D()
    common_D = dict(
        opT=_f(inp["L1_attn_out_proj_w"].T),
        gw=swz(inp["L1_ffn_gate_w"]),
        gb=_f(inp["L1_ffn_gate_b"]).reshape(1, 8),
        w1=_f(inp["L1_ffn_w1"]), b1c=swz(np.asarray(inp["L1_ffn_b1"]).T),
        w2=_f(inp["L1_ffn_w2"]), b2=_f(inp["L1_ffn_b2"]),
    )
    in_maps = []
    for c in range(NCORE):
        m = dict(common_D)
        m["ymc"] = _f(ym_g[:, c * OWN:(c + 1) * OWN])
        m["xn1Ti"] = np.ascontiguousarray(
            np.asarray(resA.results[c]["xn1T"]), dtype=ml_dtypes.bfloat16)
        m["qxi"] = _f(qx_g[c * OWN:(c + 1) * OWN])
        m["x1i"] = _f(x1_g[c * OWN:(c + 1) * OWN])
        in_maps.append(m)
    resD = run_bass_kernel_spmd(ncD, in_maps, core_ids=list(range(NCORE)))
    out = np.concatenate([np.asarray(r["fin"]) for r in resD.results], axis=0)
    DBG.clear()
    DBG.update(dict(x1=x1_g, xb=xb_g, gmax8=gmax8, qx=qx_g, dtT=dtT_g, udT=udT_g,
                    dxT=dxT_g, szT=szT_g, BT=BT_g, CT=CT_g, ym=ym_g))
    return out.reshape(1, T, D).astype(np.float32)
